# version 16
# baseline (speedup 1.0000x reference)
import numpy as np
from contextlib import ExitStack

import concourse.bass as bass
import concourse.mybir as mybir
from concourse.bass_utils import run_bass_kernel_spmd
from concourse.alu_op_type import AluOpType as ALU

F32 = mybir.dt.float32
BF16 = mybir.dt.bfloat16
AF = mybir.ActivationFunctionType
AX = mybir.AxisListType.X

N_CORES = 8
B = 16
T = 2048
D = 2048
SEQ = 2
NT = 16
NTILES = SEQ * NT
NH, HD, NKV = 16, 128, 4
BH, BDK, BDV, BM = 16, 64, 128, 64
EPS = 1e-5
QK_SCALE = 1.0 / np.sqrt(128.0)
BQ_SCALE = 1.0 / np.sqrt(64.0)


class Sem:
    def __init__(self, h):
        self.h = h
        self.n = 0


def I1(instr, sem):
    instr.then_inc(sem.h, 1)
    sem.n += 1
    return sem.n


def I16(instr, sem):
    instr.then_inc(sem.h, 16)
    sem.n += 16
    return sem.n


def vsep(p):
    p.v.nop(cycle_cnt=400)


class P:
    pass


def _bar(p):
    p.barn += 1
    tgt = 5 * p.barn
    for e in [p.t, p.v, p.s, p.g, p.y]:
        e.drain().then_inc(p.bar.h, 1)
        e.wait_ge(p.bar.h, tgt)


def _ln(p, xt_ap, sqj_ap, tmp8, xn_out, store_col=None):
    v, s = p.v, p.s
    I1(v.reduce_sum(tmp8[:, 0:1], xt_ap, axis=AX), p.s_ve)
    I1(s.activation(sqj_ap, xt_ap, AF.Square, accum_out=tmp8[:, 1:2]), p.s_se)
    v.wait_ge(p.s_se.h, p.s_se.n)
    vsep(p)
    v.tensor_scalar(out=tmp8[:, 2:3], in0=tmp8[:, 0:1], scalar1=1.0 / D,
                    scalar2=None, op0=ALU.mult)
    vsep(p)
    v.tensor_tensor(tmp8[:, 3:4], tmp8[:, 2:3], tmp8[:, 2:3], ALU.mult)
    vsep(p)
    I1(v.tensor_scalar(out=tmp8[:, 4:5], in0=tmp8[:, 1:2], scalar1=1.0 / D,
                       scalar2=tmp8[:, 3:4], op0=ALU.mult, op1=ALU.subtract),
       p.s_ve)
    s.wait_ge(p.s_ve.h, p.s_ve.n)
    s.activation(tmp8[:, 5:6], tmp8[:, 4:5], AF.Ln, bias=p.eps_c[:], scale=1.0)
    I1(s.activation(tmp8[:, 6:7], tmp8[:, 5:6], AF.Exp, bias=0.0, scale=-0.5),
       p.s_se)
    v.wait_ge(p.s_se.h, p.s_se.n)
    v.tensor_scalar(out=tmp8[:, 7:8], in0=tmp8[:, 4:5], scalar1=p.eps_c[:],
                    scalar2=None, op0=ALU.add)
    v.tensor_tensor(tmp8[:, 0:1], tmp8[:, 6:7], tmp8[:, 6:7], ALU.mult)
    vsep(p)
    v.tensor_tensor(tmp8[:, 3:4], tmp8[:, 0:1], tmp8[:, 7:8], ALU.mult)
    vsep(p)
    v.tensor_scalar(out=tmp8[:, 0:1], in0=tmp8[:, 3:4], scalar1=-0.5,
                    scalar2=1.5, op0=ALU.mult, op1=ALU.add)
    vsep(p)
    I1(v.tensor_tensor(tmp8[:, 5:6], tmp8[:, 6:7], tmp8[:, 0:1], ALU.mult),
       p.s_ve)
    if store_col is not None:
        s.wait_ge(p.s_ve.h, p.s_ve.n)
        s.copy(p.stats[:, 2 * store_col:2 * store_col + 1], tmp8[:, 2:3])
        I1(s.copy(p.stats[:, 2 * store_col + 1:2 * store_col + 2], tmp8[:, 5:6]),
           p.s_se)
        v.wait_ge(p.s_se.h, p.s_se.n)
    vsep(p)
    I1(v.tensor_scalar(out=xn_out, in0=xt_ap, scalar1=tmp8[:, 2:3],
                       scalar2=tmp8[:, 5:6], op0=ALU.subtract, op1=ALU.mult),
       p.s_ve)
    return p.s_ve.n


def _xnT(p, xn_bf, xnT_sb, pT, dst_map=None):
    t, v = p.t, p.v
    for grp in range(4):
        pb = pT[grp % 2]
        key = id(pb)
        last = p.pT_free.get(key, 0)
        if last:
            t.wait_ge(p.s_v2.h, last)
        for c in range(4):
            cc = grp * 4 + c
            mm = t.transpose(pb[:, c * 128:(c + 1) * 128],
                             xn_bf[:, cc * 128:(cc + 1) * 128], p.ident[:])
        I1(mm, p.s_pe)
        v.wait_ge(p.s_pe.h, p.s_pe.n)
        dst = dst_map(grp) if dst_map else xnT_sb[:, grp * 512:(grp + 1) * 512]
        p.pT_free[key] = I1(v.tensor_copy(dst, pb[:]), p.s_v2)
    t.wait_ge(p.s_v2.h, p.s_v2.n)


def build_program(tap=None):
    nc = bass.Bass("TRN2", target_bir_lowering=False)
    p = P()
    p.nc = nc
    p.tap = tap
    p.pT_free = {}

    def din(name, shape, dt=F32):
        return nc.dram_tensor(name, shape, dt, kind="ExternalInput").ap()

    def dscr(name, shape, dt, tapped=False):
        kind = "ExternalOutput" if (tapped and tap is not None) else "Internal"
        return nc.dram_tensor(name, shape, dt, kind=kind).ap()

    p.x0 = din("x0", [SEQ * T, D])
    p.aWq = din("aWq", [D, NH * HD])
    p.aWk = din("aWk", [D, NKV * HD])
    p.aWv = din("aWv", [D, NKV * HD])
    p.aWo = din("aWo", [NH * HD, D])
    p.coscat = din("coscat", [T, HD])
    p.sincat = din("sincat", [T, HD])
    p.fW1 = din("fW1", [D, D])
    p.fb1c = din("fb1c", [128, 16])
    p.fW2 = din("fW2", [D, 8])
    p.fb2 = din("fb2", [1, 8])
    p.fV = din("fV", [D, 512])
    p.fU = din("fU", [512, D])
    p.bWq = din("bWq", [D, BH * BDK])
    p.bWk = din("bWk", [D, BH * BDK])
    p.bWs = din("bWs", [D, BH * BM])
    p.bWv = din("bWv", [D, BH * BDV])
    p.bWg = din("bWg", [D, BH * BDV])
    p.bWo = din("bWo", [BH * BDV, D])
    p.gnw = din("gnw", [1, BDV])
    p.identi = din("identi", [128, 128])
    p.utinci = din("utinci", [128, 128])
    p.out = nc.dram_tensor("out", [SEQ * T, D], F32, kind="ExternalOutput").ap()

    tl = tap or "zz"
    p.qscr = dscr("qscr", [SEQ, T, NH * HD], BF16, tl == "A1")
    p.x1scr = dscr("x1scr", [SEQ * T, D], F32, tl == "A2")
    p.x2scr = dscr("x2scr", [SEQ * T, D], F32, tl == "A3")
    p.qbscr = dscr("qbscr", [SEQ, 8, 128, T], BF16, tl == "B1a")
    p.kbscr = dscr("kbscr", [SEQ, 8, 128, T], BF16, tl == "B1a")
    p.escr = dscr("escr", [SEQ, T, BH * BM], BF16, tl == "B1a")
    p.rzscr = dscr("rzscr", [SEQ, T, BH * BM], BF16, tl == "B1a")
    p.vbscr = dscr("vbscr", [SEQ, T, BH * BDV], BF16, tl == "B1b")
    p.gbscr = dscr("gbscr", [SEQ, T, BH * BDV], BF16, tl == "B1b")
    p.x3scr = dscr("x3scr", [SEQ * T, D], F32, tl == "B2")
    if tap in ("A1", "B1a"):
        p.stats_tap = nc.dram_tensor("stats_tap", [128, 64], F32,
                                     kind="ExternalOutput").ap()
    if tap == "A1":
        p.kfm_tap = nc.dram_tensor("kfm_tap", [128, SEQ * NKV * T], BF16,
                                   kind="ExternalOutput").ap()
        p.vtm_tap = nc.dram_tensor("vtm_tap", [128, SEQ * NT * 512], BF16,
                                   kind="ExternalOutput").ap()

    es = ExitStack()
    with es:
        def sbuf(name, shape, dt=F32):
            return es.enter_context(nc.sbuf_tensor(name, shape, dt))

        def sem(name):
            return Sem(es.enter_context(nc.semaphore(name)))

        p.ident = sbuf("ident", [128, 128], BF16)
        p.utinc = sbuf("utinc", [128, 128], BF16)
        p.eps_c = sbuf("eps_c", [128, 1])
        p.n32_c = sbuf("n32_c", [128, 1])
        p.stats = sbuf("stats", [128, 64])
        p.b2b = sbuf("b2b", [128, 8])
        p.b2row = sbuf("b2row", [1, 8])
        p.ones1 = sbuf("ones1", [1, 128])
        p.gnwcol = sbuf("gnwcol", [128, 1])
        p.gnwrow = sbuf("gnwrow", [1, BDV])

        p.bar = sem("bar")
        p.barn = 0
        for nm in ["s_w", "s_x", "s_st", "s_pe", "s_ve", "s_se", "s_ge",
                   "s_p2", "s_v2", "s_v3", "s_s2", "s_x0", "s_x1",
                   "s_tg0", "s_tg1", "s_q0", "s_q1"]:
            setattr(p, nm, sem(nm))

        with nc.Block() as block:
            p.t, p.v, p.s, p.g, p.y = nc.tensor, nc.vector, nc.scalar, \
                nc.gpsimd, nc.sync

            emit_init(p)
            _bar(p)
            stages = [("A1", emit_A1), ("A2", emit_A2), ("A3", emit_A3),
                      ("B1a", emit_B1a), ("B1b", emit_B1b), ("B2", emit_B2),
                      ("B3", emit_B3)]
            for name, f in stages:
                with ExitStack() as ses:
                    p.ses = ses
                    f(p)
                _bar(p)
                if tap == name:
                    break
        if getattr(p, "esKV", None) is not None:
            p.esKV.close()
            p.esKV = None
    return nc


def emit_init(p):
    g, v, s, t, y = p.g, p.v, p.s, p.t, p.y
    I16(g.dma_start(out=p.ident[:], in_=p.identi), p.s_w)
    I16(g.dma_start(out=p.utinc[:], in_=p.utinci), p.s_w)
    g.memset(p.eps_c[:], EPS)
    I1(g.memset(p.n32_c[:], -32.0), p.s_ge)
    I16(y.dma_start(p.b2row[:], p.fb2), p.s_x)
    I16(y.dma_start(p.gnwrow[:], p.gnw), p.s_x)
    I1(v.memset(p.ones1[:], 1.0), p.s_ve)
    with p.nc.psum_tensor("pini", [128, 8], F32) as pini, \
         p.nc.psum_tensor("pini2", [128, 1], F32) as pini2:
        t.wait_ge(p.s_x.h, p.s_x.n)
        t.wait_ge(p.s_ve.h, p.s_ve.n)
        I1(t.matmul(pini[:], p.ones1[:], p.b2row[:], start=True, stop=True),
           p.s_pe)
        I1(t.matmul(pini2[:], p.gnwrow[:], p.ones1[:, 0:1],
                    start=True, stop=True), p.s_pe)
        v.wait_ge(p.s_pe.h, p.s_pe.n)
        v.tensor_copy(p.b2b[:], pini[:])
        I1(v.tensor_copy(p.gnwcol[:], pini2[:]), p.s_ve)
    g.wait_ge(p.s_ge.h, p.s_ge.n)
    s.wait_ge(p.s_ge.h, p.s_ge.n)
    s.activation(p.stats[:, 0:1], p.eps_c[:], AF.Square)
    s.activation(p.stats[:, 0:1], p.eps_c[:], AF.Ln)
    s.activation(p.stats[:, 0:1], p.eps_c[:], AF.Exp)
    I1(s.activation(p.stats[:, 0:1], p.eps_c[:], AF.Relu), p.s_se)
    s.wait_ge(p.s_se.h, p.s_se.n)
    g.wait_ge(p.s_w.h, p.s_w.n)
    y.wait_ge(p.s_ve.h, p.s_ve.n)


def emit_A1(p):
    nc = p.nc
    sb = p.ses.enter_context
    p.esKV = ExitStack()
    p.K_sb = p.esKV.enter_context(
        nc.sbuf_tensor("K_sb", [128, SEQ * NKV * T], BF16, side="right"))
    p.V_sb = p.esKV.enter_context(
        nc.sbuf_tensor("V_sb", [128, SEQ * NT * 512], BF16, side="right"))

    aWq_sb = sb(nc.sbuf_tensor("aWq_sb", [128, 16 * 2048], BF16))
    aWk_sb = sb(nc.sbuf_tensor("aWk_sb", [128, 16 * 512], BF16))
    aWv_sb = sb(nc.sbuf_tensor("aWv_sb", [128, 16 * 512], BF16))
    cos_sb = sb(nc.sbuf_tensor("cos_sb", [128, 16 * 128], BF16))
    sin_sb = sb(nc.sbuf_tensor("sin_sb", [128, 16 * 128], BF16))
    xt = [sb(nc.sbuf_tensor(f"xtA{i}", [128, D], F32)) for i in range(2)]
    ropeA = sb(nc.sbuf_tensor("ropeA", [128, 128], F32))
    ropeB = sb(nc.sbuf_tensor("ropeB", [128, 128], F32))
    xn_bf = sb(nc.sbuf_tensor("xn_bfA", [128, D], BF16))
    xnT = sb(nc.sbuf_tensor("xnTA", [128, D], BF16))
    qrot = sb(nc.sbuf_tensor("qrotA", [128, NH * HD], BF16))
    krot = sb(nc.sbuf_tensor("krotA", [128, NKV * HD], BF16))
    tmp8 = sb(nc.sbuf_tensor("tmp8A", [128, 8], F32))
    pQ = sb(nc.psum_tensor("pQ", [128, 2048], F32))
    pKV = sb(nc.psum_tensor("pKV", [128, 1024], F32))
    pT = [sb(nc.psum_tensor(f"pTA{i}", [128, 512], BF16)) for i in range(2)]
    p.pT_free = {}

    g, v, s, t, y = p.g, p.v, p.s, p.t, p.y

    for c in range(16):
        I16(g.dma_start(out=aWq_sb[:, c * 2048:(c + 1) * 2048],
                        in_=p.aWq[c * 128:(c + 1) * 128, :]), p.s_w)
        I16(g.dma_start(out=aWk_sb[:, c * 512:(c + 1) * 512],
                        in_=p.aWk[c * 128:(c + 1) * 128, :]), p.s_w)
        I16(g.dma_start(out=aWv_sb[:, c * 512:(c + 1) * 512],
                        in_=p.aWv[c * 128:(c + 1) * 128, :]), p.s_w)
    w_done = p.s_w.n

    I16(g.dma_start(out=cos_sb[:],
                    in_=p.coscat.rearrange("(n p) m -> p n m", p=128)), p.s_w)
    I16(g.dma_start(out=sin_sb[:],
                    in_=p.sincat.rearrange("(n p) m -> p n m", p=128)), p.s_w)
    trig_done = p.s_w.n
    ve_apply = {}
    st_q = {}
    for j in range(NTILES):
        sq, jj = divmod(j, NT)
        buf = j % 2
        sx = p.s_x0 if buf == 0 else p.s_x1
        if j >= 2:
            y.wait_ge(p.s_ve.h, ve_apply[j - 2])
        x_here = I16(y.dma_start(xt[buf][:], p.x0[j * 128:(j + 1) * 128, :]), sx)
        v.wait_ge(sx.h, x_here)
        s.wait_ge(sx.h, x_here)
        if j >= 1:
            v.wait_ge(p.s_pe.h, p.pe_xn_done)
            s.wait_ge(p.s_pe.h, p.pe_xn_done)
        ve_apply[j] = _ln(p, xt[buf][:], xn_bf[:], tmp8, xn_bf[:],
                          store_col=j)

        if j == 0:
            t.wait_ge(p.s_w.h, w_done)
        t.wait_ge(p.s_ve.h, ve_apply[j])
        _xnT(p, xn_bf, xnT, pT)
        p.pe_xn_done = p.s_pe.n
        if j >= 1:
            t.wait_ge(p.s_ve.h, p.ve_kv_done)
        for d in range(16):
            t.matmul(pKV[:, 0:512], xnT[:, d * 128:(d + 1) * 128],
                     aWk_sb[:, d * 512:(d + 1) * 512], start=(d == 0),
                     stop=(d == 15))
        for d in range(16):
            mm = t.matmul(pKV[:, 512:1024], xnT[:, d * 128:(d + 1) * 128],
                          aWv_sb[:, d * 512:(d + 1) * 512], start=(d == 0),
                          stop=(d == 15))
        pe_kv = I1(mm, p.s_pe)
        if j >= 1:
            t.wait_ge(p.s_ve.h, p.ve_q_done)
        for jc in range(4):
            for d in range(16):
                mm = t.matmul(
                    pQ[:, jc * 512:(jc + 1) * 512],
                    xnT[:, d * 128:(d + 1) * 128],
                    aWq_sb[:, d * 2048 + jc * 512:d * 2048 + (jc + 1) * 512],
                    start=(d == 0), stop=(d == 15))
        pe_q = I1(mm, p.s_pe)

        if j == 0:
            v.wait_ge(p.s_w.h, trig_done)
        v.wait_ge(p.s_pe.h, pe_kv)
        ct = cos_sb[:, jj * 128:(jj + 1) * 128]
        sn = sin_sb[:, jj * 128:(jj + 1) * 128]
        v.tensor_copy(p.V_sb[:, j * 512:(j + 1) * 512], pKV[:, 512:1024])
        for h in range(NKV):
            v.tensor_tensor(ropeB[:, 0:64], pKV[:, h * 128 + 64:(h + 1) * 128],
                            sn[:, 0:64], ALU.mult)
            v.tensor_tensor(ropeB[:, 64:128], pKV[:, h * 128:h * 128 + 64],
                            sn[:, 64:128], ALU.mult)
            v.tensor_tensor(ropeA[:], pKV[:, h * 128:(h + 1) * 128], ct, ALU.mult)
            mm = v.tensor_tensor(krot[:, h * 128:(h + 1) * 128], ropeA[:],
                                 ropeB[:], ALU.add)
        p.ve_kv_done = I1(mm, p.s_ve)
        v.wait_ge(p.s_pe.h, pe_q)
        if j >= 1:
            v.wait_ge(p.s_st.h, st_q[j - 1])
        for h in range(NH):
            v.tensor_tensor(ropeB[:, 0:64], pQ[:, h * 128 + 64:(h + 1) * 128],
                            sn[:, 0:64], ALU.mult)
            v.tensor_tensor(ropeB[:, 64:128], pQ[:, h * 128:h * 128 + 64],
                            sn[:, 64:128], ALU.mult)
            v.tensor_tensor(ropeA[:], pQ[:, h * 128:(h + 1) * 128], ct, ALU.mult)
            mm = v.tensor_tensor(qrot[:, h * 128:(h + 1) * 128], ropeA[:],
                                 ropeB[:], ALU.add)
        p.ve_q_done = I1(mm, p.s_ve)
        ve_q = p.s_ve.n

        t.wait_ge(p.s_ve.h, p.ve_kv_done)
        pb = pT[0]
        last = p.pT_free.get(id(pb), 0)
        if last:
            t.wait_ge(p.s_v2.h, last)
        for hh in range(NKV):
            mm = t.transpose(pb[:, hh * 128:(hh + 1) * 128],
                             krot[:, hh * 128:(hh + 1) * 128], p.ident[:])
        I1(mm, p.s_pe)
        v.wait_ge(p.s_pe.h, p.s_pe.n)
        kdst = p.K_sb[:].rearrange("p (a b) -> p a b", a=SEQ * NKV)[
            :, sq * NKV:(sq + 1) * NKV, jj * 128:(jj + 1) * 128]
        p.pT_free[id(pb)] = I1(
            v.tensor_copy(kdst, pb[:].rearrange("p (a b) -> p a b", a=4)),
            p.s_v2)

        y.wait_ge(p.s_ve.h, ve_q)
        st_q[j] = I16(y.dma_start(p.qscr[sq, jj * 128:(jj + 1) * 128, :],
                                  qrot[:]), p.s_st)

    y.wait_ge(p.s_st.h, p.s_st.n)
    g.wait_ge(p.s_w.h, p.s_w.n)
    if p.tap == "A1":
        y.wait_ge(p.s_v2.h, p.s_v2.n)
        y.wait_ge(p.s_se.h, p.s_se.n)
        I16(y.dma_start(p.stats_tap, p.stats[:]), p.s_st)
        I16(y.dma_start(p.kfm_tap, p.K_sb[:]), p.s_st)
        I16(y.dma_start(p.vtm_tap, p.V_sb[:]), p.s_st)
        y.wait_ge(p.s_st.h, p.s_st.n)


def emit_A2(p):
    nc = p.nc
    sb = p.ses.enter_context
    aWo_sb = sb(nc.sbuf_tensor("aWo_sb", [128, 16 * 2048], BF16))
    qld = [sb(nc.sbuf_tensor(f"qldA2{i}", [128, 512], BF16)) for i in range(2)]
    qfm = sb(nc.sbuf_tensor("qfmA2", [128, 128], BF16))
    pexp2 = [sb(nc.sbuf_tensor(f"pexpA2{i}", [128, 2048], BF16))
             for i in range(2)]
    diag2 = [sb(nc.sbuf_tensor(f"diagA2{i}", [128, 256], BF16))
             for i in range(2)]
    PT_sb = sb(nc.sbuf_tensor("PTA2", [128, 16 * 512], BF16))
    O_sb = sb(nc.sbuf_tensor("OA2", [128, 16 * 128], BF16))
    xt = [sb(nc.sbuf_tensor(f"xtA2{i}", [128, D], F32)) for i in range(2)]
    x1stg = sb(nc.sbuf_tensor("x1stgA2", [128, D], F32))
    lrc = sb(nc.sbuf_tensor("lrcA2", [128, 32], F32))
    pBig = sb(nc.psum_tensor("pBigA2", [128, 2048], F32))
    pTT = [sb(nc.psum_tensor(f"pTTA2{i}", [128, 512], F32)) for i in range(2)]
    pO = sb(nc.psum_tensor("pOA2", [128, 512], F32))
    pTq = sb(nc.psum_tensor("pTqA2", [128, 128], BF16))

    g, v, s, t, y = p.g, p.v, p.s, p.t, p.y

    for c in range(16):
        I16(g.dma_start(out=aWo_sb[:, c * 2048:(c + 1) * 2048],
                        in_=p.aWo[c * 128:(c + 1) * 128, :]), p.s_w)
    wo_done = p.s_w.n

    qtr_done = {}
    p.q_here = {}
    p.pe_S_last = 0
    p.se_qfm_last = 0
    p.se_expA_last = 0
    p.se_expB_last = 0
    p.sse_alpha = {}
    p.ve_norm_last = 0
    p.norm_done = {}
    p.exp_done = {}
    p.last_unit = None
    p.pe_PT = {}
    p.head_norm = {}
    p.pt_ready = {}
    p.ve_O_last = 0
    p.ve_resid_last = 0
    ve_resid = {}
    pTT_free = {}
    st_x1 = {}
    first_wo = True

    for s0 in range(SEQ):
        for qt in range(NT):
            gidx = (s0 * NT + qt)
            def emitP(gg, hl):
                hu = gg * 4 + hl
                pexp = pexp2[hu % 2]
                qi = gidx * 4 + gg
                qbuf = qi % 2
                sq = p.s_q0 if qbuf == 0 else p.s_q1
                if hl == 0:
                    if qi >= 2:
                        y.wait_ge(p.s_pe.h, qtr_done[qi - 2])
                    p.q_here[qi] = I16(y.dma_start(
                        qld[qbuf][:],
                        p.qscr[s0, qt * 128:(qt + 1) * 128,
                               4 * gg * 128:(4 * gg + 4) * 128]), sq)
                t.wait_ge(sq.h, p.q_here[qi])
                if p.se_qfm_last:
                    t.wait_ge(p.s_s2.h, p.se_qfm_last)
                I1(t.transpose(pTq[:], qld[qbuf][:, hl * 128:(hl + 1) * 128],
                               p.ident[:]), p.s_pe)
                if hl == 3:
                    qtr_done[qi] = p.s_pe.n
                s.wait_ge(p.s_pe.h, p.s_pe.n)
                if p.pe_S_last:
                    s.wait_ge(p.s_pe.h, p.pe_S_last)
                I1(s.activation(qfm[:], pTq[:], AF.Copy, bias=0.0,
                                scale=float(QK_SCALE)), p.s_s2)
                p.se_qfm_last = p.s_s2.n
                t.wait_ge(p.s_s2.h, p.se_qfm_last)
                if gidx > 0 and gg == 0 and hl == 0:
                    t.wait_ge(p.s_ve.h, p.ve_resid_last)
                kbase = (s0 * NKV + gg) * 2048
                p.pe_S_half = []
                for jc in range(4):
                    if jc == 0 and p.se_expA_last:
                        t.wait_ge(p.s_se.h, p.se_expA_last)
                    if jc == 2 and p.se_expB_last:
                        t.wait_ge(p.s_se.h, p.se_expB_last)
                    mm = t.matmul(pBig[:, jc * 512:(jc + 1) * 512], qfm[:],
                                  p.K_sb[:, kbase + jc * 512:kbase + (jc + 1) * 512],
                                  start=True, stop=True)
                    if jc % 2 == 1:
                        p.pe_S_half.append(I1(mm, p.s_pe))
                p.pe_S_last = p.pe_S_half[-1]

            def emitP2(gg, hl):
                hu = gg * 4 + hl
                par = hu % 2
                pexp = pexp2[par]
                b = 16 * par
                mA, mB, t1, t2 = b, b + 1, b + 3, b + 4
                aA, aB, lA, lB = b + 5, b + 6, b + 7, b + 8
                lg, rc, lA2, lB2 = b + 9, b + 10, b + 11, b + 12
                gA, gB = b + 13, b + 14
                v.wait_ge(p.s_pe.h, p.pe_S_half[0])
                mmA = I1(v.reduce_max(lrc[:, mA:mA + 1], pBig[:, 0:1024],
                                      axis=AX, negate=True), p.s_ve)
                s.wait_ge(p.s_ve.h, mmA)
                if p.pe_PT.get(par, 0):
                    s.wait_ge(p.s_pe.h, p.pe_PT[par])
                p.se_expA_last = I1(
                    s.activation(pexp[:, 0:1024], pBig[:, 0:1024], AF.Exp,
                                 bias=lrc[:, mA:mA + 1], scale=1.0,
                                 accum_out=lrc[:, lA:lA + 1]), p.s_se)
                v.wait_ge(p.s_pe.h, p.pe_S_half[1])
                mmB = I1(v.reduce_max(lrc[:, mB:mB + 1], pBig[:, 1024:2048],
                                      axis=AX, negate=True), p.s_ve)
                s.wait_ge(p.s_ve.h, mmB)
                p.se_expB_last = I1(
                    s.activation(pexp[:, 1024:2048], pBig[:, 1024:2048],
                                 AF.Exp, bias=lrc[:, mB:mB + 1], scale=1.0,
                                 accum_out=lrc[:, lB:lB + 1]), p.s_se)
                p.exp_done[(gg, hl)] = p.se_expB_last
                s.activation(lrc[:, t1:t1 + 1], lrc[:, mB:mB + 1], AF.Relu,
                             bias=lrc[:, mA:mA + 1], scale=-1.0)
                s.activation(lrc[:, t2:t2 + 1], lrc[:, mA:mA + 1], AF.Relu,
                             bias=lrc[:, mB:mB + 1], scale=-1.0)
                s.activation(lrc[:, aA:aA + 1], lrc[:, t1:t1 + 1], AF.Exp,
                             bias=0.0, scale=-1.0)
                p.sse_alpha[(gg, hl)] = I1(
                    s.activation(lrc[:, aB:aB + 1], lrc[:, t2:t2 + 1],
                                 AF.Exp, bias=0.0, scale=-1.0), p.s_se)

            def emitCombine(gg, hl):
                hu = gg * 4 + hl
                par = hu % 2
                b = 16 * par
                aA, aB, lA, lB = b + 5, b + 6, b + 7, b + 8
                lg, rc, lA2, lB2 = b + 9, b + 10, b + 11, b + 12
                gA, gB = b + 13, b + 14
                v.wait_ge(p.s_se.h, p.sse_alpha[(gg, hl)])
                v.tensor_tensor(lrc[:, lA2:lA2 + 1], lrc[:, lA:lA + 1],
                                lrc[:, aA:aA + 1], ALU.mult)
                v.tensor_tensor(lrc[:, lB2:lB2 + 1], lrc[:, lB:lB + 1],
                                lrc[:, aB:aB + 1], ALU.mult)
                vsep(p)
                v.tensor_tensor(lrc[:, lg:lg + 1], lrc[:, lA2:lA2 + 1],
                                lrc[:, lB2:lB2 + 1], ALU.add)
                vsep(p)
                v.reciprocal(lrc[:, rc:rc + 1], lrc[:, lg:lg + 1])
                vsep(p)
                v.tensor_tensor(lrc[:, gA:gA + 1], lrc[:, aA:aA + 1],
                                lrc[:, rc:rc + 1], ALU.mult)
                v.tensor_tensor(lrc[:, gB:gB + 1], lrc[:, aB:aB + 1],
                                lrc[:, rc:rc + 1], ALU.mult)
                vsep(p)
                if p.pe_PT.get(par, 0):
                    v.wait_ge(p.s_pe.h, p.pe_PT[par])
                v.tensor_scalar(out=diag2[par][:, 0:128], in0=p.ident[:],
                                scalar1=lrc[:, gA:gA + 1], scalar2=None,
                                op0=ALU.mult)
                p.ve_norm_last = I1(
                    v.tensor_scalar(out=diag2[par][:, 128:256],
                                    in0=p.ident[:],
                                    scalar1=lrc[:, gB:gB + 1], scalar2=None,
                                    op0=ALU.mult), p.s_ve)
                vsep(p)
                p.norm_done[par] = p.ve_norm_last
                p.head_norm[(gg, hl)] = p.ve_norm_last

            def emitT(gg, hl):
                hu = gg * 4 + hl
                par = hu % 2
                pexp = pexp2[par]
                t.wait_ge(p.s_ve.h, p.head_norm[(gg, hl)])
                t.wait_ge(p.s_se.h, p.exp_done[(gg, hl)])
                for grp in range(4):
                    pb = pTT[grp % 2]
                    fr = pTT_free.get(grp % 2, (0, 0))
                    if fr[0]:
                        t.wait_ge(p.s_s2.h, fr[0])
                    if fr[1]:
                        t.wait_ge(p.s_v2.h, fr[1])
                    dg = (diag2[par][:, 0:128] if grp < 2
                          else diag2[par][:, 128:256])
                    for c in range(4):
                        kt = grp * 4 + c
                        mm = t.matmul(pb[:, c * 128:(c + 1) * 128],
                                      pexp[:, kt * 128:(kt + 1) * 128],
                                      dg, start=True, stop=True)
                    I1(mm, p.s_pe)
                    if grp == 3:
                        p.pe_PT[par] = p.s_pe.n
                    dst = PT_sb[:].rearrange("p (a b) -> p a b", a=16)[
                        :, grp * 4:(grp + 1) * 4, hl * 128:(hl + 1) * 128]
                    src = pb[:].rearrange("p (a b) -> p a b", a=4)
                    if grp % 2 == 0:
                        s.wait_ge(p.s_pe.h, p.s_pe.n)
                        ec = s.copy(dst, src)
                        ec.then_inc(p.s_s2.h, 1)
                        p.s_s2.n += 1
                        pTT_free[grp % 2] = (p.s_s2.n, 0)
                    else:
                        v.wait_ge(p.s_pe.h, p.s_pe.n)
                        I1(v.tensor_copy(dst, src), p.s_v2)
                        pTT_free[grp % 2] = (0, p.s_v2.n)
                p.pt_ready[(gg, hl)] = (p.s_s2.n, p.s_v2.n)

            def emitPV(gg):
                t.wait_ge(p.s_s2.h, p.pt_ready[(gg, 3)][0])
                t.wait_ge(p.s_v2.h, p.pt_ready[(gg, 3)][1])
                if p.ve_O_last:
                    t.wait_ge(p.s_ve.h, p.ve_O_last)
                vbase = s0 * NT
                for kt in range(16):
                    mm = t.matmul(pO[:],
                                  p.V_sb[:, (vbase + kt) * 512 + gg * 128:
                                         (vbase + kt) * 512 + (gg + 1) * 128],
                                  PT_sb[:, kt * 512:(kt + 1) * 512],
                                  start=(kt == 0), stop=(kt == 15))
                I1(mm, p.s_pe)
                v.wait_ge(p.s_pe.h, p.s_pe.n)
                p.ve_O_last = I1(
                    v.tensor_copy(O_sb[:, gg * 512:(gg + 1) * 512], pO[:]),
                    p.s_ve)
                vsep(p)

            units = [(gg, hl) for gg in range(4) for hl in range(4)]
            emitP(*units[0])
            emitP2(*units[0])
            emitCombine(*units[0])
            for i in range(1, 16):
                emitP(*units[i])
                emitP2(*units[i])
                emitCombine(*units[i])
                emitT(*units[i - 1])
                if units[i - 1][1] == 3:
                    emitPV(units[i - 1][0])
            emitT(*units[15])
            emitPV(3)
            if first_wo:
                t.wait_ge(p.s_w.h, wo_done)
                first_wo = False
            t.wait_ge(p.s_ve.h, p.ve_O_last)
            t.wait_ge(p.s_se.h, p.se_expB_last)
            for dc in range(4):
                for j16 in range(16):
                    mm = t.matmul(
                        pBig[:, dc * 512:(dc + 1) * 512],
                        O_sb[:, j16 * 128:(j16 + 1) * 128],
                        aWo_sb[:, j16 * 2048 + dc * 512:j16 * 2048 + (dc + 1) * 512],
                        start=(j16 == 0), stop=(j16 == 15))
            pe_wo = I1(mm, p.s_pe)
            xbuf = gidx % 2
            sx = p.s_x0 if xbuf == 0 else p.s_x1
            if gidx >= 2:
                y.wait_ge(p.s_ve.h, ve_resid[gidx - 2])
            x_here = I16(y.dma_start(
                xt[xbuf][:],
                p.x0[(s0 * NT + qt) * 128:(s0 * NT + qt + 1) * 128, :]), sx)
            v.wait_ge(sx.h, x_here)
            v.wait_ge(p.s_pe.h, pe_wo)
            if gidx >= 1:
                v.wait_ge(p.s_st.h, st_x1[gidx - 1])
            ve_resid[gidx] = I1(
                v.tensor_tensor(x1stg[:], pBig[:], xt[xbuf][:], ALU.add),
                p.s_ve)
            p.ve_resid_last = ve_resid[gidx]
            y.wait_ge(p.s_ve.h, ve_resid[gidx])
            st_x1[gidx] = I16(
                y.dma_start(p.x1scr[(s0 * NT + qt) * 128:
                                    (s0 * NT + qt + 1) * 128, :], x1stg[:]),
                p.s_st)

    y.wait_ge(p.s_st.h, p.s_st.n)
    g.wait_ge(p.s_w.h, p.s_w.n)
    p.esKV.close()
    p.esKV = None


def emit_ffn(p, src, dst, tag=""):
    nc = p.nc
    sb = p.ses.enter_context
    fW1_sb = sb(nc.sbuf_tensor(f"fW1_sb{tag}", [128, 16 * 2048], BF16))
    fV_sb = sb(nc.sbuf_tensor(f"fV_sb{tag}", [128, 16 * 512], BF16))
    fU_sb = sb(nc.sbuf_tensor(f"fU_sb{tag}", [128, 4 * 2048], BF16))
    fW2_sb = sb(nc.sbuf_tensor(f"fW2_sb{tag}", [128, 16 * 8], BF16))
    fb1_sb = sb(nc.sbuf_tensor(f"fb1_sb{tag}", [128, 16], F32))
    xt = [sb(nc.sbuf_tensor(f"xtF{tag}{i}", [128, D], F32)) for i in range(4)]
    xn_bf = sb(nc.sbuf_tensor(f"xn_bfF{tag}", [128, D], BF16))
    xnT_st = sb(nc.sbuf_tensor(f"xnT_stF{tag}", [128, 16 * 512], BF16))
    h_sb = sb(nc.sbuf_tensor(f"h_sbF{tag}", [128, 16 * 512], BF16))
    alpha = sb(nc.sbuf_tensor(f"alphaF{tag}", [128, 4 * 8], F32))
    altmp = sb(nc.sbuf_tensor(f"altmpF{tag}", [128, 12], F32))
    vxs = [sb(nc.sbuf_tensor(f"vxsF{tag}{i}", [128, 512], BF16)) for i in range(2)]
    vxT = [sb(nc.sbuf_tensor(f"vxTF{tag}{i}", [128, 512], BF16)) for i in range(2)]
    x2stg = sb(nc.sbuf_tensor(f"x2stgF{tag}", [128, D], F32))
    tmp8 = sb(nc.sbuf_tensor(f"tmp8F{tag}", [128, 8], F32))
    pH = sb(nc.psum_tensor(f"pHF{tag}", [128, 512], F32))
    pAl = sb(nc.psum_tensor(f"pAlF{tag}", [128, 8], F32))
    pVx = sb(nc.psum_tensor(f"pVxF{tag}", [128, 512], F32))
    pTT = [sb(nc.psum_tensor(f"pTTF{tag}", [128, 512], BF16))]
    pY = sb(nc.psum_tensor(f"pYF{tag}", [128, 2048], F32))

    g, v, s, t, y = p.g, p.v, p.s, p.t, p.y

    for c in range(16):
        I16(g.dma_start(out=fW1_sb[:, c * 2048:(c + 1) * 2048],
                        in_=p.fW1[c * 128:(c + 1) * 128, :]), p.s_w)
        I16(g.dma_start(out=fV_sb[:, c * 512:(c + 1) * 512],
                        in_=p.fV[c * 128:(c + 1) * 128, :]), p.s_w)
        I16(g.dma_start(out=fW2_sb[:, c * 8:(c + 1) * 8],
                        in_=p.fW2[c * 128:(c + 1) * 128, :]), p.s_w)
    for c in range(4):
        I16(g.dma_start(out=fU_sb[:, c * 2048:(c + 1) * 2048],
                        in_=p.fU[c * 128:(c + 1) * 128, :]), p.s_w)
    I16(g.dma_start(out=fb1_sb[:], in_=p.fb1c), p.s_w)
    w_done = p.s_w.n

    st_x2 = {}
    ve_resid = {}
    pe_alpha_last = 0
    se_h_last = 0
    se_al_last = 0
    p.ve_al_done = {}
    p.ve_vx_free = {}
    p.pT_free.pop(id(pTT[0]), None)
    pTT_free = [0]
    first = True
    for sti in range(8):
        parity = sti % 2
        sx = p.s_x0 if parity == 0 else p.s_x1
        for j in range(4):
            gi = sti * 4 + j
            if sti >= 1:
                y.wait_ge(p.s_ve.h, ve_resid[(sti - 1) * 4 + j])
            I16(y.dma_start(xt[j][:],
                            src[gi * 128:(gi + 1) * 128, :]), sx)
        x_all = sx.n
        for j in range(4):
            v.wait_ge(sx.h, x_all)
            s.wait_ge(sx.h, x_all)
            if not first:
                v.wait_ge(p.s_pe.h, p.pe_xn_doneF)
                s.wait_ge(p.s_pe.h, p.pe_xn_doneF)
            first = False
            if sti >= 1 and j == 0:
                v.wait_ge(p.s_pe.h, p.pe_h_lastF)
            ve_ap = _ln(p, xt[j][:], xn_bf[:], tmp8, xn_bf[:])
            t.wait_ge(p.s_ve.h, ve_ap)
            if sti == 0 and j == 0:
                t.wait_ge(p.s_w.h, w_done)
            _xnT(p, xn_bf, None, pTT + pTT,
                 dst_map=lambda grp, j=j: xnT_st[:].rearrange(
                     "p (a b) -> p a b", a=16)[:, grp * 4:(grp + 1) * 4,
                                               j * 128:(j + 1) * 128])
            p.pe_xn_doneF = p.s_pe.n
        for e in range(16):
            if se_h_last:
                t.wait_ge(p.s_se.h, se_h_last)
            if e == 0 and p.ve_vx_free.get(1, 0):
                t.wait_ge(p.s_ve.h, p.ve_vx_free[1])
            for d in range(16):
                mm = t.matmul(pH[:],
                              fW1_sb[:, d * 2048 + e * 128:d * 2048 + (e + 1) * 128],
                              xnT_st[:, d * 512:(d + 1) * 512],
                              start=(d == 0), stop=(d == 15))
            I1(mm, p.s_pe)
            if e == 15:
                p.pe_h_lastF = p.s_pe.n
            s.wait_ge(p.s_pe.h, p.s_pe.n)
            if pe_alpha_last:
                s.wait_ge(p.s_pe.h, pe_alpha_last)
            if p.ve_vx_free.get(1, 0):
                s.wait_ge(p.s_ve.h, p.ve_vx_free[1])
            se_h_last = I1(s.activation(h_sb[:, e * 512:(e + 1) * 512], pH[:],
                                        AF.Relu, bias=fb1_sb[:, e:e + 1],
                                        scale=1.0), p.s_se)
        for j in range(4):
            t.wait_ge(p.s_se.h, se_h_last)
            if se_al_last:
                t.wait_ge(p.s_se.h, se_al_last)
            for e in range(16):
                mm = t.matmul(pAl[:],
                              h_sb[:, e * 512 + j * 128:e * 512 + (j + 1) * 128],
                              fW2_sb[:, e * 8:(e + 1) * 8],
                              start=(e == 0), stop=(e == 15))
            pe_alpha_last = I1(mm, p.s_pe)
            v.wait_ge(p.s_pe.h, pe_alpha_last)
            if p.ve_al_done.get(j, 0):
                pass
            v.tensor_tensor(altmp[:, 0:8], pAl[:], p.b2b[:], ALU.add)
            vsep(p)
            I1(v.reduce_max(altmp[:, 8:9], altmp[:, 0:8], axis=AX, negate=True),
               p.s_ve)
            s.wait_ge(p.s_ve.h, p.s_ve.n)
            se_al_last = I1(
                s.activation(tmp8[:, 0:8], altmp[:, 0:8], AF.Exp,
                             bias=altmp[:, 8:9], scale=1.0,
                             accum_out=altmp[:, 9:10]), p.s_se)
            v.wait_ge(p.s_se.h, se_al_last)
            v.reciprocal(altmp[:, 10:11], altmp[:, 9:10])
            vsep(p)
            p.ve_al_done[j] = I1(v.tensor_scalar(
                out=alpha[:, j * 8:(j + 1) * 8], in0=tmp8[:, 0:8],
                scalar1=altmp[:, 10:11], scalar2=None, op0=ALU.mult), p.s_ve)
            vsep(p)
        for j in range(4):
            gi = sti * 4 + j
            par = j % 2
            pvx = pVx if par == 0 else pH
            if par == 1:
                t.wait_ge(p.s_se.h, se_h_last)
            if p.ve_vx_free.get(par, 0):
                t.wait_ge(p.s_ve.h, p.ve_vx_free[par])
            for d in range(16):
                mm = t.matmul(pvx[:, 0:512],
                              xnT_st[:, d * 512 + j * 128:d * 512 + (j + 1) * 128],
                              fV_sb[:, d * 512:(d + 1) * 512],
                              start=(d == 0), stop=(d == 15))
            I1(mm, p.s_pe)
            v.wait_ge(p.s_pe.h, p.s_pe.n)
            v.wait_ge(p.s_ve.h, p.ve_al_done[j])
            if p.pT_free.get(("vxsT", par), 0):
                v.wait_ge(p.s_pe.h, p.pT_free[("vxsT", par)])
            for k in range(8):
                mm = v.tensor_scalar(out=vxs[par][:, k * 64:(k + 1) * 64],
                                     in0=pvx[:, k * 64:(k + 1) * 64],
                                     scalar1=alpha[:, j * 8 + k:j * 8 + k + 1],
                                     scalar2=None, op0=ALU.mult)
            p.ve_vx_free[par] = I1(mm, p.s_ve)
            vsep(p)
            t.wait_ge(p.s_ve.h, p.ve_vx_free[par])
            if pTT_free[0]:
                t.wait_ge(p.s_v2.h, pTT_free[0])
            for c in range(4):
                mm = t.transpose(pTT[0][:, c * 128:(c + 1) * 128],
                                 vxs[par][:, c * 128:(c + 1) * 128], p.ident[:])
            I1(mm, p.s_pe)
            p.pT_free[("vxsT", par)] = p.s_pe.n
            s.wait_ge(p.s_pe.h, p.s_pe.n)
            sc = s.copy(vxT[par][:], pTT[0][:])
            sc.then_inc(p.s_v2.h, 1)
            p.s_v2.n += 1
            pTT_free[0] = p.s_v2.n
            t.wait_ge(p.s_v2.h, pTT_free[0])
            xb = gi % 2
            for dh in range(2):
                pyb = pY[:, dh * 1024:(dh + 1) * 1024]
                if ve_resid.get((gi - 1, dh), 0):
                    t.wait_ge(p.s_ve.h, ve_resid[(gi - 1, dh)])
                for kr in range(4):
                    for dc in range(2):
                        mm = t.matmul(
                            pyb[:, dc * 512:(dc + 1) * 512],
                            vxT[par][:, kr * 128:(kr + 1) * 128],
                            fU_sb[:, kr * 2048 + (dh * 2 + dc) * 512:
                                  kr * 2048 + (dh * 2 + dc + 1) * 512],
                            start=(kr == 0), stop=(kr == 3))
                pe_y = I1(mm, p.s_pe)
                v.wait_ge(p.s_pe.h, pe_y)
                if dh == 0 and st_x2.get(gi - 1, 0):
                    v.wait_ge(p.s_st.h, st_x2[gi - 1])
                ve_resid[(gi, dh)] = I1(
                    v.tensor_tensor(x2stg[:, dh * 1024:(dh + 1) * 1024],
                                    pyb,
                                    xt[j][:, dh * 1024:(dh + 1) * 1024],
                                    ALU.add), p.s_ve)
            ve_resid[gi] = ve_resid[(gi, 1)]
            y.wait_ge(p.s_ve.h, ve_resid[gi])
            st_x2[gi] = I16(y.dma_start(dst[gi * 128:(gi + 1) * 128, :],
                                        x2stg[:]), p.s_st)
    y.wait_ge(p.s_st.h, p.s_st.n)
    g.wait_ge(p.s_w.h, p.s_w.n)


def emit_A3(p):
    emit_ffn(p, p.x1scr, p.x2scr, tag="a")


def emit_B1a(p):
    nc = p.nc
    sb = p.ses.enter_context
    bWq_sb = sb(nc.sbuf_tensor("bWq_sb", [128, 16 * 1024], BF16))
    bWk_sb = sb(nc.sbuf_tensor("bWk_sb", [128, 16 * 1024], BF16))
    bWs_sb = sb(nc.sbuf_tensor("bWs_sb", [128, 16 * 1024], BF16))
    xt = [sb(nc.sbuf_tensor(f"xtB1{i}", [128, D], F32)) for i in range(4)]
    xn_bf = sb(nc.sbuf_tensor("xn_bfB1", [128, D], BF16))
    xnT_st = sb(nc.sbuf_tensor("xnT_stB1", [128, 16 * 512], BF16))
    qstg = [sb(nc.sbuf_tensor(f"qstgB1{i}", [128, 512], BF16)) for i in range(2)]
    kstg = [sb(nc.sbuf_tensor(f"kstgB1{i}", [128, 512], BF16)) for i in range(2)]
    estg = sb(nc.sbuf_tensor("estgB1", [128, 1024], BF16))
    e32 = sb(nc.sbuf_tensor("e32B1", [128, 1024], F32))
    rzstg = sb(nc.sbuf_tensor("rzstgB1", [128, 1024], BF16))
    onescol = sb(nc.sbuf_tensor("onescolB1", [128, 1], BF16))
    ones1b = sb(nc.sbuf_tensor("ones1bB1", [1, 128], BF16))
    prehi = sb(nc.sbuf_tensor("prehiB1", [1, 1024], BF16))
    prelo = sb(nc.sbuf_tensor("preloB1", [1, 1024], BF16))
    prelof = sb(nc.sbuf_tensor("prelofB1", [1, 1024], F32))
    tmp8 = sb(nc.sbuf_tensor("tmp8B1", [128, 8], F32))
    pP = sb(nc.psum_tensor("pPB1", [128, 512], F32))
    pS2 = sb(nc.psum_tensor("pS2B1", [128, 1024], F32))
    pZ = sb(nc.psum_tensor("pZB1", [128, 1024], F32))
    pPre = sb(nc.psum_tensor("pPreB1", [1, 1024], F32))
    pT0 = sb(nc.psum_tensor("pTB1", [128, 512], BF16))
    pT = [pT0, pT0]
    p.pT_free = {}

    g, v, s, t, y = p.g, p.v, p.s, p.t, p.y

    for c in range(16):
        I16(g.dma_start(out=bWq_sb[:, c * 1024:(c + 1) * 1024],
                        in_=p.bWq[c * 128:(c + 1) * 128, :]), p.s_w)
        I16(g.dma_start(out=bWk_sb[:, c * 1024:(c + 1) * 1024],
                        in_=p.bWk[c * 128:(c + 1) * 128, :]), p.s_w)
        I16(g.dma_start(out=bWs_sb[:, c * 1024:(c + 1) * 1024],
                        in_=p.bWs[c * 128:(c + 1) * 128, :]), p.s_w)
    w_done = p.s_w.n
    I1(v.memset(onescol[:], 1.0), p.s_ve)
    I1(v.memset(ones1b[:], 1.0), p.s_ve)

    se_stg_last = {}
    st_last = {}
    ve_resid = {}
    first = True
    pe_proj_last = 0
    se_E_last = 0
    ve_rz_last = 0
    ve_pre_last = 0
    pe_zc_last = 0
    pe_pre_count = 0
    for s0 in range(SEQ):
        for ch in range(4):
            sti = s0 * 4 + ch
            parity = sti % 2
            sx = p.s_x0 if parity == 0 else p.s_x1
            for j in range(4):
                gi = sti * 4 + j
                if sti >= 1:
                    y.wait_ge(p.s_ve.h, ve_resid.get((sti - 1) * 4 + j, 0))
                I16(y.dma_start(xt[j][:],
                                p.x2scr[gi * 128:(gi + 1) * 128, :]), sx)
            x_all = sx.n
            for j in range(4):
                gi = sti * 4 + j
                v.wait_ge(sx.h, x_all)
                s.wait_ge(sx.h, x_all)
                if not first:
                    v.wait_ge(p.s_pe.h, p.pe_xn_doneF)
                    s.wait_ge(p.s_pe.h, p.pe_xn_doneF)
                first = False
                if sti >= 1 and j == 0:
                    v.wait_ge(p.s_pe.h, pe_proj_last)
                ve_ap = _ln(p, xt[j][:], xn_bf[:], tmp8, xn_bf[:],
                            store_col=gi)
                ve_resid[gi] = ve_ap
                t.wait_ge(p.s_ve.h, ve_ap)
                if sti == 0 and j == 0:
                    t.wait_ge(p.s_w.h, w_done)
                    t.wait_ge(p.s_ve.h, p.s_ve.n)
                _xnT(p, xn_bf, None, pT,
                     dst_map=lambda grp, j=j: xnT_st[:].rearrange(
                         "p (a b) -> p a b", a=16)[:, grp * 4:(grp + 1) * 4,
                                                   j * 128:(j + 1) * 128])
                p.pe_xn_doneF = p.s_pe.n
            pcnt = 0
            for which, wsb, stgs, scr in [(0, bWq_sb, qstg, p.qbscr),
                                          (1, bWk_sb, kstg, p.kbscr)]:
                for oc in range(8):
                    par = pcnt % 2
                    pcnt += 1
                    pb = pP[:] if par == 0 else pZ[:, 0:512]
                    if se_stg_last.get(("pp", par), 0):
                        t.wait_ge(p.s_se.h, se_stg_last[("pp", par)])
                    for d in range(16):
                        mm = t.matmul(
                            pb,
                            wsb[:, d * 1024 + oc * 128:d * 1024 + (oc + 1) * 128],
                            xnT_st[:, d * 512:(d + 1) * 512],
                            start=(d == 0), stop=(d == 15))
                    pe_proj_last = I1(mm, p.s_pe)
                    s.wait_ge(p.s_pe.h, pe_proj_last)
                    stg = stgs[par]
                    if st_last.get((which, par), 0):
                        s.wait_ge(p.s_st.h, st_last[(which, par)])
                    se_stg_last[(which, par)] = se_stg_last[("pp", par)] = I1(
                        s.copy(stg[:], pb), p.s_se)
                    y.wait_ge(p.s_se.h, se_stg_last[(which, par)])
                    st_last[(which, par)] = I16(
                        y.dma_start(scr[s0, oc, :, ch * 512:(ch + 1) * 512],
                                    stg[:]), p.s_st)
            if se_stg_last.get(("pp", 1), 0):
                t.wait_ge(p.s_se.h, se_stg_last[("pp", 1)])
            for j in range(4):
                gi = sti * 4 + j
                jj = ch * 4 + j
                if se_E_last:
                    t.wait_ge(p.s_se.h, se_E_last)
                for jc in range(2):
                    for d in range(16):
                        mm = t.matmul(
                            pS2[:, jc * 512:(jc + 1) * 512],
                            xnT_st[:, d * 512 + j * 128:d * 512 + (j + 1) * 128],
                            bWs_sb[:, d * 1024 + jc * 512:d * 1024 + (jc + 1) * 512],
                            start=(d == 0), stop=(d == 15))
                pe_s = I1(mm, p.s_pe)
                s.wait_ge(p.s_pe.h, pe_s)
                if ve_pre_last:
                    s.wait_ge(p.s_ve.h, ve_pre_last)
                se_E_last = I1(s.activation(e32[:], pS2[:], AF.Exp,
                                            bias=p.n32_c[:], scale=1.0), p.s_se)
                v.wait_ge(p.s_se.h, se_E_last)
                if pe_zc_last:
                    v.wait_ge(p.s_pe.h, pe_zc_last)
                if st_last.get("E", 0):
                    v.wait_ge(p.s_st.h, st_last["E"])
                v.tensor_scalar(out=e32[:], in0=e32[:], scalar1=1.0,
                                scalar2=None, op0=ALU.min)
                vsep(p)
                I1(v.tensor_scalar(out=estg[:], in0=e32[:], scalar1=1.6e-28,
                                   scalar2=None, op0=ALU.max), p.s_ve)
                ve_E = p.s_ve.n
                if jj > 0:
                    v.wait_ge(p.s_pe.h, pe_pre_count)
                    v.tensor_copy(prehi[:], pPre[:])
                    vsep(p)
                    v.tensor_tensor(prelof[:], pPre[:], prehi[:], ALU.subtract)
                    vsep(p)
                    I1(v.tensor_copy(prelo[:], prelof[:]), p.s_ve)
                ve_pre_last = p.s_ve.n
                t.wait_ge(p.s_ve.h, ve_pre_last)
                if ve_rz_last:
                    t.wait_ge(p.s_ve.h, ve_rz_last)
                for jc in range(2):
                    mm = t.matmul(pZ[:, jc * 512:(jc + 1) * 512], p.utinc[:],
                                  estg[:, jc * 512:(jc + 1) * 512],
                                  start=True, stop=(jj == 0))
                    if jj > 0:
                        mm = t.matmul(pZ[:, jc * 512:(jc + 1) * 512], ones1b[:],
                                      prehi[:, jc * 512:(jc + 1) * 512],
                                      start=False, stop=False)
                        mm = t.matmul(pZ[:, jc * 512:(jc + 1) * 512], ones1b[:],
                                      prelo[:, jc * 512:(jc + 1) * 512],
                                      start=False, stop=True)
                pe_zc = I1(mm, p.s_pe)
                for jc in range(2):
                    mm = t.matmul(pPre[:, jc * 512:(jc + 1) * 512], onescol[:],
                                  estg[:, jc * 512:(jc + 1) * 512],
                                  start=(jj == 0), stop=(jj == 15))
                pe_pre_count = I1(mm, p.s_pe)
                pe_zc_last = pe_pre_count
                v.wait_ge(p.s_pe.h, pe_zc)
                if st_last.get("rz", 0):
                    v.wait_ge(p.s_st.h, st_last["rz"])
                with nc.allow_low_precision(
                        reason="rz stored bf16 by design (ratio structure)"):
                    ve_rz_last = I1(v.reciprocal(rzstg[:], pZ[:]), p.s_ve)
                y.wait_ge(p.s_ve.h, ve_rz_last)
                st_last["E"] = I16(
                    y.dma_start(p.escr[s0, jj * 128:(jj + 1) * 128, :],
                                estg[:]), p.s_st)
                st_last["rz"] = I16(
                    y.dma_start(p.rzscr[s0, jj * 128:(jj + 1) * 128, :],
                                rzstg[:]), p.s_st)
    y.wait_ge(p.s_st.h, p.s_st.n)
    g.wait_ge(p.s_w.h, p.s_w.n)
    if p.tap == "B1a":
        y.wait_ge(p.s_se.h, p.s_se.n)
        I16(y.dma_start(p.stats_tap, p.stats[:]), p.s_st)
        y.wait_ge(p.s_st.h, p.s_st.n)


def emit_B1b(p):
    nc = p.nc
    sb = p.ses.enter_context
    bWv_sb = sb(nc.sbuf_tensor("bWv_sb", [128, 16 * 2048], BF16))
    bWg_sb = sb(nc.sbuf_tensor("bWg_sb", [128, 16 * 2048], BF16))
    xt = [sb(nc.sbuf_tensor(f"xtB2{i}", [128, D], F32)) for i in range(4)]
    xn_bf = sb(nc.sbuf_tensor("xn_bfB2", [128, D], BF16))
    xnT_st = sb(nc.sbuf_tensor("xnT_stB2", [128, 16 * 512], BF16))
    vstg = sb(nc.sbuf_tensor("vstgB2", [128, D], BF16))
    gstg = sb(nc.sbuf_tensor("gstgB2", [128, D], BF16))
    pV4 = sb(nc.psum_tensor("pV4B2", [128, 2048], F32))
    pT0 = sb(nc.psum_tensor("pTB2", [128, 512], BF16))
    pT = [pT0, pT0]
    p.pT_free = {}

    g, v, s, t, y = p.g, p.v, p.s, p.t, p.y

    for c in range(16):
        I16(g.dma_start(out=bWv_sb[:, c * 2048:(c + 1) * 2048],
                        in_=p.bWv[c * 128:(c + 1) * 128, :]), p.s_w)
        I16(g.dma_start(out=bWg_sb[:, c * 2048:(c + 1) * 2048],
                        in_=p.bWg[c * 128:(c + 1) * 128, :]), p.s_w)
    w_done = p.s_w.n

    st_last = {}
    ve_ap = {}
    se_stg_last = 0
    first = True
    for sti in range(8):
        s0 = sti // 4
        parity = sti % 2
        sx = p.s_x0 if parity == 0 else p.s_x1
        for j in range(4):
            gi = sti * 4 + j
            if sti >= 1:
                y.wait_ge(p.s_ve.h, ve_ap.get((sti - 1) * 4 + j, 0))
            I16(y.dma_start(xt[j][:], p.x2scr[gi * 128:(gi + 1) * 128, :]), sx)
        x_all = sx.n
        for j in range(4):
            gi = sti * 4 + j
            v.wait_ge(sx.h, x_all)
            if not first:
                v.wait_ge(p.s_pe.h, p.pe_xn_doneF)
            first = False
            ve_ap[gi] = I1(
                v.tensor_scalar(out=xn_bf[:], in0=xt[j][:],
                                scalar1=p.stats[:, 2 * gi:2 * gi + 1],
                                scalar2=p.stats[:, 2 * gi + 1:2 * gi + 2],
                                op0=ALU.subtract, op1=ALU.mult), p.s_ve)
            t.wait_ge(p.s_ve.h, ve_ap[gi])
            if sti == 0 and j == 0:
                t.wait_ge(p.s_w.h, w_done)
            _xnT(p, xn_bf, None, pT,
                 dst_map=lambda grp, j=j: xnT_st[:].rearrange(
                     "p (a b) -> p a b", a=16)[:, grp * 4:(grp + 1) * 4,
                                               j * 128:(j + 1) * 128])
            p.pe_xn_doneF = p.s_pe.n
        for j in range(4):
            gi = sti * 4 + j
            jj = gi - s0 * 16
            for which, wsb, stg, scr, fn in [
                    (0, bWv_sb, vstg, p.vbscr, AF.Copy),
                    (1, bWg_sb, gstg, p.gbscr, AF.Silu)]:
                if se_stg_last:
                    t.wait_ge(p.s_se.h, se_stg_last)
                for jc in range(4):
                    for d in range(16):
                        mm = t.matmul(
                            pV4[:, jc * 512:(jc + 1) * 512],
                            xnT_st[:, d * 512 + j * 128:d * 512 + (j + 1) * 128],
                            wsb[:, d * 2048 + jc * 512:d * 2048 + (jc + 1) * 512],
                            start=(d == 0), stop=(d == 15))
                pe_p = I1(mm, p.s_pe)
                s.wait_ge(p.s_pe.h, pe_p)
                if st_last.get(which, 0):
                    s.wait_ge(p.s_st.h, st_last[which])
                se_stg_last = I1(s.activation(stg[:], pV4[:], fn), p.s_se)
                y.wait_ge(p.s_se.h, se_stg_last)
                st_last[which] = I16(
                    y.dma_start(scr[s0, jj * 128:(jj + 1) * 128, :], stg[:]),
                    p.s_st)
    y.wait_ge(p.s_st.h, p.s_st.n)
    g.wait_ge(p.s_w.h, p.s_w.n)


def emit_B2(p):
    nc = p.nc
    sb = p.ses.enter_context
    bWo_sb = sb(nc.sbuf_tensor("bWo_sb", [128, 16 * 2048], BF16))
    utinc8 = sb(nc.sbuf_tensor("utinc8", [128, 128], BF16))
    o2fm = sb(nc.sbuf_tensor("o2fmB2", [128, 16 * 8 * 128], BF16))
    AT_sb = sb(nc.sbuf_tensor("AT_sbB2", [128, 16 * 512], BF16))
    WT_sb = sb(nc.sbuf_tensor("WT_sbB2", [128, 16 * 512], BF16))
    E_fm = [sb(nc.sbuf_tensor(f"E_fmB2{i}", [64, 16 * 128], BF16))
            for i in range(2)]
    P2_tm = sb(nc.sbuf_tensor("P2_tmB2", [128, 8 * 64], BF16))
    P2_fm = sb(nc.sbuf_tensor("P2_fmB2", [64, 8 * 128], BF16))
    q_h = [sb(nc.sbuf_tensor(f"q_hB2{i}", [64, 1024], BF16)) for i in range(2)]
    k_h = [sb(nc.sbuf_tensor(f"k_hB2{i}", [64, 2048], BF16)) for i in range(2)]
    E_h = [sb(nc.sbuf_tensor(f"E_hB2{i}", [128, 16 * 64], BF16)) for i in range(2)]
    rz_h = [sb(nc.sbuf_tensor(f"rz_hB2{i}", [128, 8 * 64], BF16)) for i in range(2)]
    v_h = [sb(nc.sbuf_tensor(f"v_hB2{i}", [128, 16 * 128], BF16)) for i in range(2)]
    gs_h = [sb(nc.sbuf_tensor(f"gs_hB2{i}", [128, 8 * 128], BF16)) for i in range(2)]
    okf8 = sb(nc.sbuf_tensor("okf8B2", [128, 512], F32))
    pvt8 = sb(nc.sbuf_tensor("pvt8B2", [128, 512], BF16))
    o_bf4 = sb(nc.sbuf_tensor("o_bf4B2", [128, 512], BF16))
    o2_4 = sb(nc.sbuf_tensor("o2_4B2", [128, 512], BF16))
    jnk = sb(nc.sbuf_tensor("jnkB2", [128, 128], BF16))
    ep32 = sb(nc.sbuf_tensor("ep32B2", [128, 40], F32))
    xt = [sb(nc.sbuf_tensor(f"xtB3{i}", [128, D], F32)) for i in range(2)]
    x3stg = sb(nc.sbuf_tensor("x3stgB2", [128, D], F32))
    pAT = [sb(nc.psum_tensor(f"pATB2{i}", [128, 512], F32)) for i in range(2)]
    pNO = sb(nc.psum_tensor("pNOB2", [128, 512], F32))
    pWT = [sb(nc.psum_tensor(f"pWTB2{i}", [128, 512], F32)) for i in range(2)]
    pTb = sb(nc.psum_tensor("pTbB2", [128, 512], BF16))
    pWo4 = sb(nc.psum_tensor("pWo4B2", [128, 1024], F32))

    g, v, s, t, y = p.g, p.v, p.s, p.t, p.y

    for c in range(16):
        I16(g.dma_start(out=bWo_sb[:, c * 2048:(c + 1) * 2048],
                        in_=p.bWo[c * 128:(c + 1) * 128, :]), p.s_w)
    w_done = p.s_w.n
    v.wait_ge(p.s_w.h, w_done)
    for c in range(16):
        mm = v.tensor_scalar(out=bWo_sb[:, c * 2048:(c + 1) * 2048],
                             in0=bWo_sb[:, c * 2048:(c + 1) * 2048],
                             scalar1=p.gnwcol[:], scalar2=None, op0=ALU.mult)
    I1(mm, p.s_ve)
    wo_scaled = p.s_ve.n
    I1(v.tensor_scalar(out=utinc8[:], in0=p.utinc[:], scalar1=float(BQ_SCALE),
                       scalar2=None, op0=ALU.mult), p.s_ve)
    ut8_done = p.s_ve.n

    p.pATfree = {}
    p.pWTfree = {}
    p.at_ev = {}
    p.num_rd = {}
    p.num_wr = {}
    p.pTb_free = 0
    p.pe_num_last = 0
    p.pe_ov_last = 0
    p.pe_o2T = 0
    p.ve_NO_read_last = 0
    p.se_NO_read_last = 0
    p.ve_o2_last = 0
    p.wo_resid_last = 0
    hload = {}
    hdone = {}
    ve_resid = {}
    st_x3 = {}
    se_AT_last = 0
    ve_ATd_last = 0
    pe_num_last = 0
    se_WT_last = 0
    pe_ov_last = 0
    pe_P2T_last = 0
    ve_NO_read_last = 0
    se_NO_read_last = 0
    pTb_free = 0
    pe_o2T = 0
    first_wo = True

    for s0 in range(SEQ):
        for half in range(2):
            hi = (half + 1) * 1024
            nt = hi // 128
            hseq = s0 * 2 + half
            def ph1(h):
                hb = h % 2
                stg = p.s_tg0 if hb == 0 else p.s_tg1
                hkey = hseq * BH + h
                if hkey >= 2:
                    y.wait_ge(p.s_pe.h, hdone[hkey - 2])
                oc, ro = h // 2, 64 * (h % 2)
                I16(y.dma_start(q_h[hb][:, 0:1024],
                                p.qbscr[s0, oc, ro:ro + 64,
                                        half * 1024:hi]), stg)
                I16(y.dma_start(k_h[hb][:, 0:hi],
                                p.kbscr[s0, oc, ro:ro + 64, 0:hi]), stg)
                I16(y.dma_start(
                    E_h[hb][:, 0:nt * 64],
                    p.escr[s0, 0:hi, h * 64:(h + 1) * 64].rearrange(
                        "(n q) m -> q n m", q=128)), stg)
                I16(y.dma_start(
                    rz_h[hb][:, 0:512],
                    p.rzscr[s0, half * 1024:hi,
                            h * 64:(h + 1) * 64].rearrange(
                        "(n q) m -> q n m", q=128)), stg)
                I16(y.dma_start(
                    v_h[hb][:, 0:nt * 128],
                    p.vbscr[s0, 0:hi, h * 128:(h + 1) * 128].rearrange(
                        "(n q) m -> q n m", q=128)), stg)
                I16(y.dma_start(
                    gs_h[hb][:, 0:1024],
                    p.gbscr[s0, half * 1024:hi,
                            h * 128:(h + 1) * 128].rearrange(
                        "(n q) m -> q n m", q=128)), stg)
                hload[hkey] = stg.n

                t.wait_ge(stg.h, hload[hkey])
                if h == 0 and hseq == 0:
                    t.wait_ge(p.s_ve.h, ut8_done)
                for grp in range((nt + 3) // 4):
                    if p.pTb_free:
                        t.wait_ge(p.s_v2.h, p.pTb_free)
                    for c in range(min(4, nt - grp * 4)):
                        tt = grp * 4 + c
                        mm = t.transpose(pTb[0:64, c * 128:(c + 1) * 128],
                                         E_h[hb][:, tt * 64:(tt + 1) * 64],
                                         p.ident[:])
                    I1(mm, p.s_pe)
                    v.wait_ge(p.s_pe.h, p.s_pe.n)
                    nw = min(4, nt - grp * 4)
                    p.pTb_free = I1(
                        v.tensor_copy(
                            E_fm[hb][0:64, grp * 512:grp * 512 + nw * 128],
                            pTb[0:64, 0:nw * 128]), p.s_v2)
                    vsep(p)

                for tc in range(2):
                    nbase = hb * 512 + tc * 256
                    tcg = half * 2 + tc
                    ntau = (tcg + 1) * 4
                    for tt in range(ntau):
                        pb = pAT[tt % 2]
                        fr = p.pATfree.get(tt % 2, (0, 0))
                        if fr[0]:
                            t.wait_ge(p.s_se.h, fr[0])
                        if fr[1]:
                            t.wait_ge(p.s_ve.h, fr[1])
                        if p.pe_num_last and tt == 0:
                            t.wait_ge(p.s_pe.h, p.pe_num_last)
                        mm = t.matmul(pb[:], k_h[hb][:, tt * 128:(tt + 1) * 128],
                                      q_h[hb][:, tc * 512:(tc + 1) * 512],
                                      start=True, stop=True)
                        I1(mm, p.s_pe)
                        cs = tt - tcg * 4
                        if tt % 2 == 0:
                            s.wait_ge(p.s_pe.h, p.s_pe.n)
                            se_m = I1(
                                s.activation(AT_sb[:, tt * 512:(tt + 1) * 512],
                                             pb[:], AF.Copy, bias=0.0,
                                             scale=float(BQ_SCALE)), p.s_se)
                            marks = (se_m, 0)
                            dve_t, dve_n = p.s_se.h, se_m
                        else:
                            v.wait_ge(p.s_pe.h, p.s_pe.n)
                            ve_m = I1(v.tensor_scalar(
                                out=AT_sb[:, tt * 512:(tt + 1) * 512],
                                in0=pb[:], scalar1=float(BQ_SCALE),
                                scalar2=None, op0=ALU.mult), p.s_ve)
                            marks = (0, ve_m)
                            dve_t, dve_n = p.s_ve.h, ve_m
                        if 0 <= cs < 4:
                            v.wait_ge(dve_t, dve_n)
                            ve_m = I1(v.tensor_tensor(
                                AT_sb[:, tt * 512 + cs * 128:
                                      tt * 512 + (cs + 1) * 128],
                                pb[:, cs * 128:(cs + 1) * 128],
                                utinc8[:], ALU.mult), p.s_ve)
                            marks = (marks[0], ve_m)
                            vsep(p)
                        p.pATfree[tt % 2] = marks
                        p.at_ev[tt] = marks
                    if p.num_rd.get((hb, tc), 0):
                        t.wait_ge(p.s_ve.h, p.num_rd[(hb, tc)])
                    if h == 0 and tc == 0 and p.wo_resid_last:
                        t.wait_ge(p.s_ve.h, p.wo_resid_last)
                    for tt in range(ntau):
                        mk = p.at_ev[tt]
                        if mk[0]:
                            t.wait_ge(p.s_se.h, mk[0])
                        if mk[1]:
                            t.wait_ge(p.s_ve.h, mk[1])
                        for c in range(4):
                            jt = tcg * 4 + c
                            if tt <= jt:
                                mm = t.matmul(
                                    pWo4[:, nbase + c * 64:nbase + (c + 1) * 64],
                                    AT_sb[:, tt * 512 + c * 128:
                                          tt * 512 + (c + 1) * 128],
                                    E_h[hb][:, tt * 64:(tt + 1) * 64],
                                    start=(tt == 0 and c == 0),
                                    stop=(tt == jt))
                    p.pe_num_last = I1(mm, p.s_pe)
                    p.num_wr[(hb, tc)] = p.pe_num_last

            def ph2(h):
                hb = h % 2
                v.wait_ge(p.s_pe.h, p.num_wr[(hb, 1)])
                nrd = I1(v.tensor_tensor(okf8[:],
                                         pWo4[:, hb * 512:(hb + 1) * 512],
                                         rz_h[hb][:, 0:512], ALU.mult),
                         p.s_ve)
                p.num_rd[(hb, 0)] = p.num_rd[(hb, 1)] = nrd
                for c8 in range(8):
                    mm = v.reduce_max(ep32[:, c8:c8 + 1],
                                      okf8[:, c8 * 64:(c8 + 1) * 64],
                                      axis=AX, negate=True)
                I1(mm, p.s_ve)
                s.wait_ge(p.s_ve.h, p.s_ve.n)
                for c8 in range(8):
                    mm = s.activation(pvt8[:, c8 * 64:(c8 + 1) * 64],
                                      okf8[:, c8 * 64:(c8 + 1) * 64], AF.Exp,
                                      bias=ep32[:, c8:c8 + 1], scale=1.0,
                                      accum_out=ep32[:, 8 + c8:9 + c8])
                I1(mm, p.s_se)
                v.wait_ge(p.s_se.h, p.s_se.n)
                for c8 in range(8):
                    v.reciprocal(ep32[:, 16 + c8:17 + c8],
                                 ep32[:, 8 + c8:9 + c8])
                for c8 in range(8):
                    v.tensor_scalar(out=pvt8[:, c8 * 64:(c8 + 1) * 64],
                                    in0=pvt8[:, c8 * 64:(c8 + 1) * 64],
                                    scalar1=ep32[:, 16 + c8:17 + c8],
                                    scalar2=None, op0=ALU.mult)
                vsep(p)
                I1(v.tensor_tensor(P2_tm[:, 0:512], pvt8[:],
                                   rz_h[hb][:, 0:512], ALU.mult), p.s_ve)
                ve_P2 = p.s_ve.n
                t.wait_ge(p.s_ve.h, ve_P2)
                for grp in range(2):
                    if p.pTb_free:
                        t.wait_ge(p.s_v2.h, p.pTb_free)
                    for c in range(4):
                        lt = grp * 4 + c
                        mm = t.transpose(pTb[0:64, c * 128:(c + 1) * 128],
                                         P2_tm[:, lt * 64:(lt + 1) * 64],
                                         p.ident[:])
                    I1(mm, p.s_pe)
                    v.wait_ge(p.s_pe.h, p.s_pe.n)
                    p.pTb_free = I1(
                        v.tensor_copy(P2_fm[0:64, grp * 512:(grp + 1) * 512],
                                      pTb[0:64, 0:512]), p.s_v2)
                    vsep(p)
                for tc in range(2):
                    tcg = half * 2 + tc
                    ntau = (tcg + 1) * 4
                    t.wait_ge(p.s_v2.h, p.pTb_free)
                    for tt in range(ntau):
                        pb = pWT[tt % 2]
                        fr = p.pWTfree.get(tt % 2, (0, 0))
                        if fr[0]:
                            t.wait_ge(p.s_se.h, fr[0])
                        if fr[1]:
                            t.wait_ge(p.s_ve.h, fr[1])
                        if p.pe_ov_last and tt == 0:
                            t.wait_ge(p.s_pe.h, p.pe_ov_last)
                        mm = t.matmul(pb[:], E_fm[hb][0:64, tt * 128:(tt + 1) * 128],
                                      P2_fm[0:64, tc * 512:(tc + 1) * 512],
                                      start=True, stop=True)
                        I1(mm, p.s_pe)
                        cs = tt - tcg * 4
                        if tt % 2 == 0:
                            s.wait_ge(p.s_pe.h, p.s_pe.n)
                            se_m = I1(
                                s.copy(WT_sb[:, tt * 512:(tt + 1) * 512],
                                       pb[:]), p.s_se)
                            marks = (se_m, 0)
                            dve_t, dve_n = p.s_se.h, se_m
                        else:
                            v.wait_ge(p.s_pe.h, p.s_pe.n)
                            ve_m = I1(v.tensor_copy(
                                WT_sb[:, tt * 512:(tt + 1) * 512], pb[:]),
                                p.s_ve)
                            marks = (0, ve_m)
                            dve_t, dve_n = p.s_ve.h, ve_m
                        if 0 <= cs < 4:
                            v.wait_ge(dve_t, dve_n)
                            ve_m = I1(v.tensor_tensor(
                                WT_sb[:, tt * 512 + cs * 128:
                                      tt * 512 + (cs + 1) * 128],
                                pb[:, cs * 128:(cs + 1) * 128],
                                p.utinc[:], ALU.mult), p.s_ve)
                            marks = (marks[0], ve_m)
                            vsep(p)
                        p.pWTfree[tt % 2] = marks
                        p.at_ev[tt] = marks
                    if p.ve_NO_read_last:
                        t.wait_ge(p.s_ve.h, p.ve_NO_read_last)
                    if p.se_NO_read_last:
                        t.wait_ge(p.s_se.h, p.se_NO_read_last)
                    for tt in range(ntau):
                        mk = p.at_ev[tt]
                        if mk[0]:
                            t.wait_ge(p.s_se.h, mk[0])
                        if mk[1]:
                            t.wait_ge(p.s_ve.h, mk[1])
                        for c in range(4):
                            jt = tcg * 4 + c
                            if tt <= jt:
                                mm = t.matmul(
                                    pNO[:, c * 128:(c + 1) * 128],
                                    WT_sb[:, tt * 512 + c * 128:
                                          tt * 512 + (c + 1) * 128],
                                    v_h[hb][:, tt * 128:(tt + 1) * 128],
                                    start=(tt == 0 and c == 0),
                                    stop=(tt == jt))
                    p.pe_ov_last = I1(mm, p.s_pe)
                    s.wait_ge(p.s_pe.h, p.pe_ov_last)
                    if p.ve_o2_last:
                        s.wait_ge(p.s_ve.h, p.ve_o2_last)
                    for c in range(4):
                        s.copy(o_bf4[:, c * 128:(c + 1) * 128],
                               pNO[:, c * 128:(c + 1) * 128])
                    for c in range(4):
                        mm = s.activation(jnk[:], pNO[:, c * 128:(c + 1) * 128],
                                          AF.Square,
                                          accum_out=ep32[:, 24 + c:25 + c])
                    p.se_NO_read_last = I1(mm, p.s_se)
                    for c in range(4):
                        s.activation(ep32[:, 28 + c:29 + c],
                                     ep32[:, 24 + c:25 + c], AF.Ln,
                                     bias=p.eps_c[:], scale=1.0 / BDV)
                    for c in range(4):
                        mm = s.activation(ep32[:, 32 + c:33 + c],
                                          ep32[:, 28 + c:29 + c], AF.Exp,
                                          bias=0.0, scale=-0.5)
                    I1(mm, p.s_se)
                    v.wait_ge(p.s_se.h, p.s_se.n)
                    for c in range(4):
                        v.tensor_scalar(out=o_bf4[:, c * 128:(c + 1) * 128],
                                        in0=o_bf4[:, c * 128:(c + 1) * 128],
                                        scalar1=ep32[:, 32 + c:33 + c],
                                        scalar2=None, op0=ALU.mult)
                    for c in range(4):
                        lt = tc * 4 + c
                        mm = v.tensor_tensor(
                            o2_4[:, c * 128:(c + 1) * 128],
                            o_bf4[:, c * 128:(c + 1) * 128],
                            gs_h[hb][:, lt * 128:(lt + 1) * 128], ALU.mult)
                    ve_o2 = I1(mm, p.s_ve)
                    p.ve_NO_read_last = ve_o2
                    p.ve_o2_last = ve_o2
                    t.wait_ge(p.s_ve.h, ve_o2)
                    if p.pTb_free:
                        t.wait_ge(p.s_v2.h, p.pTb_free)
                    for c in range(4):
                        mm = t.transpose(pTb[:, c * 128:(c + 1) * 128],
                                         o2_4[:, c * 128:(c + 1) * 128],
                                         p.ident[:])
                    p.pe_o2T = I1(mm, p.s_pe)
                    v.wait_ge(p.s_pe.h, p.pe_o2T)
                    p.pTb_free = I1(
                        v.tensor_copy(o2fm[:, (h * 8 + tc * 4) * 128:
                                           (h * 8 + tc * 4 + 4) * 128],
                                      pTb[:, 0:512]), p.s_v2)
                    vsep(p)
                hdone[hseq * BH + h] = p.s_pe.n

            ph1(0)
            for h in range(1, BH):
                ph1(h)
                ph2(h - 1)
            ph2(BH - 1)
            if first_wo:
                t.wait_ge(p.s_ve.h, wo_scaled)
                first_wo = False
            t.wait_ge(p.s_v2.h, p.s_v2.n)
            for par in (0, 1):
                for tc in (0, 1):
                    if p.num_rd.get((par, tc), 0):
                        t.wait_ge(p.s_ve.h, p.num_rd[(par, tc)])
            for lt in range(8):
                gt = hseq * 8 + lt
                xb = gt % 2
                sx = p.s_x0 if xb == 0 else p.s_x1
                if gt >= 2:
                    y.wait_ge(p.s_ve.h, ve_resid.get(gt - 2, 0))
                rowg = s0 * T + half * 1024 + lt * 128
                x_here = I16(y.dma_start(xt[xb][:],
                                         p.x2scr[rowg:rowg + 128, :]), sx)
                for dh in range(2):
                    if ve_resid.get((gt, dh - 1), 0):
                        t.wait_ge(p.s_ve.h, ve_resid[(gt, dh - 1)])
                    elif gt >= 1:
                        t.wait_ge(p.s_ve.h, ve_resid.get((gt - 1, 1), 0))
                    for dc in range(2):
                        for j16 in range(16):
                            mm = t.matmul(
                                pWo4[:, dc * 512:(dc + 1) * 512],
                                o2fm[:, (j16 * 8 + lt) * 128:
                                     (j16 * 8 + lt + 1) * 128],
                                bWo_sb[:, j16 * 2048 + (dh * 2 + dc) * 512:
                                       j16 * 2048 + (dh * 2 + dc + 1) * 512],
                                start=(j16 == 0), stop=(j16 == 15))
                    pe_wo = I1(mm, p.s_pe)
                    v.wait_ge(sx.h, x_here)
                    v.wait_ge(p.s_pe.h, pe_wo)
                    if dh == 0 and st_x3.get(gt - 1, 0):
                        v.wait_ge(p.s_st.h, st_x3[gt - 1])
                    ve_resid[(gt, dh)] = I1(
                        v.tensor_tensor(x3stg[:, dh * 1024:(dh + 1) * 1024],
                                        pWo4[:],
                                        xt[xb][:, dh * 1024:(dh + 1) * 1024],
                                        ALU.add), p.s_ve)
                ve_resid[gt] = ve_resid[(gt, 1)]
                y.wait_ge(p.s_ve.h, ve_resid[gt])
                p.wo_resid_last = ve_resid[gt]
                st_x3[gt] = I16(y.dma_start(p.x3scr[rowg:rowg + 128, :],
                                            x3stg[:]), p.s_st)
    y.wait_ge(p.s_st.h, p.s_st.n)
    g.wait_ge(p.s_w.h, p.s_w.n)


def emit_B3(p):
    emit_ffn(p, p.x3scr, p.out, tag="b")



def _host_constants():
    hd = HD
    inv = 1.0 / (10000.0 ** (np.arange(0, hd, 2, dtype=np.float64) / hd))
    pos = np.arange(T, dtype=np.float64)
    fr = pos[:, None] * inv[None, :]
    cos = np.cos(fr).astype(np.float32)
    sin = np.sin(fr).astype(np.float32)
    coscat = np.ascontiguousarray(np.concatenate([cos, cos], axis=1))
    sincat = np.ascontiguousarray(np.concatenate([-sin, sin], axis=1))
    ident = np.eye(128, dtype=np.float32)
    utinc = np.triu(np.ones((128, 128), np.float32))
    return coscat, sincat, ident, utinc


def make_in_maps(inp):
    coscat, sincat, ident, utinc = _host_constants()
    W = {}
    W["aWq"] = np.ascontiguousarray(np.asarray(inp["L0_attn_a_Wq"]).T)
    W["aWk"] = np.ascontiguousarray(np.asarray(inp["L0_attn_a_Wk"]).T)
    W["aWv"] = np.ascontiguousarray(np.asarray(inp["L0_attn_a_Wv"]).T)
    W["aWo"] = np.ascontiguousarray(np.asarray(inp["L0_attn_a_Wo"]).T)
    W["coscat"] = coscat
    W["sincat"] = sincat
    W["identi"] = ident
    W["utinci"] = utinc
    W["fW1"] = np.ascontiguousarray(np.asarray(inp["L0_ffn_W1"]))
    W["fb1c"] = np.ascontiguousarray(
        np.asarray(inp["L0_ffn_b1"]).reshape(16, 128).T)
    W["fW2"] = np.ascontiguousarray(np.asarray(inp["L0_ffn_W2"]))
    W["fb2"] = np.ascontiguousarray(np.asarray(inp["L0_ffn_b2"]).reshape(1, 8))
    W["fV"] = np.ascontiguousarray(
        np.asarray(inp["L0_ffn_V"]).transpose(2, 0, 1).reshape(D, 512))
    W["fU"] = np.ascontiguousarray(
        np.asarray(inp["L0_ffn_U"]).transpose(0, 2, 1).reshape(512, D))
    for k, nm in [("bWq", "L0_attn_b_Wq"), ("bWk", "L0_attn_b_Wk"),
                  ("bWs", "L0_attn_b_Ws"), ("bWv", "L0_attn_b_Wv"),
                  ("bWg", "L0_attn_b_Wg"), ("bWo", "L0_attn_b_Wo")]:
        W[k] = np.ascontiguousarray(np.asarray(inp[nm]))
    W["gnw"] = np.ascontiguousarray(
        np.asarray(inp["L0_attn_b_g_norm_weight"]).reshape(1, BDV))
    x = np.asarray(inp["L0_attn_a_hidden_states"], dtype=np.float32)
    in_maps = []
    for c in range(N_CORES):
        m = dict(W)
        m["x0"] = np.ascontiguousarray(x[2 * c:2 * c + 2].reshape(SEQ * T, D))
        in_maps.append(m)
    return in_maps


def kernel(**inp):
    in_maps = make_in_maps(inp)
    nc = build_program()
    res = run_bass_kernel_spmd(nc, in_maps, core_ids=list(range(N_CORES)))
    out = np.concatenate(
        [np.asarray(r["out"], np.float32).reshape(SEQ, T, D)
         for r in res.results], axis=0)
    return out


# version 18
# speedup vs baseline: 1.0623x; 1.0623x over previous
import numpy as np
from contextlib import ExitStack

import concourse.bass as bass
import concourse.mybir as mybir
from concourse.bass_utils import run_bass_kernel_spmd
from concourse.alu_op_type import AluOpType as ALU

F32 = mybir.dt.float32
BF16 = mybir.dt.bfloat16
AF = mybir.ActivationFunctionType
AX = mybir.AxisListType.X

N_CORES = 8
B = 16
T = 2048
D = 2048
SEQ = 2
NT = 16
NTILES = SEQ * NT
NH, HD, NKV = 16, 128, 4
BH, BDK, BDV, BM = 16, 64, 128, 64
EPS = 1e-5
QK_SCALE = 1.0 / np.sqrt(128.0)
BQ_SCALE = 1.0 / np.sqrt(64.0)


class Sem:
    def __init__(self, h):
        self.h = h
        self.n = 0


def I1(instr, sem):
    instr.then_inc(sem.h, 1)
    sem.n += 1
    return sem.n


def I16(instr, sem):
    instr.then_inc(sem.h, 16)
    sem.n += 16
    return sem.n


def vsep(p):
    p.v.nop(cycle_cnt=400)


class P:
    pass


def _bar(p):
    p.barn += 1
    tgt = 5 * p.barn
    for e in [p.t, p.v, p.s, p.g, p.y]:
        e.drain().then_inc(p.bar.h, 1)
        e.wait_ge(p.bar.h, tgt)


def _ln(p, xt_ap, sqj_ap, tmp8, xn_out, store_col=None):
    v, s = p.v, p.s
    I1(v.reduce_sum(tmp8[:, 0:1], xt_ap, axis=AX), p.s_ve)
    I1(s.activation(sqj_ap, xt_ap, AF.Square, accum_out=tmp8[:, 1:2]), p.s_se)
    v.wait_ge(p.s_se.h, p.s_se.n)
    vsep(p)
    v.tensor_scalar(out=tmp8[:, 2:3], in0=tmp8[:, 0:1], scalar1=1.0 / D,
                    scalar2=None, op0=ALU.mult)
    vsep(p)
    v.tensor_tensor(tmp8[:, 3:4], tmp8[:, 2:3], tmp8[:, 2:3], ALU.mult)
    vsep(p)
    I1(v.tensor_scalar(out=tmp8[:, 4:5], in0=tmp8[:, 1:2], scalar1=1.0 / D,
                       scalar2=tmp8[:, 3:4], op0=ALU.mult, op1=ALU.subtract),
       p.s_ve)
    s.wait_ge(p.s_ve.h, p.s_ve.n)
    s.activation(tmp8[:, 5:6], tmp8[:, 4:5], AF.Ln, bias=p.eps_c[:], scale=1.0)
    I1(s.activation(tmp8[:, 6:7], tmp8[:, 5:6], AF.Exp, bias=0.0, scale=-0.5),
       p.s_se)
    v.wait_ge(p.s_se.h, p.s_se.n)
    v.tensor_scalar(out=tmp8[:, 7:8], in0=tmp8[:, 4:5], scalar1=p.eps_c[:],
                    scalar2=None, op0=ALU.add)
    v.tensor_tensor(tmp8[:, 0:1], tmp8[:, 6:7], tmp8[:, 6:7], ALU.mult)
    vsep(p)
    v.tensor_tensor(tmp8[:, 3:4], tmp8[:, 0:1], tmp8[:, 7:8], ALU.mult)
    vsep(p)
    v.tensor_scalar(out=tmp8[:, 0:1], in0=tmp8[:, 3:4], scalar1=-0.5,
                    scalar2=1.5, op0=ALU.mult, op1=ALU.add)
    vsep(p)
    I1(v.tensor_tensor(tmp8[:, 5:6], tmp8[:, 6:7], tmp8[:, 0:1], ALU.mult),
       p.s_ve)
    if store_col is not None:
        s.wait_ge(p.s_ve.h, p.s_ve.n)
        s.copy(p.stats[:, 2 * store_col:2 * store_col + 1], tmp8[:, 2:3])
        I1(s.copy(p.stats[:, 2 * store_col + 1:2 * store_col + 2], tmp8[:, 5:6]),
           p.s_se)
        v.wait_ge(p.s_se.h, p.s_se.n)
    vsep(p)
    I1(v.tensor_scalar(out=xn_out, in0=xt_ap, scalar1=tmp8[:, 2:3],
                       scalar2=tmp8[:, 5:6], op0=ALU.subtract, op1=ALU.mult),
       p.s_ve)
    return p.s_ve.n


def _xnT(p, xn_bf, xnT_sb, pT, dst_map=None):
    t, v = p.t, p.v
    for grp in range(4):
        pb = pT[grp % 2]
        key = id(pb)
        last = p.pT_free.get(key, 0)
        if last:
            t.wait_ge(p.s_v2.h, last)
        for c in range(4):
            cc = grp * 4 + c
            mm = t.transpose(pb[:, c * 128:(c + 1) * 128],
                             xn_bf[:, cc * 128:(cc + 1) * 128], p.ident[:])
        I1(mm, p.s_pe)
        v.wait_ge(p.s_pe.h, p.s_pe.n)
        dst = dst_map(grp) if dst_map else xnT_sb[:, grp * 512:(grp + 1) * 512]
        p.pT_free[key] = I1(v.tensor_copy(dst, pb[:]), p.s_v2)
    t.wait_ge(p.s_v2.h, p.s_v2.n)


def build_program(tap=None):
    nc = bass.Bass("TRN2", target_bir_lowering=False)
    p = P()
    p.nc = nc
    p.tap = tap
    p.pT_free = {}

    def din(name, shape, dt=F32):
        return nc.dram_tensor(name, shape, dt, kind="ExternalInput").ap()

    def dscr(name, shape, dt, tapped=False):
        kind = "ExternalOutput" if (tapped and tap is not None) else "Internal"
        return nc.dram_tensor(name, shape, dt, kind=kind).ap()

    p.x0 = din("x0", [SEQ * T, D])
    p.aWq = din("aWq", [D, NH * HD])
    p.aWk = din("aWk", [D, NKV * HD])
    p.aWv = din("aWv", [D, NKV * HD])
    p.aWo = din("aWo", [NH * HD, D])
    p.coscat = din("coscat", [T, HD])
    p.sincat = din("sincat", [T, HD])
    p.fW1 = din("fW1", [D, D])
    p.fb1c = din("fb1c", [128, 16])
    p.fW2 = din("fW2", [D, 8])
    p.fb2 = din("fb2", [1, 8])
    p.fV = din("fV", [D, 512])
    p.fU = din("fU", [512, D])
    p.bWq = din("bWq", [D, BH * BDK])
    p.bWk = din("bWk", [D, BH * BDK])
    p.bWs = din("bWs", [D, BH * BM])
    p.bWv = din("bWv", [D, BH * BDV])
    p.bWg = din("bWg", [D, BH * BDV])
    p.bWo = din("bWo", [BH * BDV, D])
    p.gnw = din("gnw", [1, BDV])
    p.identi = din("identi", [128, 128])
    p.utinci = din("utinci", [128, 128])
    p.out = nc.dram_tensor("out", [SEQ * T, D], F32, kind="ExternalOutput").ap()

    tl = tap or "zz"
    p.qscr = dscr("qscr", [SEQ, T, NH * HD], BF16, tl == "A1")
    p.x1scr = dscr("x1scr", [SEQ * T, D], F32, tl == "A2")
    p.x2scr = dscr("x2scr", [SEQ * T, D], F32, tl == "A3")
    p.qbscr = dscr("qbscr", [SEQ, 8, 128, T], BF16, tl == "B1a")
    p.kbscr = dscr("kbscr", [SEQ, 8, 128, T], BF16, tl == "B1a")
    p.escr = dscr("escr", [SEQ, T, BH * BM], BF16, tl == "B1a")
    p.rzscr = dscr("rzscr", [SEQ, T, BH * BM], BF16, tl == "B1a")
    p.vbscr = dscr("vbscr", [SEQ, T, BH * BDV], BF16, tl == "B1b")
    p.gbscr = dscr("gbscr", [SEQ, T, BH * BDV], BF16, tl == "B1b")
    p.x3scr = dscr("x3scr", [SEQ * T, D], F32, tl == "B2")
    if tap in ("A1", "B1a"):
        p.stats_tap = nc.dram_tensor("stats_tap", [128, 64], F32,
                                     kind="ExternalOutput").ap()
    if tap == "A1":
        p.kfm_tap = nc.dram_tensor("kfm_tap", [128, SEQ * NKV * T], BF16,
                                   kind="ExternalOutput").ap()
        p.vtm_tap = nc.dram_tensor("vtm_tap", [128, SEQ * NT * 512], BF16,
                                   kind="ExternalOutput").ap()

    es = ExitStack()
    with es:
        def sbuf(name, shape, dt=F32):
            return es.enter_context(nc.sbuf_tensor(name, shape, dt))

        def sem(name):
            return Sem(es.enter_context(nc.semaphore(name)))

        p.ident = sbuf("ident", [128, 128], BF16)
        p.utinc = sbuf("utinc", [128, 128], BF16)
        p.eps_c = sbuf("eps_c", [128, 1])
        p.n32_c = sbuf("n32_c", [128, 1])
        p.stats = sbuf("stats", [128, 64])
        p.b2b = sbuf("b2b", [128, 8])
        p.b2row = sbuf("b2row", [1, 8])
        p.ones1 = sbuf("ones1", [1, 128])
        p.gnwcol = sbuf("gnwcol", [128, 1])
        p.gnwrow = sbuf("gnwrow", [1, BDV])

        p.bar = sem("bar")
        p.barn = 0
        for nm in ["s_w", "s_x", "s_st", "s_pe", "s_ve", "s_se", "s_ge",
                   "s_p2", "s_v2", "s_v3", "s_s2", "s_x0", "s_x1",
                   "s_tg0", "s_tg1", "s_q0", "s_q1"]:
            setattr(p, nm, sem(nm))

        with nc.Block() as block:
            p.t, p.v, p.s, p.g, p.y = nc.tensor, nc.vector, nc.scalar, \
                nc.gpsimd, nc.sync

            emit_init(p)
            _bar(p)
            stages = [("A1", emit_A1), ("A2", emit_A2), ("A3", emit_A3),
                      ("B1a", emit_B1a), ("B1b", emit_B1b), ("B2", emit_B2),
                      ("B3", emit_B3)]
            for name, f in stages:
                with ExitStack() as ses:
                    p.ses = ses
                    f(p)
                _bar(p)
                if tap == name:
                    break
        if getattr(p, "esKV", None) is not None:
            p.esKV.close()
            p.esKV = None
    return nc


def emit_init(p):
    g, v, s, t, y = p.g, p.v, p.s, p.t, p.y
    I16(g.dma_start(out=p.ident[:], in_=p.identi), p.s_w)
    I16(g.dma_start(out=p.utinc[:], in_=p.utinci), p.s_w)
    g.memset(p.eps_c[:], EPS)
    I1(g.memset(p.n32_c[:], -32.0), p.s_ge)
    I16(y.dma_start(p.b2row[:], p.fb2), p.s_x)
    I16(y.dma_start(p.gnwrow[:], p.gnw), p.s_x)
    I1(v.memset(p.ones1[:], 1.0), p.s_ve)
    with p.nc.psum_tensor("pini", [128, 8], F32) as pini, \
         p.nc.psum_tensor("pini2", [128, 1], F32) as pini2:
        t.wait_ge(p.s_x.h, p.s_x.n)
        t.wait_ge(p.s_ve.h, p.s_ve.n)
        I1(t.matmul(pini[:], p.ones1[:], p.b2row[:], start=True, stop=True),
           p.s_pe)
        I1(t.matmul(pini2[:], p.gnwrow[:], p.ones1[:, 0:1],
                    start=True, stop=True), p.s_pe)
        v.wait_ge(p.s_pe.h, p.s_pe.n)
        v.tensor_copy(p.b2b[:], pini[:])
        I1(v.tensor_copy(p.gnwcol[:], pini2[:]), p.s_ve)
    g.wait_ge(p.s_ge.h, p.s_ge.n)
    s.wait_ge(p.s_ge.h, p.s_ge.n)
    s.activation(p.stats[:, 0:1], p.eps_c[:], AF.Square)
    s.activation(p.stats[:, 0:1], p.eps_c[:], AF.Ln)
    s.activation(p.stats[:, 0:1], p.eps_c[:], AF.Exp)
    I1(s.activation(p.stats[:, 0:1], p.eps_c[:], AF.Relu), p.s_se)
    s.wait_ge(p.s_se.h, p.s_se.n)
    g.wait_ge(p.s_w.h, p.s_w.n)
    y.wait_ge(p.s_ve.h, p.s_ve.n)


def emit_A1(p):
    nc = p.nc
    sb = p.ses.enter_context
    p.esKV = ExitStack()
    p.K_sb = p.esKV.enter_context(
        nc.sbuf_tensor("K_sb", [128, SEQ * NKV * T], BF16, side="right"))
    p.V_sb = p.esKV.enter_context(
        nc.sbuf_tensor("V_sb", [128, SEQ * NT * 512], BF16, side="right"))

    aWq_sb = sb(nc.sbuf_tensor("aWq_sb", [128, 16 * 2048], BF16))
    aWk_sb = sb(nc.sbuf_tensor("aWk_sb", [128, 16 * 512], BF16))
    aWv_sb = sb(nc.sbuf_tensor("aWv_sb", [128, 16 * 512], BF16))
    cos_sb = sb(nc.sbuf_tensor("cos_sb", [128, 16 * 128], BF16))
    sin_sb = sb(nc.sbuf_tensor("sin_sb", [128, 16 * 128], BF16))
    xt = [sb(nc.sbuf_tensor(f"xtA{i}", [128, D], F32)) for i in range(2)]
    ropeA = sb(nc.sbuf_tensor("ropeA", [128, 128], F32))
    ropeB = sb(nc.sbuf_tensor("ropeB", [128, 128], F32))
    xn_bf = sb(nc.sbuf_tensor("xn_bfA", [128, D], BF16))
    xnT = sb(nc.sbuf_tensor("xnTA", [128, D], BF16))
    qrot = sb(nc.sbuf_tensor("qrotA", [128, NH * HD], BF16))
    krot = sb(nc.sbuf_tensor("krotA", [128, NKV * HD], BF16))
    tmp8 = sb(nc.sbuf_tensor("tmp8A", [128, 8], F32))
    pQ = sb(nc.psum_tensor("pQ", [128, 2048], F32))
    pKV = sb(nc.psum_tensor("pKV", [128, 1024], F32))
    pT = [sb(nc.psum_tensor(f"pTA{i}", [128, 512], BF16)) for i in range(2)]
    p.pT_free = {}

    g, v, s, t, y = p.g, p.v, p.s, p.t, p.y

    for c in range(16):
        I16(g.dma_start(out=aWq_sb[:, c * 2048:(c + 1) * 2048],
                        in_=p.aWq[c * 128:(c + 1) * 128, :]), p.s_w)
        I16(g.dma_start(out=aWk_sb[:, c * 512:(c + 1) * 512],
                        in_=p.aWk[c * 128:(c + 1) * 128, :]), p.s_w)
        I16(g.dma_start(out=aWv_sb[:, c * 512:(c + 1) * 512],
                        in_=p.aWv[c * 128:(c + 1) * 128, :]), p.s_w)
    w_done = p.s_w.n

    I16(g.dma_start(out=cos_sb[:],
                    in_=p.coscat.rearrange("(n p) m -> p n m", p=128)), p.s_w)
    I16(g.dma_start(out=sin_sb[:],
                    in_=p.sincat.rearrange("(n p) m -> p n m", p=128)), p.s_w)
    trig_done = p.s_w.n
    ve_apply = {}
    st_q = {}
    for j in range(NTILES):
        sq, jj = divmod(j, NT)
        buf = j % 2
        sx = p.s_x0 if buf == 0 else p.s_x1
        if j >= 2:
            y.wait_ge(p.s_ve.h, ve_apply[j - 2])
        x_here = I16(y.dma_start(xt[buf][:], p.x0[j * 128:(j + 1) * 128, :]), sx)
        v.wait_ge(sx.h, x_here)
        s.wait_ge(sx.h, x_here)
        if j >= 1:
            v.wait_ge(p.s_pe.h, p.pe_xn_done)
            s.wait_ge(p.s_pe.h, p.pe_xn_done)
        ve_apply[j] = _ln(p, xt[buf][:], xn_bf[:], tmp8, xn_bf[:],
                          store_col=j)

        if j == 0:
            t.wait_ge(p.s_w.h, w_done)
        t.wait_ge(p.s_ve.h, ve_apply[j])
        _xnT(p, xn_bf, xnT, pT)
        p.pe_xn_done = p.s_pe.n
        if j >= 1:
            t.wait_ge(p.s_ve.h, p.ve_kv_done)
        for d in range(16):
            t.matmul(pKV[:, 0:512], xnT[:, d * 128:(d + 1) * 128],
                     aWk_sb[:, d * 512:(d + 1) * 512], start=(d == 0),
                     stop=(d == 15))
        for d in range(16):
            mm = t.matmul(pKV[:, 512:1024], xnT[:, d * 128:(d + 1) * 128],
                          aWv_sb[:, d * 512:(d + 1) * 512], start=(d == 0),
                          stop=(d == 15))
        pe_kv = I1(mm, p.s_pe)
        if j >= 1:
            t.wait_ge(p.s_ve.h, p.ve_q_done)
        for jc in range(4):
            for d in range(16):
                mm = t.matmul(
                    pQ[:, jc * 512:(jc + 1) * 512],
                    xnT[:, d * 128:(d + 1) * 128],
                    aWq_sb[:, d * 2048 + jc * 512:d * 2048 + (jc + 1) * 512],
                    start=(d == 0), stop=(d == 15))
        pe_q = I1(mm, p.s_pe)

        if j == 0:
            v.wait_ge(p.s_w.h, trig_done)
        v.wait_ge(p.s_pe.h, pe_kv)
        ct = cos_sb[:, jj * 128:(jj + 1) * 128]
        sn = sin_sb[:, jj * 128:(jj + 1) * 128]
        v.tensor_copy(p.V_sb[:, j * 512:(j + 1) * 512], pKV[:, 512:1024])
        for h in range(NKV):
            v.tensor_tensor(ropeB[:, 0:64], pKV[:, h * 128 + 64:(h + 1) * 128],
                            sn[:, 0:64], ALU.mult)
            v.tensor_tensor(ropeB[:, 64:128], pKV[:, h * 128:h * 128 + 64],
                            sn[:, 64:128], ALU.mult)
            v.tensor_tensor(ropeA[:], pKV[:, h * 128:(h + 1) * 128], ct, ALU.mult)
            mm = v.tensor_tensor(krot[:, h * 128:(h + 1) * 128], ropeA[:],
                                 ropeB[:], ALU.add)
        p.ve_kv_done = I1(mm, p.s_ve)
        v.wait_ge(p.s_pe.h, pe_q)
        if j >= 1:
            v.wait_ge(p.s_st.h, st_q[j - 1])
        for h in range(NH):
            v.tensor_tensor(ropeB[:, 0:64], pQ[:, h * 128 + 64:(h + 1) * 128],
                            sn[:, 0:64], ALU.mult)
            v.tensor_tensor(ropeB[:, 64:128], pQ[:, h * 128:h * 128 + 64],
                            sn[:, 64:128], ALU.mult)
            v.tensor_tensor(ropeA[:], pQ[:, h * 128:(h + 1) * 128], ct, ALU.mult)
            mm = v.tensor_tensor(qrot[:, h * 128:(h + 1) * 128], ropeA[:],
                                 ropeB[:], ALU.add)
        p.ve_q_done = I1(mm, p.s_ve)
        ve_q = p.s_ve.n

        t.wait_ge(p.s_ve.h, p.ve_kv_done)
        pb = pT[0]
        last = p.pT_free.get(id(pb), 0)
        if last:
            t.wait_ge(p.s_v2.h, last)
        for hh in range(NKV):
            mm = t.transpose(pb[:, hh * 128:(hh + 1) * 128],
                             krot[:, hh * 128:(hh + 1) * 128], p.ident[:])
        I1(mm, p.s_pe)
        v.wait_ge(p.s_pe.h, p.s_pe.n)
        kdst = p.K_sb[:].rearrange("p (a b) -> p a b", a=SEQ * NKV)[
            :, sq * NKV:(sq + 1) * NKV, jj * 128:(jj + 1) * 128]
        p.pT_free[id(pb)] = I1(
            v.tensor_copy(kdst, pb[:].rearrange("p (a b) -> p a b", a=4)),
            p.s_v2)

        y.wait_ge(p.s_ve.h, ve_q)
        st_q[j] = I16(y.dma_start(p.qscr[sq, jj * 128:(jj + 1) * 128, :],
                                  qrot[:]), p.s_st)

    y.wait_ge(p.s_st.h, p.s_st.n)
    g.wait_ge(p.s_w.h, p.s_w.n)
    if p.tap == "A1":
        y.wait_ge(p.s_v2.h, p.s_v2.n)
        y.wait_ge(p.s_se.h, p.s_se.n)
        I16(y.dma_start(p.stats_tap, p.stats[:]), p.s_st)
        I16(y.dma_start(p.kfm_tap, p.K_sb[:]), p.s_st)
        I16(y.dma_start(p.vtm_tap, p.V_sb[:]), p.s_st)
        y.wait_ge(p.s_st.h, p.s_st.n)


def emit_A2(p):
    nc = p.nc
    sb = p.ses.enter_context
    aWo_sb = sb(nc.sbuf_tensor("aWo_sb", [128, 16 * 2048], BF16))
    qld = [sb(nc.sbuf_tensor(f"qldA2{i}", [128, 512], BF16)) for i in range(2)]
    qfm = sb(nc.sbuf_tensor("qfmA2", [128, 128], BF16))
    pexp2 = [sb(nc.sbuf_tensor(f"pexpA2{i}", [128, 2048], BF16))
             for i in range(2)]
    diag2 = [sb(nc.sbuf_tensor(f"diagA2{i}", [128, 128], BF16))
             for i in range(2)]
    PT_sb = sb(nc.sbuf_tensor("PTA2", [128, 16 * 512], BF16))
    O_sb = sb(nc.sbuf_tensor("OA2", [128, 16 * 128], BF16))
    xt = [sb(nc.sbuf_tensor(f"xtA2{i}", [128, D], F32)) for i in range(2)]
    x1stg = sb(nc.sbuf_tensor("x1stgA2", [128, D], F32))
    lrc = sb(nc.sbuf_tensor("lrcA2", [128, 12], F32))
    pBig = sb(nc.psum_tensor("pBigA2", [128, 2048], F32))
    pTT = [sb(nc.psum_tensor(f"pTTA2{i}", [128, 512], F32)) for i in range(2)]
    pO = sb(nc.psum_tensor("pOA2", [128, 512], F32))
    pTq = sb(nc.psum_tensor("pTqA2", [128, 128], BF16))

    g, v, s, t, y = p.g, p.v, p.s, p.t, p.y

    for c in range(16):
        I16(g.dma_start(out=aWo_sb[:, c * 2048:(c + 1) * 2048],
                        in_=p.aWo[c * 128:(c + 1) * 128, :]), p.s_w)
    wo_done = p.s_w.n

    qtr_done = {}
    p.q_here = {}
    p.pe_S_last = 0
    p.se_qfm_last = 0
    p.se_sevac_last = 0
    p.se_exp_last = 0
    p.ve_max_last = 0
    p.ve_norm_last = 0
    p.norm_done = {}
    p.exp_done = {}
    p.last_unit = None
    p.pe_PT = {}
    p.head_norm = {}
    p.pt_ready = {}
    p.ve_O_last = 0
    p.ve_resid_last = 0
    ve_resid = {}
    pTT_free = {}
    st_x1 = {}
    first_wo = True

    for s0 in range(SEQ):
        for qt in range(NT):
            gidx = (s0 * NT + qt)
            def issue_qload(qj):
                if qj in p.q_here or qj >= 4 * SEQ * NT:
                    return
                gg2, gidx2 = qj % 4, qj // 4
                qt2, s02 = gidx2 % NT, gidx2 // NT
                sq2 = p.s_q0 if qj % 2 == 0 else p.s_q1
                if qj >= 2:
                    y.wait_ge(p.s_pe.h, qtr_done[qj - 2])
                p.q_here[qj] = I16(y.dma_start(
                    qld[qj % 2][:],
                    p.qscr[s02, qt2 * 128:(qt2 + 1) * 128,
                           4 * gg2 * 128:(4 * gg2 + 4) * 128]), sq2)

            def emitP(gg, hl):
                hu = gg * 4 + hl
                pexp = pexp2[hu % 2]
                qi = gidx * 4 + gg
                qbuf = qi % 2
                sq = p.s_q0 if qbuf == 0 else p.s_q1
                if hl == 0:
                    issue_qload(qi)
                    issue_qload(qi + 1)
                t.wait_ge(sq.h, p.q_here[qi])
                if p.se_qfm_last:
                    t.wait_ge(p.s_s2.h, p.se_qfm_last)
                I1(t.transpose(pTq[:], qld[qbuf][:, hl * 128:(hl + 1) * 128],
                               p.ident[:]), p.s_pe)
                if hl == 3:
                    qtr_done[qi] = p.s_pe.n
                s.wait_ge(p.s_pe.h, p.s_pe.n)
                if p.pe_S_last:
                    s.wait_ge(p.s_pe.h, p.pe_S_last)
                I1(s.activation(qfm[:], pTq[:], AF.Copy, bias=0.0,
                                scale=float(QK_SCALE)), p.s_s2)
                p.se_qfm_last = p.s_s2.n
                t.wait_ge(p.s_s2.h, p.se_qfm_last)
                if p.se_exp_last:
                    t.wait_ge(p.s_se.h, p.se_exp_last)
                    t.wait_ge(p.s_ve.h, p.ve_max_last)
                if gidx > 0 and gg == 0 and hl == 0:
                    t.wait_ge(p.s_ve.h, p.ve_resid_last)
                kbase = (s0 * NKV + gg) * 2048
                p.pe_S_half = []
                for jc in range(4):
                    mm = t.matmul(pBig[:, jc * 512:(jc + 1) * 512], qfm[:],
                                  p.K_sb[:, kbase + jc * 512:kbase + (jc + 1) * 512],
                                  start=True, stop=True)
                    if jc % 2 == 1:
                        p.pe_S_half.append(I1(mm, p.s_pe))
                p.pe_S_last = p.pe_S_half[-1]

            def emitP2(gg, hl):
                hu = gg * 4 + hl
                pexp = pexp2[hu % 2]
                c_hi, c_lo, c_ne = (2, 3, 8) if hu % 2 == 0 else (6, 7, 9)
                v.wait_ge(p.s_pe.h, p.pe_S_half[0])
                v.reduce_max(lrc[:, c_lo:c_lo + 1], pBig[:, 0:1024],
                             axis=AX, negate=True)
                v.wait_ge(p.s_pe.h, p.pe_S_half[1])
                I1(v.reduce_max(lrc[:, c_hi:c_hi + 1], pBig[:, 1024:2048],
                                axis=AX, negate=True), p.s_ve)
                vsep(p)
                p.ve_max_last = I1(
                    v.tensor_tensor(lrc[:, c_ne:c_ne + 1],
                                    lrc[:, c_hi:c_hi + 1],
                                    lrc[:, c_lo:c_lo + 1], ALU.min), p.s_ve)
                s.wait_ge(p.s_ve.h, p.ve_max_last)
                if p.pe_PT.get(hu % 2, 0):
                    s.wait_ge(p.s_pe.h, p.pe_PT[hu % 2])
                for hc in range(2):
                    mm = s.activation(pexp[:, hc * 1024:(hc + 1) * 1024],
                                      pBig[:, hc * 1024:(hc + 1) * 1024],
                                      AF.Exp, bias=lrc[:, c_ne:c_ne + 1],
                                      scale=1.0, accum_out=lrc[:, hc:hc + 1])
                p.se_exp_last = I1(mm, p.s_se)
                p.exp_done[(gg, hl)] = p.se_exp_last
                v.wait_ge(p.s_se.h, p.se_exp_last)
                v.tensor_tensor(lrc[:, 4:5], lrc[:, 0:1], lrc[:, 1:2], ALU.add)
                vsep(p)
                v.reciprocal(lrc[:, 5:6], lrc[:, 4:5])
                vsep(p)
                if p.pe_PT.get(hu % 2, 0):
                    v.wait_ge(p.s_pe.h, p.pe_PT[hu % 2])
                p.ve_norm_last = I1(v.tensor_scalar(
                    out=diag2[hu % 2][:], in0=p.ident[:], scalar1=lrc[:, 5:6],
                    scalar2=None, op0=ALU.mult), p.s_ve)
                vsep(p)
                p.norm_done[hu % 2] = p.ve_norm_last
                p.head_norm[(gg, hl)] = p.ve_norm_last

            def emitT(gg, hl):
                hu = gg * 4 + hl
                pexp = pexp2[hu % 2]
                dg = diag2[hu % 2]
                t.wait_ge(p.s_ve.h, p.head_norm[(gg, hl)])
                t.wait_ge(p.s_se.h, p.exp_done[(gg, hl)])
                for grp in range(4):
                    pb = pTT[grp % 2]
                    last = pTT_free.get(grp % 2, 0)
                    if last:
                        t.wait_ge(p.s_v2.h, last)
                    for c in range(4):
                        kt = grp * 4 + c
                        mm = t.matmul(pb[:, c * 128:(c + 1) * 128],
                                      pexp[:, kt * 128:(kt + 1) * 128],
                                      dg[:], start=True, stop=True)
                    I1(mm, p.s_pe)
                    if grp == 3:
                        p.pe_PT[hu % 2] = p.s_pe.n
                    dst = PT_sb[:].rearrange("p (a b) -> p a b", a=16)[
                        :, grp * 4:(grp + 1) * 4, hl * 128:(hl + 1) * 128]
                    src = pb[:].rearrange("p (a b) -> p a b", a=4)
                    s.wait_ge(p.s_pe.h, p.s_pe.n)
                    ec = s.copy(dst, src)
                    ec.then_inc(p.s_v2.h, 1)
                    p.s_v2.n += 1
                    pTT_free[grp % 2] = p.s_v2.n
                p.pt_ready[(gg, hl)] = p.s_v2.n

            def emitPV(gg):
                t.wait_ge(p.s_v2.h, p.pt_ready[(gg, 3)])
                if p.ve_O_last:
                    t.wait_ge(p.s_ve.h, p.ve_O_last)
                vbase = s0 * NT
                for kt in range(16):
                    mm = t.matmul(pO[:],
                                  p.V_sb[:, (vbase + kt) * 512 + gg * 128:
                                         (vbase + kt) * 512 + (gg + 1) * 128],
                                  PT_sb[:, kt * 512:(kt + 1) * 512],
                                  start=(kt == 0), stop=(kt == 15))
                I1(mm, p.s_pe)
                v.wait_ge(p.s_pe.h, p.s_pe.n)
                p.ve_O_last = I1(
                    v.tensor_copy(O_sb[:, gg * 512:(gg + 1) * 512], pO[:]),
                    p.s_ve)
                vsep(p)

            units = [(gg, hl) for gg in range(4) for hl in range(4)]
            emitP(*units[0])
            emitP2(*units[0])
            for i in range(1, 16):
                emitP(*units[i])
                emitT(*units[i - 1])
                if units[i - 1][1] == 3:
                    emitPV(units[i - 1][0])
                emitP2(*units[i])
            emitT(*units[15])
            emitPV(3)
            if first_wo:
                t.wait_ge(p.s_w.h, wo_done)
                first_wo = False
            t.wait_ge(p.s_ve.h, p.ve_O_last)
            t.wait_ge(p.s_se.h, p.se_exp_last)
            t.wait_ge(p.s_ve.h, p.ve_max_last)
            for dc in range(4):
                for j16 in range(16):
                    mm = t.matmul(
                        pBig[:, dc * 512:(dc + 1) * 512],
                        O_sb[:, j16 * 128:(j16 + 1) * 128],
                        aWo_sb[:, j16 * 2048 + dc * 512:j16 * 2048 + (dc + 1) * 512],
                        start=(j16 == 0), stop=(j16 == 15))
            pe_wo = I1(mm, p.s_pe)
            xbuf = gidx % 2
            sx = p.s_x0 if xbuf == 0 else p.s_x1
            if gidx >= 2:
                y.wait_ge(p.s_ve.h, ve_resid[gidx - 2])
            x_here = I16(y.dma_start(
                xt[xbuf][:],
                p.x0[(s0 * NT + qt) * 128:(s0 * NT + qt + 1) * 128, :]), sx)
            v.wait_ge(sx.h, x_here)
            v.wait_ge(p.s_pe.h, pe_wo)
            if gidx >= 1:
                v.wait_ge(p.s_st.h, st_x1[gidx - 1])
            ve_resid[gidx] = I1(
                v.tensor_tensor(x1stg[:], pBig[:], xt[xbuf][:], ALU.add),
                p.s_ve)
            p.ve_resid_last = ve_resid[gidx]
            y.wait_ge(p.s_ve.h, ve_resid[gidx])
            st_x1[gidx] = I16(
                y.dma_start(p.x1scr[(s0 * NT + qt) * 128:
                                    (s0 * NT + qt + 1) * 128, :], x1stg[:]),
                p.s_st)

    y.wait_ge(p.s_st.h, p.s_st.n)
    g.wait_ge(p.s_w.h, p.s_w.n)
    p.esKV.close()
    p.esKV = None


def emit_ffn(p, src, dst, tag=""):
    nc = p.nc
    sb = p.ses.enter_context
    fW1_sb = sb(nc.sbuf_tensor(f"fW1_sb{tag}", [128, 16 * 2048], BF16))
    fV_sb = sb(nc.sbuf_tensor(f"fV_sb{tag}", [128, 16 * 512], BF16))
    fU_sb = sb(nc.sbuf_tensor(f"fU_sb{tag}", [128, 4 * 2048], BF16))
    fW2_sb = sb(nc.sbuf_tensor(f"fW2_sb{tag}", [128, 16 * 8], BF16))
    fb1_sb = sb(nc.sbuf_tensor(f"fb1_sb{tag}", [128, 16], F32))
    xt = [sb(nc.sbuf_tensor(f"xtF{tag}{i}", [128, D], F32)) for i in range(4)]
    xn_bf = sb(nc.sbuf_tensor(f"xn_bfF{tag}", [128, D], BF16))
    xnT_st = sb(nc.sbuf_tensor(f"xnT_stF{tag}", [128, 16 * 512], BF16))
    h_sb = sb(nc.sbuf_tensor(f"h_sbF{tag}", [128, 16 * 512], BF16))
    alpha = sb(nc.sbuf_tensor(f"alphaF{tag}", [128, 4 * 8], F32))
    altmp = sb(nc.sbuf_tensor(f"altmpF{tag}", [128, 12], F32))
    vxs = [sb(nc.sbuf_tensor(f"vxsF{tag}{i}", [128, 512], BF16)) for i in range(2)]
    vxT = [sb(nc.sbuf_tensor(f"vxTF{tag}{i}", [128, 512], BF16)) for i in range(2)]
    x2stg = sb(nc.sbuf_tensor(f"x2stgF{tag}", [128, D], F32))
    tmp8 = sb(nc.sbuf_tensor(f"tmp8F{tag}", [128, 8], F32))
    pH = sb(nc.psum_tensor(f"pHF{tag}", [128, 512], F32))
    pAl = sb(nc.psum_tensor(f"pAlF{tag}", [128, 8], F32))
    pVx = sb(nc.psum_tensor(f"pVxF{tag}", [128, 512], F32))
    pTT = [sb(nc.psum_tensor(f"pTTF{tag}", [128, 512], BF16))]
    pY = sb(nc.psum_tensor(f"pYF{tag}", [128, 2048], F32))

    g, v, s, t, y = p.g, p.v, p.s, p.t, p.y

    for c in range(16):
        I16(g.dma_start(out=fW1_sb[:, c * 2048:(c + 1) * 2048],
                        in_=p.fW1[c * 128:(c + 1) * 128, :]), p.s_w)
        I16(g.dma_start(out=fV_sb[:, c * 512:(c + 1) * 512],
                        in_=p.fV[c * 128:(c + 1) * 128, :]), p.s_w)
        I16(g.dma_start(out=fW2_sb[:, c * 8:(c + 1) * 8],
                        in_=p.fW2[c * 128:(c + 1) * 128, :]), p.s_w)
    for c in range(4):
        I16(g.dma_start(out=fU_sb[:, c * 2048:(c + 1) * 2048],
                        in_=p.fU[c * 128:(c + 1) * 128, :]), p.s_w)
    I16(g.dma_start(out=fb1_sb[:], in_=p.fb1c), p.s_w)
    w_done = p.s_w.n

    st_x2 = {}
    ve_resid = {}
    pe_alpha_last = 0
    se_h_last = 0
    se_al_last = 0
    p.ve_al_done = {}
    p.ve_vx_free = {}
    p.pT_free.pop(id(pTT[0]), None)
    pTT_free = [0]
    first = True
    for sti in range(8):
        parity = sti % 2
        sx = p.s_x0 if parity == 0 else p.s_x1
        for j in range(4):
            gi = sti * 4 + j
            if sti >= 1:
                y.wait_ge(p.s_ve.h, ve_resid[(sti - 1) * 4 + j])
            I16(y.dma_start(xt[j][:],
                            src[gi * 128:(gi + 1) * 128, :]), sx)
        x_all = sx.n
        for j in range(4):
            v.wait_ge(sx.h, x_all)
            s.wait_ge(sx.h, x_all)
            if not first:
                v.wait_ge(p.s_pe.h, p.pe_xn_doneF)
                s.wait_ge(p.s_pe.h, p.pe_xn_doneF)
            first = False
            if sti >= 1 and j == 0:
                v.wait_ge(p.s_pe.h, p.pe_h_lastF)
            ve_ap = _ln(p, xt[j][:], xn_bf[:], tmp8, xn_bf[:])
            t.wait_ge(p.s_ve.h, ve_ap)
            if sti == 0 and j == 0:
                t.wait_ge(p.s_w.h, w_done)
            _xnT(p, xn_bf, None, pTT + pTT,
                 dst_map=lambda grp, j=j: xnT_st[:].rearrange(
                     "p (a b) -> p a b", a=16)[:, grp * 4:(grp + 1) * 4,
                                               j * 128:(j + 1) * 128])
            p.pe_xn_doneF = p.s_pe.n
        for e in range(16):
            if se_h_last:
                t.wait_ge(p.s_se.h, se_h_last)
            if e == 0 and p.ve_vx_free.get(1, 0):
                t.wait_ge(p.s_ve.h, p.ve_vx_free[1])
            for d in range(16):
                mm = t.matmul(pH[:],
                              fW1_sb[:, d * 2048 + e * 128:d * 2048 + (e + 1) * 128],
                              xnT_st[:, d * 512:(d + 1) * 512],
                              start=(d == 0), stop=(d == 15))
            I1(mm, p.s_pe)
            if e == 15:
                p.pe_h_lastF = p.s_pe.n
            s.wait_ge(p.s_pe.h, p.s_pe.n)
            if pe_alpha_last:
                s.wait_ge(p.s_pe.h, pe_alpha_last)
            if p.ve_vx_free.get(1, 0):
                s.wait_ge(p.s_ve.h, p.ve_vx_free[1])
            se_h_last = I1(s.activation(h_sb[:, e * 512:(e + 1) * 512], pH[:],
                                        AF.Relu, bias=fb1_sb[:, e:e + 1],
                                        scale=1.0), p.s_se)
        for j in range(4):
            t.wait_ge(p.s_se.h, se_h_last)
            if se_al_last:
                t.wait_ge(p.s_se.h, se_al_last)
            for e in range(16):
                mm = t.matmul(pAl[:],
                              h_sb[:, e * 512 + j * 128:e * 512 + (j + 1) * 128],
                              fW2_sb[:, e * 8:(e + 1) * 8],
                              start=(e == 0), stop=(e == 15))
            pe_alpha_last = I1(mm, p.s_pe)
            v.wait_ge(p.s_pe.h, pe_alpha_last)
            if p.ve_al_done.get(j, 0):
                pass
            v.tensor_tensor(altmp[:, 0:8], pAl[:], p.b2b[:], ALU.add)
            vsep(p)
            I1(v.reduce_max(altmp[:, 8:9], altmp[:, 0:8], axis=AX, negate=True),
               p.s_ve)
            s.wait_ge(p.s_ve.h, p.s_ve.n)
            se_al_last = I1(
                s.activation(tmp8[:, 0:8], altmp[:, 0:8], AF.Exp,
                             bias=altmp[:, 8:9], scale=1.0,
                             accum_out=altmp[:, 9:10]), p.s_se)
            v.wait_ge(p.s_se.h, se_al_last)
            v.reciprocal(altmp[:, 10:11], altmp[:, 9:10])
            vsep(p)
            p.ve_al_done[j] = I1(v.tensor_scalar(
                out=alpha[:, j * 8:(j + 1) * 8], in0=tmp8[:, 0:8],
                scalar1=altmp[:, 10:11], scalar2=None, op0=ALU.mult), p.s_ve)
            vsep(p)
        for j in range(4):
            gi = sti * 4 + j
            par = j % 2
            pvx = pVx if par == 0 else pH
            if par == 1:
                t.wait_ge(p.s_se.h, se_h_last)
            if p.ve_vx_free.get(par, 0):
                t.wait_ge(p.s_ve.h, p.ve_vx_free[par])
            for d in range(16):
                mm = t.matmul(pvx[:, 0:512],
                              xnT_st[:, d * 512 + j * 128:d * 512 + (j + 1) * 128],
                              fV_sb[:, d * 512:(d + 1) * 512],
                              start=(d == 0), stop=(d == 15))
            I1(mm, p.s_pe)
            v.wait_ge(p.s_pe.h, p.s_pe.n)
            v.wait_ge(p.s_ve.h, p.ve_al_done[j])
            if p.pT_free.get(("vxsT", par), 0):
                v.wait_ge(p.s_pe.h, p.pT_free[("vxsT", par)])
            for k in range(8):
                mm = v.tensor_scalar(out=vxs[par][:, k * 64:(k + 1) * 64],
                                     in0=pvx[:, k * 64:(k + 1) * 64],
                                     scalar1=alpha[:, j * 8 + k:j * 8 + k + 1],
                                     scalar2=None, op0=ALU.mult)
            p.ve_vx_free[par] = I1(mm, p.s_ve)
            vsep(p)
            t.wait_ge(p.s_ve.h, p.ve_vx_free[par])
            if pTT_free[0]:
                t.wait_ge(p.s_v2.h, pTT_free[0])
            for c in range(4):
                mm = t.transpose(pTT[0][:, c * 128:(c + 1) * 128],
                                 vxs[par][:, c * 128:(c + 1) * 128], p.ident[:])
            I1(mm, p.s_pe)
            p.pT_free[("vxsT", par)] = p.s_pe.n
            s.wait_ge(p.s_pe.h, p.s_pe.n)
            sc = s.copy(vxT[par][:], pTT[0][:])
            sc.then_inc(p.s_v2.h, 1)
            p.s_v2.n += 1
            pTT_free[0] = p.s_v2.n
            t.wait_ge(p.s_v2.h, pTT_free[0])
            xb = gi % 2
            for dh in range(2):
                pyb = pY[:, dh * 1024:(dh + 1) * 1024]
                if ve_resid.get((gi - 1, dh), 0):
                    t.wait_ge(p.s_ve.h, ve_resid[(gi - 1, dh)])
                for kr in range(4):
                    for dc in range(2):
                        mm = t.matmul(
                            pyb[:, dc * 512:(dc + 1) * 512],
                            vxT[par][:, kr * 128:(kr + 1) * 128],
                            fU_sb[:, kr * 2048 + (dh * 2 + dc) * 512:
                                  kr * 2048 + (dh * 2 + dc + 1) * 512],
                            start=(kr == 0), stop=(kr == 3))
                pe_y = I1(mm, p.s_pe)
                v.wait_ge(p.s_pe.h, pe_y)
                if dh == 0 and st_x2.get(gi - 1, 0):
                    v.wait_ge(p.s_st.h, st_x2[gi - 1])
                ve_resid[(gi, dh)] = I1(
                    v.tensor_tensor(x2stg[:, dh * 1024:(dh + 1) * 1024],
                                    pyb,
                                    xt[j][:, dh * 1024:(dh + 1) * 1024],
                                    ALU.add), p.s_ve)
            ve_resid[gi] = ve_resid[(gi, 1)]
            y.wait_ge(p.s_ve.h, ve_resid[gi])
            st_x2[gi] = I16(y.dma_start(dst[gi * 128:(gi + 1) * 128, :],
                                        x2stg[:]), p.s_st)
    y.wait_ge(p.s_st.h, p.s_st.n)
    g.wait_ge(p.s_w.h, p.s_w.n)


def emit_A3(p):
    emit_ffn(p, p.x1scr, p.x2scr, tag="a")


def emit_B1a(p):
    nc = p.nc
    sb = p.ses.enter_context
    bWq_sb = sb(nc.sbuf_tensor("bWq_sb", [128, 16 * 1024], BF16))
    bWk_sb = sb(nc.sbuf_tensor("bWk_sb", [128, 16 * 1024], BF16))
    bWs_sb = sb(nc.sbuf_tensor("bWs_sb", [128, 16 * 1024], BF16))
    xt = [sb(nc.sbuf_tensor(f"xtB1{i}", [128, D], F32)) for i in range(4)]
    xn_bf = sb(nc.sbuf_tensor("xn_bfB1", [128, D], BF16))
    xnT_st = sb(nc.sbuf_tensor("xnT_stB1", [128, 16 * 512], BF16))
    qstg = [sb(nc.sbuf_tensor(f"qstgB1{i}", [128, 512], BF16)) for i in range(2)]
    kstg = [sb(nc.sbuf_tensor(f"kstgB1{i}", [128, 512], BF16)) for i in range(2)]
    estg = sb(nc.sbuf_tensor("estgB1", [128, 1024], BF16))
    e32 = sb(nc.sbuf_tensor("e32B1", [128, 1024], F32))
    rzstg = sb(nc.sbuf_tensor("rzstgB1", [128, 1024], BF16))
    onescol = sb(nc.sbuf_tensor("onescolB1", [128, 1], BF16))
    ones1b = sb(nc.sbuf_tensor("ones1bB1", [1, 128], BF16))
    prehi = sb(nc.sbuf_tensor("prehiB1", [1, 1024], BF16))
    prelo = sb(nc.sbuf_tensor("preloB1", [1, 1024], BF16))
    prelof = sb(nc.sbuf_tensor("prelofB1", [1, 1024], F32))
    tmp8 = sb(nc.sbuf_tensor("tmp8B1", [128, 8], F32))
    pP = sb(nc.psum_tensor("pPB1", [128, 512], F32))
    pS2 = sb(nc.psum_tensor("pS2B1", [128, 1024], F32))
    pZ = sb(nc.psum_tensor("pZB1", [128, 1024], F32))
    pPre = sb(nc.psum_tensor("pPreB1", [1, 1024], F32))
    pT0 = sb(nc.psum_tensor("pTB1", [128, 512], BF16))
    pT = [pT0, pT0]
    p.pT_free = {}

    g, v, s, t, y = p.g, p.v, p.s, p.t, p.y

    for c in range(16):
        I16(g.dma_start(out=bWq_sb[:, c * 1024:(c + 1) * 1024],
                        in_=p.bWq[c * 128:(c + 1) * 128, :]), p.s_w)
        I16(g.dma_start(out=bWk_sb[:, c * 1024:(c + 1) * 1024],
                        in_=p.bWk[c * 128:(c + 1) * 128, :]), p.s_w)
        I16(g.dma_start(out=bWs_sb[:, c * 1024:(c + 1) * 1024],
                        in_=p.bWs[c * 128:(c + 1) * 128, :]), p.s_w)
    w_done = p.s_w.n
    I1(v.memset(onescol[:], 1.0), p.s_ve)
    I1(v.memset(ones1b[:], 1.0), p.s_ve)

    se_stg_last = {}
    st_last = {}
    ve_resid = {}
    first = True
    pe_proj_last = 0
    se_E_last = 0
    ve_rz_last = 0
    ve_pre_last = 0
    pe_zc_last = 0
    pe_pre_count = 0
    for s0 in range(SEQ):
        for ch in range(4):
            sti = s0 * 4 + ch
            parity = sti % 2
            sx = p.s_x0 if parity == 0 else p.s_x1
            for j in range(4):
                gi = sti * 4 + j
                if sti >= 1:
                    y.wait_ge(p.s_ve.h, ve_resid.get((sti - 1) * 4 + j, 0))
                I16(y.dma_start(xt[j][:],
                                p.x2scr[gi * 128:(gi + 1) * 128, :]), sx)
            x_all = sx.n
            for j in range(4):
                gi = sti * 4 + j
                v.wait_ge(sx.h, x_all)
                s.wait_ge(sx.h, x_all)
                if not first:
                    v.wait_ge(p.s_pe.h, p.pe_xn_doneF)
                    s.wait_ge(p.s_pe.h, p.pe_xn_doneF)
                first = False
                if sti >= 1 and j == 0:
                    v.wait_ge(p.s_pe.h, pe_proj_last)
                ve_ap = _ln(p, xt[j][:], xn_bf[:], tmp8, xn_bf[:],
                            store_col=gi)
                ve_resid[gi] = ve_ap
                t.wait_ge(p.s_ve.h, ve_ap)
                if sti == 0 and j == 0:
                    t.wait_ge(p.s_w.h, w_done)
                    t.wait_ge(p.s_ve.h, p.s_ve.n)
                _xnT(p, xn_bf, None, pT,
                     dst_map=lambda grp, j=j: xnT_st[:].rearrange(
                         "p (a b) -> p a b", a=16)[:, grp * 4:(grp + 1) * 4,
                                                   j * 128:(j + 1) * 128])
                p.pe_xn_doneF = p.s_pe.n
            pcnt = 0
            for which, wsb, stgs, scr in [(0, bWq_sb, qstg, p.qbscr),
                                          (1, bWk_sb, kstg, p.kbscr)]:
                for oc in range(8):
                    par = pcnt % 2
                    pcnt += 1
                    pb = pP[:] if par == 0 else pZ[:, 0:512]
                    if se_stg_last.get(("pp", par), 0):
                        t.wait_ge(p.s_se.h, se_stg_last[("pp", par)])
                    for d in range(16):
                        mm = t.matmul(
                            pb,
                            wsb[:, d * 1024 + oc * 128:d * 1024 + (oc + 1) * 128],
                            xnT_st[:, d * 512:(d + 1) * 512],
                            start=(d == 0), stop=(d == 15))
                    pe_proj_last = I1(mm, p.s_pe)
                    s.wait_ge(p.s_pe.h, pe_proj_last)
                    stg = stgs[par]
                    if st_last.get((which, par), 0):
                        s.wait_ge(p.s_st.h, st_last[(which, par)])
                    se_stg_last[(which, par)] = se_stg_last[("pp", par)] = I1(
                        s.copy(stg[:], pb), p.s_se)
                    y.wait_ge(p.s_se.h, se_stg_last[(which, par)])
                    st_last[(which, par)] = I16(
                        y.dma_start(scr[s0, oc, :, ch * 512:(ch + 1) * 512],
                                    stg[:]), p.s_st)
            if se_stg_last.get(("pp", 1), 0):
                t.wait_ge(p.s_se.h, se_stg_last[("pp", 1)])
            for j in range(4):
                gi = sti * 4 + j
                jj = ch * 4 + j
                if se_E_last:
                    t.wait_ge(p.s_se.h, se_E_last)
                for jc in range(2):
                    for d in range(16):
                        mm = t.matmul(
                            pS2[:, jc * 512:(jc + 1) * 512],
                            xnT_st[:, d * 512 + j * 128:d * 512 + (j + 1) * 128],
                            bWs_sb[:, d * 1024 + jc * 512:d * 1024 + (jc + 1) * 512],
                            start=(d == 0), stop=(d == 15))
                pe_s = I1(mm, p.s_pe)
                s.wait_ge(p.s_pe.h, pe_s)
                if ve_pre_last:
                    s.wait_ge(p.s_ve.h, ve_pre_last)
                se_E_last = I1(s.activation(e32[:], pS2[:], AF.Exp,
                                            bias=p.n32_c[:], scale=1.0), p.s_se)
                v.wait_ge(p.s_se.h, se_E_last)
                if pe_zc_last:
                    v.wait_ge(p.s_pe.h, pe_zc_last)
                if st_last.get("E", 0):
                    v.wait_ge(p.s_st.h, st_last["E"])
                v.tensor_scalar(out=e32[:], in0=e32[:], scalar1=1.0,
                                scalar2=None, op0=ALU.min)
                vsep(p)
                I1(v.tensor_scalar(out=estg[:], in0=e32[:], scalar1=1.6e-28,
                                   scalar2=None, op0=ALU.max), p.s_ve)
                ve_E = p.s_ve.n
                if jj > 0:
                    v.wait_ge(p.s_pe.h, pe_pre_count)
                    v.tensor_copy(prehi[:], pPre[:])
                    vsep(p)
                    v.tensor_tensor(prelof[:], pPre[:], prehi[:], ALU.subtract)
                    vsep(p)
                    I1(v.tensor_copy(prelo[:], prelof[:]), p.s_ve)
                ve_pre_last = p.s_ve.n
                t.wait_ge(p.s_ve.h, ve_pre_last)
                if ve_rz_last:
                    t.wait_ge(p.s_ve.h, ve_rz_last)
                for jc in range(2):
                    mm = t.matmul(pZ[:, jc * 512:(jc + 1) * 512], p.utinc[:],
                                  estg[:, jc * 512:(jc + 1) * 512],
                                  start=True, stop=(jj == 0))
                    if jj > 0:
                        mm = t.matmul(pZ[:, jc * 512:(jc + 1) * 512], ones1b[:],
                                      prehi[:, jc * 512:(jc + 1) * 512],
                                      start=False, stop=False)
                        mm = t.matmul(pZ[:, jc * 512:(jc + 1) * 512], ones1b[:],
                                      prelo[:, jc * 512:(jc + 1) * 512],
                                      start=False, stop=True)
                pe_zc = I1(mm, p.s_pe)
                for jc in range(2):
                    mm = t.matmul(pPre[:, jc * 512:(jc + 1) * 512], onescol[:],
                                  estg[:, jc * 512:(jc + 1) * 512],
                                  start=(jj == 0), stop=(jj == 15))
                pe_pre_count = I1(mm, p.s_pe)
                pe_zc_last = pe_pre_count
                v.wait_ge(p.s_pe.h, pe_zc)
                if st_last.get("rz", 0):
                    v.wait_ge(p.s_st.h, st_last["rz"])
                with nc.allow_low_precision(
                        reason="rz stored bf16 by design (ratio structure)"):
                    ve_rz_last = I1(v.reciprocal(rzstg[:], pZ[:]), p.s_ve)
                y.wait_ge(p.s_ve.h, ve_rz_last)
                st_last["E"] = I16(
                    y.dma_start(p.escr[s0, jj * 128:(jj + 1) * 128, :],
                                estg[:]), p.s_st)
                st_last["rz"] = I16(
                    y.dma_start(p.rzscr[s0, jj * 128:(jj + 1) * 128, :],
                                rzstg[:]), p.s_st)
    y.wait_ge(p.s_st.h, p.s_st.n)
    g.wait_ge(p.s_w.h, p.s_w.n)
    if p.tap == "B1a":
        y.wait_ge(p.s_se.h, p.s_se.n)
        I16(y.dma_start(p.stats_tap, p.stats[:]), p.s_st)
        y.wait_ge(p.s_st.h, p.s_st.n)


def emit_B1b(p):
    nc = p.nc
    sb = p.ses.enter_context
    bWv_sb = sb(nc.sbuf_tensor("bWv_sb", [128, 16 * 2048], BF16))
    bWg_sb = sb(nc.sbuf_tensor("bWg_sb", [128, 16 * 2048], BF16))
    xt = [sb(nc.sbuf_tensor(f"xtB2{i}", [128, D], F32)) for i in range(4)]
    xn_bf = sb(nc.sbuf_tensor("xn_bfB2", [128, D], BF16))
    xnT_st = sb(nc.sbuf_tensor("xnT_stB2", [128, 16 * 512], BF16))
    vstg = sb(nc.sbuf_tensor("vstgB2", [128, D], BF16))
    gstg = sb(nc.sbuf_tensor("gstgB2", [128, D], BF16))
    pV4 = sb(nc.psum_tensor("pV4B2", [128, 2048], F32))
    pT0 = sb(nc.psum_tensor("pTB2", [128, 512], BF16))
    pT = [pT0, pT0]
    p.pT_free = {}

    g, v, s, t, y = p.g, p.v, p.s, p.t, p.y

    for c in range(16):
        I16(g.dma_start(out=bWv_sb[:, c * 2048:(c + 1) * 2048],
                        in_=p.bWv[c * 128:(c + 1) * 128, :]), p.s_w)
        I16(g.dma_start(out=bWg_sb[:, c * 2048:(c + 1) * 2048],
                        in_=p.bWg[c * 128:(c + 1) * 128, :]), p.s_w)
    w_done = p.s_w.n

    st_last = {}
    ve_ap = {}
    se_stg_last = 0
    first = True
    for sti in range(8):
        s0 = sti // 4
        parity = sti % 2
        sx = p.s_x0 if parity == 0 else p.s_x1
        for j in range(4):
            gi = sti * 4 + j
            if sti >= 1:
                y.wait_ge(p.s_ve.h, ve_ap.get((sti - 1) * 4 + j, 0))
            I16(y.dma_start(xt[j][:], p.x2scr[gi * 128:(gi + 1) * 128, :]), sx)
        x_all = sx.n
        for j in range(4):
            gi = sti * 4 + j
            v.wait_ge(sx.h, x_all)
            if not first:
                v.wait_ge(p.s_pe.h, p.pe_xn_doneF)
            first = False
            ve_ap[gi] = I1(
                v.tensor_scalar(out=xn_bf[:], in0=xt[j][:],
                                scalar1=p.stats[:, 2 * gi:2 * gi + 1],
                                scalar2=p.stats[:, 2 * gi + 1:2 * gi + 2],
                                op0=ALU.subtract, op1=ALU.mult), p.s_ve)
            t.wait_ge(p.s_ve.h, ve_ap[gi])
            if sti == 0 and j == 0:
                t.wait_ge(p.s_w.h, w_done)
            _xnT(p, xn_bf, None, pT,
                 dst_map=lambda grp, j=j: xnT_st[:].rearrange(
                     "p (a b) -> p a b", a=16)[:, grp * 4:(grp + 1) * 4,
                                               j * 128:(j + 1) * 128])
            p.pe_xn_doneF = p.s_pe.n
        for j in range(4):
            gi = sti * 4 + j
            jj = gi - s0 * 16
            for which, wsb, stg, scr, fn in [
                    (0, bWv_sb, vstg, p.vbscr, AF.Copy),
                    (1, bWg_sb, gstg, p.gbscr, AF.Silu)]:
                if se_stg_last:
                    t.wait_ge(p.s_se.h, se_stg_last)
                for jc in range(4):
                    for d in range(16):
                        mm = t.matmul(
                            pV4[:, jc * 512:(jc + 1) * 512],
                            xnT_st[:, d * 512 + j * 128:d * 512 + (j + 1) * 128],
                            wsb[:, d * 2048 + jc * 512:d * 2048 + (jc + 1) * 512],
                            start=(d == 0), stop=(d == 15))
                pe_p = I1(mm, p.s_pe)
                s.wait_ge(p.s_pe.h, pe_p)
                if st_last.get(which, 0):
                    s.wait_ge(p.s_st.h, st_last[which])
                se_stg_last = I1(s.activation(stg[:], pV4[:], fn), p.s_se)
                y.wait_ge(p.s_se.h, se_stg_last)
                st_last[which] = I16(
                    y.dma_start(scr[s0, jj * 128:(jj + 1) * 128, :], stg[:]),
                    p.s_st)
    y.wait_ge(p.s_st.h, p.s_st.n)
    g.wait_ge(p.s_w.h, p.s_w.n)


def emit_B2(p):
    nc = p.nc
    sb = p.ses.enter_context
    bWo_sb = sb(nc.sbuf_tensor("bWo_sb", [128, 16 * 2048], BF16))
    utinc8 = sb(nc.sbuf_tensor("utinc8", [128, 128], BF16))
    o2fm = sb(nc.sbuf_tensor("o2fmB2", [128, 16 * 8 * 128], BF16))
    AT_sb = sb(nc.sbuf_tensor("AT_sbB2", [128, 16 * 512], BF16))
    WT_sb = sb(nc.sbuf_tensor("WT_sbB2", [128, 16 * 512], BF16))
    E_fm = [sb(nc.sbuf_tensor(f"E_fmB2{i}", [64, 16 * 128], BF16))
            for i in range(2)]
    P2_tm = sb(nc.sbuf_tensor("P2_tmB2", [128, 8 * 64], BF16))
    P2_fm = sb(nc.sbuf_tensor("P2_fmB2", [64, 8 * 128], BF16))
    q_h = [sb(nc.sbuf_tensor(f"q_hB2{i}", [64, 1024], BF16)) for i in range(2)]
    k_h = [sb(nc.sbuf_tensor(f"k_hB2{i}", [64, 2048], BF16)) for i in range(2)]
    E_h = [sb(nc.sbuf_tensor(f"E_hB2{i}", [128, 16 * 64], BF16)) for i in range(2)]
    rz_h = [sb(nc.sbuf_tensor(f"rz_hB2{i}", [128, 8 * 64], BF16)) for i in range(2)]
    v_h = [sb(nc.sbuf_tensor(f"v_hB2{i}", [128, 16 * 128], BF16)) for i in range(2)]
    gs_h = [sb(nc.sbuf_tensor(f"gs_hB2{i}", [128, 8 * 128], BF16)) for i in range(2)]
    okf8 = sb(nc.sbuf_tensor("okf8B2", [128, 512], F32))
    pvt8 = sb(nc.sbuf_tensor("pvt8B2", [128, 512], BF16))
    o_bf4 = sb(nc.sbuf_tensor("o_bf4B2", [128, 512], BF16))
    o2_4 = sb(nc.sbuf_tensor("o2_4B2", [128, 512], BF16))
    jnk = sb(nc.sbuf_tensor("jnkB2", [128, 128], BF16))
    ep32 = sb(nc.sbuf_tensor("ep32B2", [128, 40], F32))
    xt = [sb(nc.sbuf_tensor(f"xtB3{i}", [128, D], F32)) for i in range(2)]
    x3stg = sb(nc.sbuf_tensor("x3stgB2", [128, D], F32))
    pAT = [sb(nc.psum_tensor(f"pATB2{i}", [128, 512], F32)) for i in range(2)]
    pNO = sb(nc.psum_tensor("pNOB2", [128, 512], F32))
    pWT = [sb(nc.psum_tensor(f"pWTB2{i}", [128, 512], F32)) for i in range(2)]
    pTb = sb(nc.psum_tensor("pTbB2", [128, 512], BF16))
    pWo4 = sb(nc.psum_tensor("pWo4B2", [128, 1024], F32))

    g, v, s, t, y = p.g, p.v, p.s, p.t, p.y

    for c in range(16):
        I16(g.dma_start(out=bWo_sb[:, c * 2048:(c + 1) * 2048],
                        in_=p.bWo[c * 128:(c + 1) * 128, :]), p.s_w)
    w_done = p.s_w.n
    v.wait_ge(p.s_w.h, w_done)
    for c in range(16):
        mm = v.tensor_scalar(out=bWo_sb[:, c * 2048:(c + 1) * 2048],
                             in0=bWo_sb[:, c * 2048:(c + 1) * 2048],
                             scalar1=p.gnwcol[:], scalar2=None, op0=ALU.mult)
    I1(mm, p.s_ve)
    wo_scaled = p.s_ve.n
    I1(v.tensor_scalar(out=utinc8[:], in0=p.utinc[:], scalar1=float(BQ_SCALE),
                       scalar2=None, op0=ALU.mult), p.s_ve)
    ut8_done = p.s_ve.n

    p.pATfree = {}
    p.pWTfree = {}
    p.at_ev = {}
    p.num_rd = {}
    p.num_wr = {}
    p.pTb_free = 0
    p.pe_num_last = 0
    p.pe_ov_last = 0
    p.pe_o2T = 0
    p.ve_NO_read_last = 0
    p.se_NO_read_last = 0
    p.ve_o2_last = 0
    p.wo_resid_last = 0
    hload = {}
    hdone = {}
    ve_resid = {}
    st_x3 = {}
    se_AT_last = 0
    ve_ATd_last = 0
    pe_num_last = 0
    se_WT_last = 0
    pe_ov_last = 0
    pe_P2T_last = 0
    ve_NO_read_last = 0
    se_NO_read_last = 0
    pTb_free = 0
    pe_o2T = 0
    first_wo = True

    for s0 in range(SEQ):
        for half in range(2):
            hi = (half + 1) * 1024
            nt = hi // 128
            hseq = s0 * 2 + half
            def ph1(h):
                hb = h % 2
                stg = p.s_tg0 if hb == 0 else p.s_tg1
                hkey = hseq * BH + h
                if hkey >= 2:
                    y.wait_ge(p.s_pe.h, hdone[hkey - 2])
                oc, ro = h // 2, 64 * (h % 2)
                I16(y.dma_start(q_h[hb][:, 0:1024],
                                p.qbscr[s0, oc, ro:ro + 64,
                                        half * 1024:hi]), stg)
                I16(y.dma_start(k_h[hb][:, 0:hi],
                                p.kbscr[s0, oc, ro:ro + 64, 0:hi]), stg)
                I16(y.dma_start(
                    E_h[hb][:, 0:nt * 64],
                    p.escr[s0, 0:hi, h * 64:(h + 1) * 64].rearrange(
                        "(n q) m -> q n m", q=128)), stg)
                I16(y.dma_start(
                    rz_h[hb][:, 0:512],
                    p.rzscr[s0, half * 1024:hi,
                            h * 64:(h + 1) * 64].rearrange(
                        "(n q) m -> q n m", q=128)), stg)
                I16(y.dma_start(
                    v_h[hb][:, 0:nt * 128],
                    p.vbscr[s0, 0:hi, h * 128:(h + 1) * 128].rearrange(
                        "(n q) m -> q n m", q=128)), stg)
                I16(y.dma_start(
                    gs_h[hb][:, 0:1024],
                    p.gbscr[s0, half * 1024:hi,
                            h * 128:(h + 1) * 128].rearrange(
                        "(n q) m -> q n m", q=128)), stg)
                hload[hkey] = stg.n

                t.wait_ge(stg.h, hload[hkey])
                if h == 0 and hseq == 0:
                    t.wait_ge(p.s_ve.h, ut8_done)
                for grp in range((nt + 3) // 4):
                    if p.pTb_free:
                        t.wait_ge(p.s_v2.h, p.pTb_free)
                    for c in range(min(4, nt - grp * 4)):
                        tt = grp * 4 + c
                        mm = t.transpose(pTb[0:64, c * 128:(c + 1) * 128],
                                         E_h[hb][:, tt * 64:(tt + 1) * 64],
                                         p.ident[:])
                    I1(mm, p.s_pe)
                    v.wait_ge(p.s_pe.h, p.s_pe.n)
                    nw = min(4, nt - grp * 4)
                    p.pTb_free = I1(
                        v.tensor_copy(
                            E_fm[hb][0:64, grp * 512:grp * 512 + nw * 128],
                            pTb[0:64, 0:nw * 128]), p.s_v2)
                    vsep(p)

                for tc in range(2):
                    nbase = hb * 512 + tc * 256
                    tcg = half * 2 + tc
                    ntau = (tcg + 1) * 4
                    for tt in range(ntau):
                        pb = pAT[tt % 2]
                        fr = p.pATfree.get(tt % 2, (0, 0))
                        if fr[0]:
                            t.wait_ge(p.s_se.h, fr[0])
                        if fr[1]:
                            t.wait_ge(p.s_ve.h, fr[1])
                        if p.pe_num_last and tt == 0:
                            t.wait_ge(p.s_pe.h, p.pe_num_last)
                        mm = t.matmul(pb[:], k_h[hb][:, tt * 128:(tt + 1) * 128],
                                      q_h[hb][:, tc * 512:(tc + 1) * 512],
                                      start=True, stop=True)
                        I1(mm, p.s_pe)
                        cs = tt - tcg * 4
                        if tt % 2 == 0:
                            s.wait_ge(p.s_pe.h, p.s_pe.n)
                            se_m = I1(
                                s.activation(AT_sb[:, tt * 512:(tt + 1) * 512],
                                             pb[:], AF.Copy, bias=0.0,
                                             scale=float(BQ_SCALE)), p.s_se)
                            marks = (se_m, 0)
                            dve_t, dve_n = p.s_se.h, se_m
                        else:
                            v.wait_ge(p.s_pe.h, p.s_pe.n)
                            ve_m = I1(v.tensor_scalar(
                                out=AT_sb[:, tt * 512:(tt + 1) * 512],
                                in0=pb[:], scalar1=float(BQ_SCALE),
                                scalar2=None, op0=ALU.mult), p.s_ve)
                            marks = (0, ve_m)
                            dve_t, dve_n = p.s_ve.h, ve_m
                        if 0 <= cs < 4:
                            v.wait_ge(dve_t, dve_n)
                            ve_m = I1(v.tensor_tensor(
                                AT_sb[:, tt * 512 + cs * 128:
                                      tt * 512 + (cs + 1) * 128],
                                pb[:, cs * 128:(cs + 1) * 128],
                                utinc8[:], ALU.mult), p.s_ve)
                            marks = (marks[0], ve_m)
                            vsep(p)
                        p.pATfree[tt % 2] = marks
                        p.at_ev[tt] = marks
                    if p.num_rd.get((hb, tc), 0):
                        t.wait_ge(p.s_ve.h, p.num_rd[(hb, tc)])
                    if h == 0 and tc == 0 and p.wo_resid_last:
                        t.wait_ge(p.s_ve.h, p.wo_resid_last)
                    for tt in range(ntau):
                        mk = p.at_ev[tt]
                        if mk[0]:
                            t.wait_ge(p.s_se.h, mk[0])
                        if mk[1]:
                            t.wait_ge(p.s_ve.h, mk[1])
                        for c in range(4):
                            jt = tcg * 4 + c
                            if tt <= jt:
                                mm = t.matmul(
                                    pWo4[:, nbase + c * 64:nbase + (c + 1) * 64],
                                    AT_sb[:, tt * 512 + c * 128:
                                          tt * 512 + (c + 1) * 128],
                                    E_h[hb][:, tt * 64:(tt + 1) * 64],
                                    start=(tt == 0 and c == 0),
                                    stop=(tt == jt))
                    p.pe_num_last = I1(mm, p.s_pe)
                    p.num_wr[(hb, tc)] = p.pe_num_last

            def ph2(h):
                hb = h % 2
                v.wait_ge(p.s_pe.h, p.num_wr[(hb, 1)])
                nrd = I1(v.tensor_tensor(okf8[:],
                                         pWo4[:, hb * 512:(hb + 1) * 512],
                                         rz_h[hb][:, 0:512], ALU.mult),
                         p.s_ve)
                p.num_rd[(hb, 0)] = p.num_rd[(hb, 1)] = nrd
                for c8 in range(8):
                    mm = v.reduce_max(ep32[:, c8:c8 + 1],
                                      okf8[:, c8 * 64:(c8 + 1) * 64],
                                      axis=AX, negate=True)
                I1(mm, p.s_ve)
                s.wait_ge(p.s_ve.h, p.s_ve.n)
                for c8 in range(8):
                    mm = s.activation(pvt8[:, c8 * 64:(c8 + 1) * 64],
                                      okf8[:, c8 * 64:(c8 + 1) * 64], AF.Exp,
                                      bias=ep32[:, c8:c8 + 1], scale=1.0,
                                      accum_out=ep32[:, 8 + c8:9 + c8])
                I1(mm, p.s_se)
                v.wait_ge(p.s_se.h, p.s_se.n)
                for c8 in range(8):
                    v.reciprocal(ep32[:, 16 + c8:17 + c8],
                                 ep32[:, 8 + c8:9 + c8])
                for c8 in range(8):
                    v.tensor_scalar(out=pvt8[:, c8 * 64:(c8 + 1) * 64],
                                    in0=pvt8[:, c8 * 64:(c8 + 1) * 64],
                                    scalar1=ep32[:, 16 + c8:17 + c8],
                                    scalar2=None, op0=ALU.mult)
                vsep(p)
                I1(v.tensor_tensor(P2_tm[:, 0:512], pvt8[:],
                                   rz_h[hb][:, 0:512], ALU.mult), p.s_ve)
                ve_P2 = p.s_ve.n
                t.wait_ge(p.s_ve.h, ve_P2)
                for grp in range(2):
                    if p.pTb_free:
                        t.wait_ge(p.s_v2.h, p.pTb_free)
                    for c in range(4):
                        lt = grp * 4 + c
                        mm = t.transpose(pTb[0:64, c * 128:(c + 1) * 128],
                                         P2_tm[:, lt * 64:(lt + 1) * 64],
                                         p.ident[:])
                    I1(mm, p.s_pe)
                    v.wait_ge(p.s_pe.h, p.s_pe.n)
                    p.pTb_free = I1(
                        v.tensor_copy(P2_fm[0:64, grp * 512:(grp + 1) * 512],
                                      pTb[0:64, 0:512]), p.s_v2)
                    vsep(p)
                for tc in range(2):
                    tcg = half * 2 + tc
                    ntau = (tcg + 1) * 4
                    t.wait_ge(p.s_v2.h, p.pTb_free)
                    for tt in range(ntau):
                        pb = pWT[tt % 2]
                        fr = p.pWTfree.get(tt % 2, (0, 0))
                        if fr[0]:
                            t.wait_ge(p.s_se.h, fr[0])
                        if fr[1]:
                            t.wait_ge(p.s_ve.h, fr[1])
                        if p.pe_ov_last and tt == 0:
                            t.wait_ge(p.s_pe.h, p.pe_ov_last)
                        mm = t.matmul(pb[:], E_fm[hb][0:64, tt * 128:(tt + 1) * 128],
                                      P2_fm[0:64, tc * 512:(tc + 1) * 512],
                                      start=True, stop=True)
                        I1(mm, p.s_pe)
                        cs = tt - tcg * 4
                        if tt % 2 == 0:
                            s.wait_ge(p.s_pe.h, p.s_pe.n)
                            se_m = I1(
                                s.copy(WT_sb[:, tt * 512:(tt + 1) * 512],
                                       pb[:]), p.s_se)
                            marks = (se_m, 0)
                            dve_t, dve_n = p.s_se.h, se_m
                        else:
                            v.wait_ge(p.s_pe.h, p.s_pe.n)
                            ve_m = I1(v.tensor_copy(
                                WT_sb[:, tt * 512:(tt + 1) * 512], pb[:]),
                                p.s_ve)
                            marks = (0, ve_m)
                            dve_t, dve_n = p.s_ve.h, ve_m
                        if 0 <= cs < 4:
                            v.wait_ge(dve_t, dve_n)
                            ve_m = I1(v.tensor_tensor(
                                WT_sb[:, tt * 512 + cs * 128:
                                      tt * 512 + (cs + 1) * 128],
                                pb[:, cs * 128:(cs + 1) * 128],
                                p.utinc[:], ALU.mult), p.s_ve)
                            marks = (marks[0], ve_m)
                            vsep(p)
                        p.pWTfree[tt % 2] = marks
                        p.at_ev[tt] = marks
                    if p.ve_NO_read_last:
                        t.wait_ge(p.s_ve.h, p.ve_NO_read_last)
                    if p.se_NO_read_last:
                        t.wait_ge(p.s_se.h, p.se_NO_read_last)
                    for tt in range(ntau):
                        mk = p.at_ev[tt]
                        if mk[0]:
                            t.wait_ge(p.s_se.h, mk[0])
                        if mk[1]:
                            t.wait_ge(p.s_ve.h, mk[1])
                        for c in range(4):
                            jt = tcg * 4 + c
                            if tt <= jt:
                                mm = t.matmul(
                                    pNO[:, c * 128:(c + 1) * 128],
                                    WT_sb[:, tt * 512 + c * 128:
                                          tt * 512 + (c + 1) * 128],
                                    v_h[hb][:, tt * 128:(tt + 1) * 128],
                                    start=(tt == 0 and c == 0),
                                    stop=(tt == jt))
                    p.pe_ov_last = I1(mm, p.s_pe)
                    s.wait_ge(p.s_pe.h, p.pe_ov_last)
                    if p.ve_o2_last:
                        s.wait_ge(p.s_ve.h, p.ve_o2_last)
                    for c in range(4):
                        s.copy(o_bf4[:, c * 128:(c + 1) * 128],
                               pNO[:, c * 128:(c + 1) * 128])
                    for c in range(4):
                        mm = s.activation(jnk[:], pNO[:, c * 128:(c + 1) * 128],
                                          AF.Square,
                                          accum_out=ep32[:, 24 + c:25 + c])
                    p.se_NO_read_last = I1(mm, p.s_se)
                    for c in range(4):
                        s.activation(ep32[:, 28 + c:29 + c],
                                     ep32[:, 24 + c:25 + c], AF.Ln,
                                     bias=p.eps_c[:], scale=1.0 / BDV)
                    for c in range(4):
                        mm = s.activation(ep32[:, 32 + c:33 + c],
                                          ep32[:, 28 + c:29 + c], AF.Exp,
                                          bias=0.0, scale=-0.5)
                    I1(mm, p.s_se)
                    v.wait_ge(p.s_se.h, p.s_se.n)
                    for c in range(4):
                        v.tensor_scalar(out=o_bf4[:, c * 128:(c + 1) * 128],
                                        in0=o_bf4[:, c * 128:(c + 1) * 128],
                                        scalar1=ep32[:, 32 + c:33 + c],
                                        scalar2=None, op0=ALU.mult)
                    for c in range(4):
                        lt = tc * 4 + c
                        mm = v.tensor_tensor(
                            o2_4[:, c * 128:(c + 1) * 128],
                            o_bf4[:, c * 128:(c + 1) * 128],
                            gs_h[hb][:, lt * 128:(lt + 1) * 128], ALU.mult)
                    ve_o2 = I1(mm, p.s_ve)
                    p.ve_NO_read_last = ve_o2
                    p.ve_o2_last = ve_o2
                    t.wait_ge(p.s_ve.h, ve_o2)
                    if p.pTb_free:
                        t.wait_ge(p.s_v2.h, p.pTb_free)
                    for c in range(4):
                        mm = t.transpose(pTb[:, c * 128:(c + 1) * 128],
                                         o2_4[:, c * 128:(c + 1) * 128],
                                         p.ident[:])
                    p.pe_o2T = I1(mm, p.s_pe)
                    v.wait_ge(p.s_pe.h, p.pe_o2T)
                    p.pTb_free = I1(
                        v.tensor_copy(o2fm[:, (h * 8 + tc * 4) * 128:
                                           (h * 8 + tc * 4 + 4) * 128],
                                      pTb[:, 0:512]), p.s_v2)
                    vsep(p)
                hdone[hseq * BH + h] = p.s_pe.n

            ph1(0)
            for h in range(1, BH):
                ph1(h)
                ph2(h - 1)
            ph2(BH - 1)
            if first_wo:
                t.wait_ge(p.s_ve.h, wo_scaled)
                first_wo = False
            t.wait_ge(p.s_v2.h, p.s_v2.n)
            for par in (0, 1):
                for tc in (0, 1):
                    if p.num_rd.get((par, tc), 0):
                        t.wait_ge(p.s_ve.h, p.num_rd[(par, tc)])
            for lt in range(8):
                gt = hseq * 8 + lt
                xb = gt % 2
                sx = p.s_x0 if xb == 0 else p.s_x1
                if gt >= 2:
                    y.wait_ge(p.s_ve.h, ve_resid.get(gt - 2, 0))
                rowg = s0 * T + half * 1024 + lt * 128
                x_here = I16(y.dma_start(xt[xb][:],
                                         p.x2scr[rowg:rowg + 128, :]), sx)
                for dh in range(2):
                    if ve_resid.get((gt, dh - 1), 0):
                        t.wait_ge(p.s_ve.h, ve_resid[(gt, dh - 1)])
                    elif gt >= 1:
                        t.wait_ge(p.s_ve.h, ve_resid.get((gt - 1, 1), 0))
                    for dc in range(2):
                        for j16 in range(16):
                            mm = t.matmul(
                                pWo4[:, dc * 512:(dc + 1) * 512],
                                o2fm[:, (j16 * 8 + lt) * 128:
                                     (j16 * 8 + lt + 1) * 128],
                                bWo_sb[:, j16 * 2048 + (dh * 2 + dc) * 512:
                                       j16 * 2048 + (dh * 2 + dc + 1) * 512],
                                start=(j16 == 0), stop=(j16 == 15))
                    pe_wo = I1(mm, p.s_pe)
                    v.wait_ge(sx.h, x_here)
                    v.wait_ge(p.s_pe.h, pe_wo)
                    if dh == 0 and st_x3.get(gt - 1, 0):
                        v.wait_ge(p.s_st.h, st_x3[gt - 1])
                    ve_resid[(gt, dh)] = I1(
                        v.tensor_tensor(x3stg[:, dh * 1024:(dh + 1) * 1024],
                                        pWo4[:],
                                        xt[xb][:, dh * 1024:(dh + 1) * 1024],
                                        ALU.add), p.s_ve)
                ve_resid[gt] = ve_resid[(gt, 1)]
                y.wait_ge(p.s_ve.h, ve_resid[gt])
                p.wo_resid_last = ve_resid[gt]
                st_x3[gt] = I16(y.dma_start(p.x3scr[rowg:rowg + 128, :],
                                            x3stg[:]), p.s_st)
    y.wait_ge(p.s_st.h, p.s_st.n)
    g.wait_ge(p.s_w.h, p.s_w.n)


def emit_B3(p):
    emit_ffn(p, p.x3scr, p.out, tag="b")



def _host_constants():
    hd = HD
    inv = 1.0 / (10000.0 ** (np.arange(0, hd, 2, dtype=np.float64) / hd))
    pos = np.arange(T, dtype=np.float64)
    fr = pos[:, None] * inv[None, :]
    cos = np.cos(fr).astype(np.float32)
    sin = np.sin(fr).astype(np.float32)
    coscat = np.ascontiguousarray(np.concatenate([cos, cos], axis=1))
    sincat = np.ascontiguousarray(np.concatenate([-sin, sin], axis=1))
    ident = np.eye(128, dtype=np.float32)
    utinc = np.triu(np.ones((128, 128), np.float32))
    return coscat, sincat, ident, utinc


def make_in_maps(inp):
    coscat, sincat, ident, utinc = _host_constants()
    W = {}
    W["aWq"] = np.ascontiguousarray(np.asarray(inp["L0_attn_a_Wq"]).T)
    W["aWk"] = np.ascontiguousarray(np.asarray(inp["L0_attn_a_Wk"]).T)
    W["aWv"] = np.ascontiguousarray(np.asarray(inp["L0_attn_a_Wv"]).T)
    W["aWo"] = np.ascontiguousarray(np.asarray(inp["L0_attn_a_Wo"]).T)
    W["coscat"] = coscat
    W["sincat"] = sincat
    W["identi"] = ident
    W["utinci"] = utinc
    W["fW1"] = np.ascontiguousarray(np.asarray(inp["L0_ffn_W1"]))
    W["fb1c"] = np.ascontiguousarray(
        np.asarray(inp["L0_ffn_b1"]).reshape(16, 128).T)
    W["fW2"] = np.ascontiguousarray(np.asarray(inp["L0_ffn_W2"]))
    W["fb2"] = np.ascontiguousarray(np.asarray(inp["L0_ffn_b2"]).reshape(1, 8))
    W["fV"] = np.ascontiguousarray(
        np.asarray(inp["L0_ffn_V"]).transpose(2, 0, 1).reshape(D, 512))
    W["fU"] = np.ascontiguousarray(
        np.asarray(inp["L0_ffn_U"]).transpose(0, 2, 1).reshape(512, D))
    for k, nm in [("bWq", "L0_attn_b_Wq"), ("bWk", "L0_attn_b_Wk"),
                  ("bWs", "L0_attn_b_Ws"), ("bWv", "L0_attn_b_Wv"),
                  ("bWg", "L0_attn_b_Wg"), ("bWo", "L0_attn_b_Wo")]:
        W[k] = np.ascontiguousarray(np.asarray(inp[nm]))
    W["gnw"] = np.ascontiguousarray(
        np.asarray(inp["L0_attn_b_g_norm_weight"]).reshape(1, BDV))
    x = np.asarray(inp["L0_attn_a_hidden_states"], dtype=np.float32)
    in_maps = []
    for c in range(N_CORES):
        m = dict(W)
        m["x0"] = np.ascontiguousarray(x[2 * c:2 * c + 2].reshape(SEQ * T, D))
        in_maps.append(m)
    return in_maps


def kernel(**inp):
    in_maps = make_in_maps(inp)
    nc = build_program()
    res = run_bass_kernel_spmd(nc, in_maps, core_ids=list(range(N_CORES)))
    out = np.concatenate(
        [np.asarray(r["out"], np.float32).reshape(SEQ, T, D)
         for r in res.results], axis=0)
    return out


# version 20
# speedup vs baseline: 1.0918x; 1.0278x over previous
import numpy as np
from contextlib import ExitStack

import concourse.bass as bass
import concourse.mybir as mybir
from concourse.bass_utils import run_bass_kernel_spmd
from concourse.alu_op_type import AluOpType as ALU

F32 = mybir.dt.float32
BF16 = mybir.dt.bfloat16
AF = mybir.ActivationFunctionType
AX = mybir.AxisListType.X

N_CORES = 8
B = 16
T = 2048
D = 2048
SEQ = 2
NT = 16
NTILES = SEQ * NT
NH, HD, NKV = 16, 128, 4
BH, BDK, BDV, BM = 16, 64, 128, 64
EPS = 1e-5
QK_SCALE = 1.0 / np.sqrt(128.0)
BQ_SCALE = 1.0 / np.sqrt(64.0)


class Sem:
    def __init__(self, h):
        self.h = h
        self.n = 0


def I1(instr, sem):
    instr.then_inc(sem.h, 1)
    sem.n += 1
    return sem.n


def I16(instr, sem):
    instr.then_inc(sem.h, 16)
    sem.n += 16
    return sem.n


def vsep(p):
    p.v.nop(cycle_cnt=400)


class P:
    pass


def _bar(p):
    p.barn += 1
    tgt = 5 * p.barn
    for e in [p.t, p.v, p.s, p.g, p.y]:
        e.drain().then_inc(p.bar.h, 1)
        e.wait_ge(p.bar.h, tgt)


def _ln(p, xt_ap, sqj_ap, tmp8, xn_out, store_col=None):
    v, s = p.v, p.s
    I1(v.reduce_sum(tmp8[:, 0:1], xt_ap, axis=AX), p.s_ve)
    I1(s.activation(sqj_ap, xt_ap, AF.Square, accum_out=tmp8[:, 1:2]), p.s_se)
    v.wait_ge(p.s_se.h, p.s_se.n)
    vsep(p)
    v.tensor_scalar(out=tmp8[:, 2:3], in0=tmp8[:, 0:1], scalar1=1.0 / D,
                    scalar2=None, op0=ALU.mult)
    vsep(p)
    v.tensor_tensor(tmp8[:, 3:4], tmp8[:, 2:3], tmp8[:, 2:3], ALU.mult)
    vsep(p)
    I1(v.tensor_scalar(out=tmp8[:, 4:5], in0=tmp8[:, 1:2], scalar1=1.0 / D,
                       scalar2=tmp8[:, 3:4], op0=ALU.mult, op1=ALU.subtract),
       p.s_ve)
    s.wait_ge(p.s_ve.h, p.s_ve.n)
    s.activation(tmp8[:, 5:6], tmp8[:, 4:5], AF.Ln, bias=p.eps_c[:], scale=1.0)
    I1(s.activation(tmp8[:, 6:7], tmp8[:, 5:6], AF.Exp, bias=0.0, scale=-0.5),
       p.s_se)
    v.wait_ge(p.s_se.h, p.s_se.n)
    v.tensor_scalar(out=tmp8[:, 7:8], in0=tmp8[:, 4:5], scalar1=p.eps_c[:],
                    scalar2=None, op0=ALU.add)
    v.tensor_tensor(tmp8[:, 0:1], tmp8[:, 6:7], tmp8[:, 6:7], ALU.mult)
    vsep(p)
    v.tensor_tensor(tmp8[:, 3:4], tmp8[:, 0:1], tmp8[:, 7:8], ALU.mult)
    vsep(p)
    v.tensor_scalar(out=tmp8[:, 0:1], in0=tmp8[:, 3:4], scalar1=-0.5,
                    scalar2=1.5, op0=ALU.mult, op1=ALU.add)
    vsep(p)
    I1(v.tensor_tensor(tmp8[:, 5:6], tmp8[:, 6:7], tmp8[:, 0:1], ALU.mult),
       p.s_ve)
    if store_col is not None:
        s.wait_ge(p.s_ve.h, p.s_ve.n)
        s.copy(p.stats[:, 2 * store_col:2 * store_col + 1], tmp8[:, 2:3])
        I1(s.copy(p.stats[:, 2 * store_col + 1:2 * store_col + 2], tmp8[:, 5:6]),
           p.s_se)
        v.wait_ge(p.s_se.h, p.s_se.n)
    vsep(p)
    I1(v.tensor_scalar(out=xn_out, in0=xt_ap, scalar1=tmp8[:, 2:3],
                       scalar2=tmp8[:, 5:6], op0=ALU.subtract, op1=ALU.mult),
       p.s_ve)
    return p.s_ve.n


def _xnT(p, xn_bf, xnT_sb, pT, dst_map=None):
    t, v = p.t, p.v
    for grp in range(4):
        pb = pT[grp % 2]
        key = id(pb)
        last = p.pT_free.get(key, 0)
        if last:
            t.wait_ge(p.s_v2.h, last)
        for c in range(4):
            cc = grp * 4 + c
            mm = t.transpose(pb[:, c * 128:(c + 1) * 128],
                             xn_bf[:, cc * 128:(cc + 1) * 128], p.ident[:])
        I1(mm, p.s_pe)
        v.wait_ge(p.s_pe.h, p.s_pe.n)
        dst = dst_map(grp) if dst_map else xnT_sb[:, grp * 512:(grp + 1) * 512]
        p.pT_free[key] = I1(v.tensor_copy(dst, pb[:]), p.s_v2)
    t.wait_ge(p.s_v2.h, p.s_v2.n)


def build_program(tap=None):
    nc = bass.Bass("TRN2", target_bir_lowering=False)
    p = P()
    p.nc = nc
    p.tap = tap
    p.pT_free = {}

    def din(name, shape, dt=F32):
        return nc.dram_tensor(name, shape, dt, kind="ExternalInput").ap()

    def dscr(name, shape, dt, tapped=False):
        kind = "ExternalOutput" if (tapped and tap is not None) else "Internal"
        return nc.dram_tensor(name, shape, dt, kind=kind).ap()

    p.x0 = din("x0", [SEQ * T, D])
    p.aWq = din("aWq", [D, NH * HD])
    p.aWk = din("aWk", [D, NKV * HD])
    p.aWv = din("aWv", [D, NKV * HD])
    p.aWo = din("aWo", [NH * HD, D])
    p.coscat = din("coscat", [T, HD])
    p.sincat = din("sincat", [T, HD])
    p.fW1 = din("fW1", [D, D])
    p.fb1c = din("fb1c", [128, 16])
    p.fW2 = din("fW2", [D, 8])
    p.fb2 = din("fb2", [1, 8])
    p.fV = din("fV", [D, 512])
    p.fU = din("fU", [512, D])
    p.bWq = din("bWq", [D, BH * BDK])
    p.bWk = din("bWk", [D, BH * BDK])
    p.bWs = din("bWs", [D, BH * BM])
    p.bWv = din("bWv", [D, BH * BDV])
    p.bWg = din("bWg", [D, BH * BDV])
    p.bWo = din("bWo", [BH * BDV, D])
    p.gnw = din("gnw", [1, BDV])
    p.identi = din("identi", [128, 128])
    p.utinci = din("utinci", [128, 128])
    p.out = nc.dram_tensor("out", [SEQ * T, D], F32, kind="ExternalOutput").ap()

    tl = tap or "zz"
    p.qscr = dscr("qscr", [SEQ, T, NH * HD], BF16, tl == "A1")
    p.x1scr = dscr("x1scr", [SEQ * T, D], F32, tl == "A2")
    p.x2scr = dscr("x2scr", [SEQ * T, D], F32, tl == "A3")
    p.qbscr = dscr("qbscr", [SEQ, 8, 128, T], BF16, tl == "B1a")
    p.kbscr = dscr("kbscr", [SEQ, 8, 128, T], BF16, tl == "B1a")
    p.escr = dscr("escr", [SEQ, T, BH * BM], BF16, tl == "B1a")
    p.rzscr = dscr("rzscr", [SEQ, T, BH * BM], BF16, tl == "B1a")
    p.vbscr = dscr("vbscr", [SEQ, T, BH * BDV], BF16, tl == "B1b")
    p.gbscr = dscr("gbscr", [SEQ, T, BH * BDV], BF16, tl == "B1b")
    p.x3scr = dscr("x3scr", [SEQ * T, D], F32, tl == "B2")
    if tap in ("A1", "B1a"):
        p.stats_tap = nc.dram_tensor("stats_tap", [128, 64], F32,
                                     kind="ExternalOutput").ap()
    if tap == "A1":
        p.kfm_tap = nc.dram_tensor("kfm_tap", [128, SEQ * NKV * T], BF16,
                                   kind="ExternalOutput").ap()
        p.vtm_tap = nc.dram_tensor("vtm_tap", [128, SEQ * NT * 512], BF16,
                                   kind="ExternalOutput").ap()

    es = ExitStack()
    with es:
        def sbuf(name, shape, dt=F32):
            return es.enter_context(nc.sbuf_tensor(name, shape, dt))

        def sem(name):
            return Sem(es.enter_context(nc.semaphore(name)))

        p.ident = sbuf("ident", [128, 128], BF16)
        p.utinc = sbuf("utinc", [128, 128], BF16)
        p.eps_c = sbuf("eps_c", [128, 1])
        p.n32_c = sbuf("n32_c", [128, 1])
        p.stats = sbuf("stats", [128, 64])
        p.b2b = sbuf("b2b", [128, 8])
        p.b2row = sbuf("b2row", [1, 8])
        p.ones1 = sbuf("ones1", [1, 128])
        p.gnwcol = sbuf("gnwcol", [128, 1])
        p.gnwrow = sbuf("gnwrow", [1, BDV])

        p.bar = sem("bar")
        p.barn = 0
        for nm in ["s_w", "s_x", "s_st", "s_pe", "s_ve", "s_se", "s_ge",
                   "s_p2", "s_v2", "s_v3", "s_s2", "s_x0", "s_x1",
                   "s_tg0", "s_tg1", "s_q0", "s_q1"]:
            setattr(p, nm, sem(nm))

        with nc.Block() as block:
            p.t, p.v, p.s, p.g, p.y = nc.tensor, nc.vector, nc.scalar, \
                nc.gpsimd, nc.sync

            emit_init(p)
            _bar(p)
            stages = [("A1", emit_A1), ("A2", emit_A2), ("A3", emit_A3),
                      ("B1a", emit_B1a), ("B1b", emit_B1b), ("B2", emit_B2),
                      ("B3", emit_B3)]
            for name, f in stages:
                with ExitStack() as ses:
                    p.ses = ses
                    f(p)
                _bar(p)
                if tap == name:
                    break
        if getattr(p, "esKV", None) is not None:
            p.esKV.close()
            p.esKV = None
    return nc


def emit_init(p):
    g, v, s, t, y = p.g, p.v, p.s, p.t, p.y
    I16(g.dma_start(out=p.ident[:], in_=p.identi), p.s_w)
    I16(g.dma_start(out=p.utinc[:], in_=p.utinci), p.s_w)
    g.memset(p.eps_c[:], EPS)
    I1(g.memset(p.n32_c[:], -32.0), p.s_ge)
    I16(y.dma_start(p.b2row[:], p.fb2), p.s_x)
    I16(y.dma_start(p.gnwrow[:], p.gnw), p.s_x)
    I1(v.memset(p.ones1[:], 1.0), p.s_ve)
    with p.nc.psum_tensor("pini", [128, 8], F32) as pini, \
         p.nc.psum_tensor("pini2", [128, 1], F32) as pini2:
        t.wait_ge(p.s_x.h, p.s_x.n)
        t.wait_ge(p.s_ve.h, p.s_ve.n)
        I1(t.matmul(pini[:], p.ones1[:], p.b2row[:], start=True, stop=True),
           p.s_pe)
        I1(t.matmul(pini2[:], p.gnwrow[:], p.ones1[:, 0:1],
                    start=True, stop=True), p.s_pe)
        v.wait_ge(p.s_pe.h, p.s_pe.n)
        v.tensor_copy(p.b2b[:], pini[:])
        I1(v.tensor_copy(p.gnwcol[:], pini2[:]), p.s_ve)
    g.wait_ge(p.s_ge.h, p.s_ge.n)
    s.wait_ge(p.s_ge.h, p.s_ge.n)
    s.activation(p.stats[:, 0:1], p.eps_c[:], AF.Square)
    s.activation(p.stats[:, 0:1], p.eps_c[:], AF.Ln)
    s.activation(p.stats[:, 0:1], p.eps_c[:], AF.Exp)
    I1(s.activation(p.stats[:, 0:1], p.eps_c[:], AF.Relu), p.s_se)
    s.wait_ge(p.s_se.h, p.s_se.n)
    g.wait_ge(p.s_w.h, p.s_w.n)
    y.wait_ge(p.s_ve.h, p.s_ve.n)


def emit_A1(p):
    nc = p.nc
    sb = p.ses.enter_context
    p.esKV = ExitStack()
    p.K_sb = p.esKV.enter_context(
        nc.sbuf_tensor("K_sb", [128, SEQ * NKV * T], BF16, side="right"))
    p.V_sb = p.esKV.enter_context(
        nc.sbuf_tensor("V_sb", [128, SEQ * NT * 512], BF16, side="right"))

    aWq_sb = sb(nc.sbuf_tensor("aWq_sb", [128, 16 * 2048], BF16))
    aWk_sb = sb(nc.sbuf_tensor("aWk_sb", [128, 16 * 512], BF16))
    aWv_sb = sb(nc.sbuf_tensor("aWv_sb", [128, 16 * 512], BF16))
    cos_sb = sb(nc.sbuf_tensor("cos_sb", [128, 16 * 128], BF16))
    sin_sb = sb(nc.sbuf_tensor("sin_sb", [128, 16 * 128], BF16))
    xt = [sb(nc.sbuf_tensor(f"xtA{i}", [128, D], F32)) for i in range(2)]
    ropeA = sb(nc.sbuf_tensor("ropeA", [128, 128], F32))
    ropeB = sb(nc.sbuf_tensor("ropeB", [128, 128], F32))
    xn_bf = sb(nc.sbuf_tensor("xn_bfA", [128, D], BF16))
    xnT = sb(nc.sbuf_tensor("xnTA", [128, D], BF16))
    qrot = sb(nc.sbuf_tensor("qrotA", [128, NH * HD], BF16))
    krot = sb(nc.sbuf_tensor("krotA", [128, NKV * HD], BF16))
    tmp8 = sb(nc.sbuf_tensor("tmp8A", [128, 8], F32))
    pQ = sb(nc.psum_tensor("pQ", [128, 2048], F32))
    pKV = sb(nc.psum_tensor("pKV", [128, 1024], F32))
    pT = [sb(nc.psum_tensor(f"pTA{i}", [128, 512], BF16)) for i in range(2)]
    p.pT_free = {}

    g, v, s, t, y = p.g, p.v, p.s, p.t, p.y

    for c in range(16):
        I16(g.dma_start(out=aWq_sb[:, c * 2048:(c + 1) * 2048],
                        in_=p.aWq[c * 128:(c + 1) * 128, :]), p.s_w)
        I16(g.dma_start(out=aWk_sb[:, c * 512:(c + 1) * 512],
                        in_=p.aWk[c * 128:(c + 1) * 128, :]), p.s_w)
        I16(g.dma_start(out=aWv_sb[:, c * 512:(c + 1) * 512],
                        in_=p.aWv[c * 128:(c + 1) * 128, :]), p.s_w)
    w_done = p.s_w.n

    I16(g.dma_start(out=cos_sb[:],
                    in_=p.coscat.rearrange("(n p) m -> p n m", p=128)), p.s_w)
    I16(g.dma_start(out=sin_sb[:],
                    in_=p.sincat.rearrange("(n p) m -> p n m", p=128)), p.s_w)
    trig_done = p.s_w.n
    ve_apply = {}
    st_q = {}
    for j in range(NTILES):
        sq, jj = divmod(j, NT)
        buf = j % 2
        sx = p.s_x0 if buf == 0 else p.s_x1
        if j >= 2:
            y.wait_ge(p.s_ve.h, ve_apply[j - 2])
        x_here = I16(y.dma_start(xt[buf][:], p.x0[j * 128:(j + 1) * 128, :]), sx)
        v.wait_ge(sx.h, x_here)
        s.wait_ge(sx.h, x_here)
        if j >= 1:
            v.wait_ge(p.s_pe.h, p.pe_xn_done)
            s.wait_ge(p.s_pe.h, p.pe_xn_done)
        ve_apply[j] = _ln(p, xt[buf][:], xn_bf[:], tmp8, xn_bf[:],
                          store_col=j)

        if j == 0:
            t.wait_ge(p.s_w.h, w_done)
        t.wait_ge(p.s_ve.h, ve_apply[j])
        _xnT(p, xn_bf, xnT, pT)
        p.pe_xn_done = p.s_pe.n
        if j >= 1:
            t.wait_ge(p.s_ve.h, p.ve_kv_done)
        for d in range(16):
            t.matmul(pKV[:, 0:512], xnT[:, d * 128:(d + 1) * 128],
                     aWk_sb[:, d * 512:(d + 1) * 512], start=(d == 0),
                     stop=(d == 15))
        for d in range(16):
            mm = t.matmul(pKV[:, 512:1024], xnT[:, d * 128:(d + 1) * 128],
                          aWv_sb[:, d * 512:(d + 1) * 512], start=(d == 0),
                          stop=(d == 15))
        pe_kv = I1(mm, p.s_pe)
        if j >= 1:
            t.wait_ge(p.s_ve.h, p.ve_q_done)
        for jc in range(4):
            for d in range(16):
                mm = t.matmul(
                    pQ[:, jc * 512:(jc + 1) * 512],
                    xnT[:, d * 128:(d + 1) * 128],
                    aWq_sb[:, d * 2048 + jc * 512:d * 2048 + (jc + 1) * 512],
                    start=(d == 0), stop=(d == 15))
        pe_q = I1(mm, p.s_pe)

        if j == 0:
            v.wait_ge(p.s_w.h, trig_done)
        v.wait_ge(p.s_pe.h, pe_kv)
        ct = cos_sb[:, jj * 128:(jj + 1) * 128]
        sn = sin_sb[:, jj * 128:(jj + 1) * 128]
        v.tensor_copy(p.V_sb[:, j * 512:(j + 1) * 512], pKV[:, 512:1024])
        for h in range(NKV):
            v.tensor_tensor(ropeB[:, 0:64], pKV[:, h * 128 + 64:(h + 1) * 128],
                            sn[:, 0:64], ALU.mult)
            v.tensor_tensor(ropeB[:, 64:128], pKV[:, h * 128:h * 128 + 64],
                            sn[:, 64:128], ALU.mult)
            v.tensor_tensor(ropeA[:], pKV[:, h * 128:(h + 1) * 128], ct, ALU.mult)
            mm = v.tensor_tensor(krot[:, h * 128:(h + 1) * 128], ropeA[:],
                                 ropeB[:], ALU.add)
        p.ve_kv_done = I1(mm, p.s_ve)
        v.wait_ge(p.s_pe.h, pe_q)
        if j >= 1:
            v.wait_ge(p.s_st.h, st_q[j - 1])
        for h in range(NH):
            v.tensor_tensor(ropeB[:, 0:64], pQ[:, h * 128 + 64:(h + 1) * 128],
                            sn[:, 0:64], ALU.mult)
            v.tensor_tensor(ropeB[:, 64:128], pQ[:, h * 128:h * 128 + 64],
                            sn[:, 64:128], ALU.mult)
            v.tensor_tensor(ropeA[:], pQ[:, h * 128:(h + 1) * 128], ct, ALU.mult)
            mm = v.tensor_tensor(qrot[:, h * 128:(h + 1) * 128], ropeA[:],
                                 ropeB[:], ALU.add)
        p.ve_q_done = I1(mm, p.s_ve)
        ve_q = p.s_ve.n

        t.wait_ge(p.s_ve.h, p.ve_kv_done)
        pb = pT[0]
        last = p.pT_free.get(id(pb), 0)
        if last:
            t.wait_ge(p.s_v2.h, last)
        for hh in range(NKV):
            mm = t.transpose(pb[:, hh * 128:(hh + 1) * 128],
                             krot[:, hh * 128:(hh + 1) * 128], p.ident[:])
        I1(mm, p.s_pe)
        v.wait_ge(p.s_pe.h, p.s_pe.n)
        kdst = p.K_sb[:].rearrange("p (a b) -> p a b", a=SEQ * NKV)[
            :, sq * NKV:(sq + 1) * NKV, jj * 128:(jj + 1) * 128]
        p.pT_free[id(pb)] = I1(
            v.tensor_copy(kdst, pb[:].rearrange("p (a b) -> p a b", a=4)),
            p.s_v2)

        y.wait_ge(p.s_ve.h, ve_q)
        st_q[j] = I16(y.dma_start(p.qscr[sq, jj * 128:(jj + 1) * 128, :],
                                  qrot[:]), p.s_st)

    y.wait_ge(p.s_st.h, p.s_st.n)
    g.wait_ge(p.s_w.h, p.s_w.n)
    if p.tap == "A1":
        y.wait_ge(p.s_v2.h, p.s_v2.n)
        y.wait_ge(p.s_se.h, p.s_se.n)
        I16(y.dma_start(p.stats_tap, p.stats[:]), p.s_st)
        I16(y.dma_start(p.kfm_tap, p.K_sb[:]), p.s_st)
        I16(y.dma_start(p.vtm_tap, p.V_sb[:]), p.s_st)
        y.wait_ge(p.s_st.h, p.s_st.n)


def emit_A2(p):
    nc = p.nc
    sb = p.ses.enter_context
    aWo_sb = sb(nc.sbuf_tensor("aWo_sb", [128, 16 * 2048], BF16))
    qld = [sb(nc.sbuf_tensor(f"qldA2{i}", [128, 512], BF16)) for i in range(2)]
    qfm = sb(nc.sbuf_tensor("qfmA2", [128, 128], BF16))
    pexp2 = [sb(nc.sbuf_tensor(f"pexpA2{i}", [128, 2048], BF16))
             for i in range(2)]
    diag2 = [sb(nc.sbuf_tensor(f"diagA2{i}", [128, 128], BF16))
             for i in range(2)]
    PT_sb = sb(nc.sbuf_tensor("PTA2", [128, 16 * 512], BF16))
    O_sb = sb(nc.sbuf_tensor("OA2", [128, 16 * 128], BF16))
    xt = [sb(nc.sbuf_tensor(f"xtA2{i}", [128, D], F32)) for i in range(2)]
    x1stg = sb(nc.sbuf_tensor("x1stgA2", [128, D], F32))
    lrc = sb(nc.sbuf_tensor("lrcA2", [128, 12], F32))
    pBig = sb(nc.psum_tensor("pBigA2", [128, 2048], F32))
    pTT = [sb(nc.psum_tensor(f"pTTA2{i}", [128, 512], F32)) for i in range(2)]
    pO = sb(nc.psum_tensor("pOA2", [128, 512], F32))
    pTq = sb(nc.psum_tensor("pTqA2", [128, 128], BF16))

    g, v, s, t, y = p.g, p.v, p.s, p.t, p.y

    for c in range(16):
        I16(g.dma_start(out=aWo_sb[:, c * 2048:(c + 1) * 2048],
                        in_=p.aWo[c * 128:(c + 1) * 128, :]), p.s_w)
    wo_done = p.s_w.n

    qtr_done = {}
    p.q_here = {}
    p.pe_S_last = 0
    p.se_qfm_last = 0
    p.se_sevac_last = 0
    p.se_exp_last = 0
    p.ve_max_last = 0
    p.ve_norm_last = 0
    p.norm_done = {}
    p.exp_done = {}
    p.last_unit = None
    p.pe_PT = {}
    p.head_norm = {}
    p.pt_ready = {}
    p.ve_O_last = 0
    p.ve_resid_last = 0
    ve_resid = {}
    pTT_free = {}
    st_x1 = {}
    first_wo = True

    for s0 in range(SEQ):
        for qt in range(NT):
            gidx = (s0 * NT + qt)
            def issue_qload(qj):
                if qj in p.q_here or qj >= 4 * SEQ * NT:
                    return
                gg2, gidx2 = qj % 4, qj // 4
                qt2, s02 = gidx2 % NT, gidx2 // NT
                sq2 = p.s_q0 if qj % 2 == 0 else p.s_q1
                if qj >= 2:
                    y.wait_ge(p.s_pe.h, qtr_done[qj - 2])
                p.q_here[qj] = I16(y.dma_start(
                    qld[qj % 2][:],
                    p.qscr[s02, qt2 * 128:(qt2 + 1) * 128,
                           4 * gg2 * 128:(4 * gg2 + 4) * 128]), sq2)

            def emitP(gg, hl):
                hu = gg * 4 + hl
                pexp = pexp2[hu % 2]
                qi = gidx * 4 + gg
                qbuf = qi % 2
                sq = p.s_q0 if qbuf == 0 else p.s_q1
                if hl == 0:
                    issue_qload(qi)
                    issue_qload(qi + 1)
                t.wait_ge(sq.h, p.q_here[qi])
                if p.se_qfm_last:
                    t.wait_ge(p.s_s2.h, p.se_qfm_last)
                I1(t.transpose(pTq[:], qld[qbuf][:, hl * 128:(hl + 1) * 128],
                               p.ident[:]), p.s_pe)
                if hl == 3:
                    qtr_done[qi] = p.s_pe.n
                s.wait_ge(p.s_pe.h, p.s_pe.n)
                if p.pe_S_last:
                    s.wait_ge(p.s_pe.h, p.pe_S_last)
                I1(s.activation(qfm[:], pTq[:], AF.Copy, bias=0.0,
                                scale=float(QK_SCALE)), p.s_s2)
                p.se_qfm_last = p.s_s2.n
                t.wait_ge(p.s_s2.h, p.se_qfm_last)
                if p.se_exp_last:
                    t.wait_ge(p.s_se.h, p.se_exp_last)
                    t.wait_ge(p.s_ve.h, p.ve_max_last)
                if gidx > 0 and gg == 0 and hl == 0:
                    t.wait_ge(p.s_ve.h, p.ve_resid_last)
                kbase = (s0 * NKV + gg) * 2048
                p.pe_S_half = []
                for jc in range(4):
                    mm = t.matmul(pBig[:, jc * 512:(jc + 1) * 512], qfm[:],
                                  p.K_sb[:, kbase + jc * 512:kbase + (jc + 1) * 512],
                                  start=True, stop=True)
                    if jc % 2 == 1:
                        p.pe_S_half.append(I1(mm, p.s_pe))
                p.pe_S_last = p.pe_S_half[-1]

            def emitP2(gg, hl):
                hu = gg * 4 + hl
                pexp = pexp2[hu % 2]
                c_hi, c_lo, c_ne = (2, 3, 8) if hu % 2 == 0 else (6, 7, 9)
                v.wait_ge(p.s_pe.h, p.pe_S_half[0])
                v.reduce_max(lrc[:, c_lo:c_lo + 1], pBig[:, 0:1024],
                             axis=AX, negate=True)
                v.wait_ge(p.s_pe.h, p.pe_S_half[1])
                I1(v.reduce_max(lrc[:, c_hi:c_hi + 1], pBig[:, 1024:2048],
                                axis=AX, negate=True), p.s_ve)
                vsep(p)
                p.ve_max_last = I1(
                    v.tensor_tensor(lrc[:, c_ne:c_ne + 1],
                                    lrc[:, c_hi:c_hi + 1],
                                    lrc[:, c_lo:c_lo + 1], ALU.min), p.s_ve)
                s.wait_ge(p.s_ve.h, p.ve_max_last)
                if p.pe_PT.get(hu % 2, 0):
                    s.wait_ge(p.s_pe.h, p.pe_PT[hu % 2])
                for hc in range(2):
                    mm = s.activation(pexp[:, hc * 1024:(hc + 1) * 1024],
                                      pBig[:, hc * 1024:(hc + 1) * 1024],
                                      AF.Exp, bias=lrc[:, c_ne:c_ne + 1],
                                      scale=1.0, accum_out=lrc[:, hc:hc + 1])
                p.se_exp_last = I1(mm, p.s_se)
                p.exp_done[(gg, hl)] = p.se_exp_last
                v.wait_ge(p.s_se.h, p.se_exp_last)
                v.tensor_tensor(lrc[:, 4:5], lrc[:, 0:1], lrc[:, 1:2], ALU.add)
                vsep(p)
                v.reciprocal(lrc[:, 5:6], lrc[:, 4:5])
                vsep(p)
                if p.pe_PT.get(hu % 2, 0):
                    v.wait_ge(p.s_pe.h, p.pe_PT[hu % 2])
                p.ve_norm_last = I1(v.tensor_scalar(
                    out=diag2[hu % 2][:], in0=p.ident[:], scalar1=lrc[:, 5:6],
                    scalar2=None, op0=ALU.mult), p.s_ve)
                vsep(p)
                p.norm_done[hu % 2] = p.ve_norm_last
                p.head_norm[(gg, hl)] = p.ve_norm_last

            def emitT(gg, hl):
                hu = gg * 4 + hl
                pexp = pexp2[hu % 2]
                dg = diag2[hu % 2]
                t.wait_ge(p.s_ve.h, p.head_norm[(gg, hl)])
                t.wait_ge(p.s_se.h, p.exp_done[(gg, hl)])
                for grp in range(4):
                    pb = pTT[grp % 2]
                    last = pTT_free.get(grp % 2, 0)
                    if last:
                        t.wait_ge(p.s_v2.h, last)
                    for c in range(4):
                        kt = grp * 4 + c
                        mm = t.matmul(pb[:, c * 128:(c + 1) * 128],
                                      pexp[:, kt * 128:(kt + 1) * 128],
                                      dg[:], start=True, stop=True)
                    I1(mm, p.s_pe)
                    if grp == 3:
                        p.pe_PT[hu % 2] = p.s_pe.n
                    dst = PT_sb[:].rearrange("p (a b) -> p a b", a=16)[
                        :, grp * 4:(grp + 1) * 4, hl * 128:(hl + 1) * 128]
                    src = pb[:].rearrange("p (a b) -> p a b", a=4)
                    s.wait_ge(p.s_pe.h, p.s_pe.n)
                    ec = s.copy(dst, src)
                    ec.then_inc(p.s_v2.h, 1)
                    p.s_v2.n += 1
                    pTT_free[grp % 2] = p.s_v2.n
                p.pt_ready[(gg, hl)] = p.s_v2.n

            def emitPV(gg):
                t.wait_ge(p.s_v2.h, p.pt_ready[(gg, 3)])
                if p.ve_O_last:
                    t.wait_ge(p.s_ve.h, p.ve_O_last)
                vbase = s0 * NT
                for kt in range(16):
                    mm = t.matmul(pO[:],
                                  p.V_sb[:, (vbase + kt) * 512 + gg * 128:
                                         (vbase + kt) * 512 + (gg + 1) * 128],
                                  PT_sb[:, kt * 512:(kt + 1) * 512],
                                  start=(kt == 0), stop=(kt == 15))
                I1(mm, p.s_pe)
                v.wait_ge(p.s_pe.h, p.s_pe.n)
                p.ve_O_last = I1(
                    v.tensor_copy(O_sb[:, gg * 512:(gg + 1) * 512], pO[:]),
                    p.s_ve)
                vsep(p)

            units = [(gg, hl) for gg in range(4) for hl in range(4)]
            emitP(*units[0])
            emitP2(*units[0])
            for i in range(1, 16):
                emitP(*units[i])
                emitT(*units[i - 1])
                if units[i - 1][1] == 3:
                    emitPV(units[i - 1][0])
                emitP2(*units[i])
            emitT(*units[15])
            emitPV(3)
            if first_wo:
                t.wait_ge(p.s_w.h, wo_done)
                first_wo = False
            t.wait_ge(p.s_ve.h, p.ve_O_last)
            t.wait_ge(p.s_se.h, p.se_exp_last)
            t.wait_ge(p.s_ve.h, p.ve_max_last)
            for dc in range(4):
                for j16 in range(16):
                    mm = t.matmul(
                        pBig[:, dc * 512:(dc + 1) * 512],
                        O_sb[:, j16 * 128:(j16 + 1) * 128],
                        aWo_sb[:, j16 * 2048 + dc * 512:j16 * 2048 + (dc + 1) * 512],
                        start=(j16 == 0), stop=(j16 == 15))
            pe_wo = I1(mm, p.s_pe)
            xbuf = gidx % 2
            sx = p.s_x0 if xbuf == 0 else p.s_x1
            if gidx >= 2:
                y.wait_ge(p.s_ve.h, ve_resid[gidx - 2])
            x_here = I16(y.dma_start(
                xt[xbuf][:],
                p.x0[(s0 * NT + qt) * 128:(s0 * NT + qt + 1) * 128, :]), sx)
            v.wait_ge(sx.h, x_here)
            v.wait_ge(p.s_pe.h, pe_wo)
            if gidx >= 1:
                v.wait_ge(p.s_st.h, st_x1[gidx - 1])
            ve_resid[gidx] = I1(
                v.tensor_tensor(x1stg[:], pBig[:], xt[xbuf][:], ALU.add),
                p.s_ve)
            p.ve_resid_last = ve_resid[gidx]
            y.wait_ge(p.s_ve.h, ve_resid[gidx])
            st_x1[gidx] = I16(
                y.dma_start(p.x1scr[(s0 * NT + qt) * 128:
                                    (s0 * NT + qt + 1) * 128, :], x1stg[:]),
                p.s_st)

    y.wait_ge(p.s_st.h, p.s_st.n)
    g.wait_ge(p.s_w.h, p.s_w.n)
    p.esKV.close()
    p.esKV = None


def emit_ffn(p, src, dst, tag=""):
    nc = p.nc
    sb = p.ses.enter_context
    fW1_sb = sb(nc.sbuf_tensor(f"fW1_sb{tag}", [128, 16 * 2048], BF16))
    fV_sb = sb(nc.sbuf_tensor(f"fV_sb{tag}", [128, 16 * 512], BF16))
    fU_sb = sb(nc.sbuf_tensor(f"fU_sb{tag}", [128, 4 * 2048], BF16))
    fW2_sb = sb(nc.sbuf_tensor(f"fW2_sb{tag}", [128, 16 * 8], BF16))
    fb1_sb = sb(nc.sbuf_tensor(f"fb1_sb{tag}", [128, 16], F32))
    xt = [sb(nc.sbuf_tensor(f"xtF{tag}{i}", [128, D], F32)) for i in range(4)]
    xn_bf = sb(nc.sbuf_tensor(f"xn_bfF{tag}", [128, D], BF16))
    xnT_st = sb(nc.sbuf_tensor(f"xnT_stF{tag}", [128, 16 * 512], BF16))
    h_sb = sb(nc.sbuf_tensor(f"h_sbF{tag}", [128, 16 * 512], BF16))
    alpha = sb(nc.sbuf_tensor(f"alphaF{tag}", [128, 4 * 8], F32))
    altmp = sb(nc.sbuf_tensor(f"altmpF{tag}", [128, 12], F32))
    vxs = [sb(nc.sbuf_tensor(f"vxsF{tag}{i}", [128, 512], BF16)) for i in range(2)]
    vxT = [sb(nc.sbuf_tensor(f"vxTF{tag}{i}", [128, 512], BF16)) for i in range(2)]
    x2stg = sb(nc.sbuf_tensor(f"x2stgF{tag}", [128, D], F32))
    tmp8 = sb(nc.sbuf_tensor(f"tmp8F{tag}", [128, 8], F32))
    pH = sb(nc.psum_tensor(f"pHF{tag}", [128, 512], F32))
    pAl = sb(nc.psum_tensor(f"pAlF{tag}", [128, 8], F32))
    pVx = sb(nc.psum_tensor(f"pVxF{tag}", [128, 512], F32))
    pTT = [sb(nc.psum_tensor(f"pTTF{tag}", [128, 512], BF16))]
    pY = sb(nc.psum_tensor(f"pYF{tag}", [128, 2048], F32))

    g, v, s, t, y = p.g, p.v, p.s, p.t, p.y

    for c in range(16):
        I16(g.dma_start(out=fW1_sb[:, c * 2048:(c + 1) * 2048],
                        in_=p.fW1[c * 128:(c + 1) * 128, :]), p.s_w)
        I16(g.dma_start(out=fV_sb[:, c * 512:(c + 1) * 512],
                        in_=p.fV[c * 128:(c + 1) * 128, :]), p.s_w)
        I16(g.dma_start(out=fW2_sb[:, c * 8:(c + 1) * 8],
                        in_=p.fW2[c * 128:(c + 1) * 128, :]), p.s_w)
    for c in range(4):
        I16(g.dma_start(out=fU_sb[:, c * 2048:(c + 1) * 2048],
                        in_=p.fU[c * 128:(c + 1) * 128, :]), p.s_w)
    I16(g.dma_start(out=fb1_sb[:], in_=p.fb1c), p.s_w)
    w_done = p.s_w.n

    st_x2 = {}
    ve_resid = {}
    pe_alpha_last = 0
    se_h_last = 0
    se_al_last = 0
    p.ve_al_done = {}
    p.ve_vx_free = {}
    p.pT_free.pop(id(pTT[0]), None)
    pTT_free = [0]
    first = True
    for sti in range(8):
        parity = sti % 2
        sx = p.s_x0 if parity == 0 else p.s_x1
        for j in range(4):
            gi = sti * 4 + j
            if sti >= 1:
                y.wait_ge(p.s_ve.h, ve_resid[(sti - 1) * 4 + j])
            I16(y.dma_start(xt[j][:],
                            src[gi * 128:(gi + 1) * 128, :]), sx)
        x_all = sx.n
        for j in range(4):
            v.wait_ge(sx.h, x_all)
            s.wait_ge(sx.h, x_all)
            if not first:
                v.wait_ge(p.s_pe.h, p.pe_xn_doneF)
                s.wait_ge(p.s_pe.h, p.pe_xn_doneF)
            first = False
            if sti >= 1 and j == 0:
                v.wait_ge(p.s_pe.h, p.pe_h_lastF)
            ve_ap = _ln(p, xt[j][:], xn_bf[:], tmp8, xn_bf[:])
            t.wait_ge(p.s_ve.h, ve_ap)
            if sti == 0 and j == 0:
                t.wait_ge(p.s_w.h, w_done)
            _xnT(p, xn_bf, None, pTT + pTT,
                 dst_map=lambda grp, j=j: xnT_st[:].rearrange(
                     "p (a b) -> p a b", a=16)[:, grp * 4:(grp + 1) * 4,
                                               j * 128:(j + 1) * 128])
            p.pe_xn_doneF = p.s_pe.n
        for e in range(16):
            if se_h_last:
                t.wait_ge(p.s_se.h, se_h_last)
            if e == 0 and p.ve_vx_free.get(1, 0):
                t.wait_ge(p.s_ve.h, p.ve_vx_free[1])
            for d in range(16):
                mm = t.matmul(pH[:],
                              fW1_sb[:, d * 2048 + e * 128:d * 2048 + (e + 1) * 128],
                              xnT_st[:, d * 512:(d + 1) * 512],
                              start=(d == 0), stop=(d == 15))
            I1(mm, p.s_pe)
            if e == 15:
                p.pe_h_lastF = p.s_pe.n
            s.wait_ge(p.s_pe.h, p.s_pe.n)
            if pe_alpha_last:
                s.wait_ge(p.s_pe.h, pe_alpha_last)
            if p.ve_vx_free.get(1, 0):
                s.wait_ge(p.s_ve.h, p.ve_vx_free[1])
            se_h_last = I1(s.activation(h_sb[:, e * 512:(e + 1) * 512], pH[:],
                                        AF.Relu, bias=fb1_sb[:, e:e + 1],
                                        scale=1.0), p.s_se)
        for j in range(4):
            t.wait_ge(p.s_se.h, se_h_last)
            if se_al_last:
                t.wait_ge(p.s_se.h, se_al_last)
            for e in range(16):
                mm = t.matmul(pAl[:],
                              h_sb[:, e * 512 + j * 128:e * 512 + (j + 1) * 128],
                              fW2_sb[:, e * 8:(e + 1) * 8],
                              start=(e == 0), stop=(e == 15))
            pe_alpha_last = I1(mm, p.s_pe)
            v.wait_ge(p.s_pe.h, pe_alpha_last)
            if p.ve_al_done.get(j, 0):
                pass
            v.tensor_tensor(altmp[:, 0:8], pAl[:], p.b2b[:], ALU.add)
            vsep(p)
            I1(v.reduce_max(altmp[:, 8:9], altmp[:, 0:8], axis=AX, negate=True),
               p.s_ve)
            s.wait_ge(p.s_ve.h, p.s_ve.n)
            se_al_last = I1(
                s.activation(tmp8[:, 0:8], altmp[:, 0:8], AF.Exp,
                             bias=altmp[:, 8:9], scale=1.0,
                             accum_out=altmp[:, 9:10]), p.s_se)
            v.wait_ge(p.s_se.h, se_al_last)
            v.reciprocal(altmp[:, 10:11], altmp[:, 9:10])
            vsep(p)
            p.ve_al_done[j] = I1(v.tensor_scalar(
                out=alpha[:, j * 8:(j + 1) * 8], in0=tmp8[:, 0:8],
                scalar1=altmp[:, 10:11], scalar2=None, op0=ALU.mult), p.s_ve)
            vsep(p)
        for j in range(4):
            gi = sti * 4 + j
            par = j % 2
            pvx = pVx if par == 0 else pH
            if par == 1:
                t.wait_ge(p.s_se.h, se_h_last)
            if p.ve_vx_free.get(par, 0):
                t.wait_ge(p.s_ve.h, p.ve_vx_free[par])
            for d in range(16):
                mm = t.matmul(pvx[:, 0:512],
                              xnT_st[:, d * 512 + j * 128:d * 512 + (j + 1) * 128],
                              fV_sb[:, d * 512:(d + 1) * 512],
                              start=(d == 0), stop=(d == 15))
            I1(mm, p.s_pe)
            v.wait_ge(p.s_pe.h, p.s_pe.n)
            v.wait_ge(p.s_ve.h, p.ve_al_done[j])
            if p.pT_free.get(("vxsT", par), 0):
                v.wait_ge(p.s_pe.h, p.pT_free[("vxsT", par)])
            for k in range(8):
                mm = v.tensor_scalar(out=vxs[par][:, k * 64:(k + 1) * 64],
                                     in0=pvx[:, k * 64:(k + 1) * 64],
                                     scalar1=alpha[:, j * 8 + k:j * 8 + k + 1],
                                     scalar2=None, op0=ALU.mult)
            p.ve_vx_free[par] = I1(mm, p.s_ve)
            vsep(p)
            t.wait_ge(p.s_ve.h, p.ve_vx_free[par])
            if pTT_free[0]:
                t.wait_ge(p.s_v2.h, pTT_free[0])
            for c in range(4):
                mm = t.transpose(pTT[0][:, c * 128:(c + 1) * 128],
                                 vxs[par][:, c * 128:(c + 1) * 128], p.ident[:])
            I1(mm, p.s_pe)
            p.pT_free[("vxsT", par)] = p.s_pe.n
            s.wait_ge(p.s_pe.h, p.s_pe.n)
            sc = s.copy(vxT[par][:], pTT[0][:])
            sc.then_inc(p.s_v2.h, 1)
            p.s_v2.n += 1
            pTT_free[0] = p.s_v2.n
            t.wait_ge(p.s_v2.h, pTT_free[0])
            xb = gi % 2
            for dh in range(2):
                pyb = pY[:, dh * 1024:(dh + 1) * 1024]
                if ve_resid.get((gi - 1, dh), 0):
                    t.wait_ge(p.s_ve.h, ve_resid[(gi - 1, dh)])
                for kr in range(4):
                    for dc in range(2):
                        mm = t.matmul(
                            pyb[:, dc * 512:(dc + 1) * 512],
                            vxT[par][:, kr * 128:(kr + 1) * 128],
                            fU_sb[:, kr * 2048 + (dh * 2 + dc) * 512:
                                  kr * 2048 + (dh * 2 + dc + 1) * 512],
                            start=(kr == 0), stop=(kr == 3))
                pe_y = I1(mm, p.s_pe)
                v.wait_ge(p.s_pe.h, pe_y)
                if dh == 0 and st_x2.get(gi - 1, 0):
                    v.wait_ge(p.s_st.h, st_x2[gi - 1])
                ve_resid[(gi, dh)] = I1(
                    v.tensor_tensor(x2stg[:, dh * 1024:(dh + 1) * 1024],
                                    pyb,
                                    xt[j][:, dh * 1024:(dh + 1) * 1024],
                                    ALU.add), p.s_ve)
            ve_resid[gi] = ve_resid[(gi, 1)]
            y.wait_ge(p.s_ve.h, ve_resid[gi])
            st_x2[gi] = I16(y.dma_start(dst[gi * 128:(gi + 1) * 128, :],
                                        x2stg[:]), p.s_st)
    y.wait_ge(p.s_st.h, p.s_st.n)
    g.wait_ge(p.s_w.h, p.s_w.n)


def emit_A3(p):
    emit_ffn(p, p.x1scr, p.x2scr, tag="a")


def emit_B1a(p):
    nc = p.nc
    sb = p.ses.enter_context
    bWq_sb = sb(nc.sbuf_tensor("bWq_sb", [128, 16 * 1024], BF16))
    bWk_sb = sb(nc.sbuf_tensor("bWk_sb", [128, 16 * 1024], BF16))
    bWs_sb = sb(nc.sbuf_tensor("bWs_sb", [128, 16 * 1024], BF16))
    xt = [sb(nc.sbuf_tensor(f"xtB1{i}", [128, D], F32)) for i in range(4)]
    xn_bf = sb(nc.sbuf_tensor("xn_bfB1", [128, D], BF16))
    xnT_st = sb(nc.sbuf_tensor("xnT_stB1", [128, 16 * 512], BF16))
    qstg = [sb(nc.sbuf_tensor(f"qstgB1{i}", [128, 512], BF16)) for i in range(2)]
    kstg = [sb(nc.sbuf_tensor(f"kstgB1{i}", [128, 512], BF16)) for i in range(2)]
    estg = sb(nc.sbuf_tensor("estgB1", [128, 1024], BF16))
    e32 = sb(nc.sbuf_tensor("e32B1", [128, 1024], F32))
    rzstg = sb(nc.sbuf_tensor("rzstgB1", [128, 1024], BF16))
    onescol = sb(nc.sbuf_tensor("onescolB1", [128, 1], BF16))
    ones1b = sb(nc.sbuf_tensor("ones1bB1", [1, 128], BF16))
    prehi = sb(nc.sbuf_tensor("prehiB1", [1, 1024], BF16))
    prelo = sb(nc.sbuf_tensor("preloB1", [1, 1024], BF16))
    prelof = sb(nc.sbuf_tensor("prelofB1", [1, 1024], F32))
    tmp8 = sb(nc.sbuf_tensor("tmp8B1", [128, 8], F32))
    pP = sb(nc.psum_tensor("pPB1", [128, 512], F32))
    pS2 = sb(nc.psum_tensor("pS2B1", [128, 1024], F32))
    pZ = sb(nc.psum_tensor("pZB1", [128, 1024], F32))
    pPre = sb(nc.psum_tensor("pPreB1", [1, 1024], F32))
    pT0 = sb(nc.psum_tensor("pTB1", [128, 512], BF16))
    pT = [pT0, pT0]
    p.pT_free = {}

    g, v, s, t, y = p.g, p.v, p.s, p.t, p.y

    for c in range(16):
        I16(g.dma_start(out=bWq_sb[:, c * 1024:(c + 1) * 1024],
                        in_=p.bWq[c * 128:(c + 1) * 128, :]), p.s_w)
        I16(g.dma_start(out=bWk_sb[:, c * 1024:(c + 1) * 1024],
                        in_=p.bWk[c * 128:(c + 1) * 128, :]), p.s_w)
        I16(g.dma_start(out=bWs_sb[:, c * 1024:(c + 1) * 1024],
                        in_=p.bWs[c * 128:(c + 1) * 128, :]), p.s_w)
    w_done = p.s_w.n
    I1(v.memset(onescol[:], 1.0), p.s_ve)
    I1(v.memset(ones1b[:], 1.0), p.s_ve)

    se_stg_last = {}
    st_last = {}
    ve_resid = {}
    first = True
    pe_proj_last = 0
    se_E_last = 0
    ve_rz_last = 0
    ve_pre_last = 0
    pe_zc_last = 0
    pe_pre_count = 0
    for s0 in range(SEQ):
        for ch in range(4):
            sti = s0 * 4 + ch
            parity = sti % 2
            sx = p.s_x0 if parity == 0 else p.s_x1
            for j in range(4):
                gi = sti * 4 + j
                if sti >= 1:
                    y.wait_ge(p.s_ve.h, ve_resid.get((sti - 1) * 4 + j, 0))
                I16(y.dma_start(xt[j][:],
                                p.x2scr[gi * 128:(gi + 1) * 128, :]), sx)
            x_all = sx.n
            for j in range(4):
                gi = sti * 4 + j
                v.wait_ge(sx.h, x_all)
                s.wait_ge(sx.h, x_all)
                if not first:
                    v.wait_ge(p.s_pe.h, p.pe_xn_doneF)
                    s.wait_ge(p.s_pe.h, p.pe_xn_doneF)
                first = False
                if sti >= 1 and j == 0:
                    v.wait_ge(p.s_pe.h, pe_proj_last)
                ve_ap = _ln(p, xt[j][:], xn_bf[:], tmp8, xn_bf[:],
                            store_col=gi)
                ve_resid[gi] = ve_ap
                t.wait_ge(p.s_ve.h, ve_ap)
                if sti == 0 and j == 0:
                    t.wait_ge(p.s_w.h, w_done)
                    t.wait_ge(p.s_ve.h, p.s_ve.n)
                _xnT(p, xn_bf, None, pT,
                     dst_map=lambda grp, j=j: xnT_st[:].rearrange(
                         "p (a b) -> p a b", a=16)[:, grp * 4:(grp + 1) * 4,
                                                   j * 128:(j + 1) * 128])
                p.pe_xn_doneF = p.s_pe.n
            pcnt = 0
            for which, wsb, stgs, scr in [(0, bWq_sb, qstg, p.qbscr),
                                          (1, bWk_sb, kstg, p.kbscr)]:
                for oc in range(8):
                    par = pcnt % 2
                    pcnt += 1
                    pb = pP[:] if par == 0 else pZ[:, 0:512]
                    if se_stg_last.get(("pp", par), 0):
                        t.wait_ge(p.s_se.h, se_stg_last[("pp", par)])
                    for d in range(16):
                        mm = t.matmul(
                            pb,
                            wsb[:, d * 1024 + oc * 128:d * 1024 + (oc + 1) * 128],
                            xnT_st[:, d * 512:(d + 1) * 512],
                            start=(d == 0), stop=(d == 15))
                    pe_proj_last = I1(mm, p.s_pe)
                    s.wait_ge(p.s_pe.h, pe_proj_last)
                    stg = stgs[par]
                    if st_last.get((which, par), 0):
                        s.wait_ge(p.s_st.h, st_last[(which, par)])
                    se_stg_last[(which, par)] = se_stg_last[("pp", par)] = I1(
                        s.copy(stg[:], pb), p.s_se)
                    y.wait_ge(p.s_se.h, se_stg_last[(which, par)])
                    st_last[(which, par)] = I16(
                        y.dma_start(scr[s0, oc, :, ch * 512:(ch + 1) * 512],
                                    stg[:]), p.s_st)
            if se_stg_last.get(("pp", 1), 0):
                t.wait_ge(p.s_se.h, se_stg_last[("pp", 1)])
            for j in range(4):
                gi = sti * 4 + j
                jj = ch * 4 + j
                if se_E_last:
                    t.wait_ge(p.s_se.h, se_E_last)
                for jc in range(2):
                    for d in range(16):
                        mm = t.matmul(
                            pS2[:, jc * 512:(jc + 1) * 512],
                            xnT_st[:, d * 512 + j * 128:d * 512 + (j + 1) * 128],
                            bWs_sb[:, d * 1024 + jc * 512:d * 1024 + (jc + 1) * 512],
                            start=(d == 0), stop=(d == 15))
                pe_s = I1(mm, p.s_pe)
                s.wait_ge(p.s_pe.h, pe_s)
                if ve_pre_last:
                    s.wait_ge(p.s_ve.h, ve_pre_last)
                se_E_last = I1(s.activation(e32[:], pS2[:], AF.Exp,
                                            bias=p.n32_c[:], scale=1.0), p.s_se)
                v.wait_ge(p.s_se.h, se_E_last)
                if pe_zc_last:
                    v.wait_ge(p.s_pe.h, pe_zc_last)
                if st_last.get("E", 0):
                    v.wait_ge(p.s_st.h, st_last["E"])
                v.tensor_scalar(out=e32[:], in0=e32[:], scalar1=1.0,
                                scalar2=None, op0=ALU.min)
                vsep(p)
                I1(v.tensor_scalar(out=estg[:], in0=e32[:], scalar1=1.6e-28,
                                   scalar2=None, op0=ALU.max), p.s_ve)
                ve_E = p.s_ve.n
                if jj > 0:
                    v.wait_ge(p.s_pe.h, pe_pre_count)
                    v.tensor_copy(prehi[:], pPre[:])
                    vsep(p)
                    v.tensor_tensor(prelof[:], pPre[:], prehi[:], ALU.subtract)
                    vsep(p)
                    I1(v.tensor_copy(prelo[:], prelof[:]), p.s_ve)
                ve_pre_last = p.s_ve.n
                t.wait_ge(p.s_ve.h, ve_pre_last)
                if ve_rz_last:
                    t.wait_ge(p.s_ve.h, ve_rz_last)
                for jc in range(2):
                    mm = t.matmul(pZ[:, jc * 512:(jc + 1) * 512], p.utinc[:],
                                  estg[:, jc * 512:(jc + 1) * 512],
                                  start=True, stop=(jj == 0))
                    if jj > 0:
                        mm = t.matmul(pZ[:, jc * 512:(jc + 1) * 512], ones1b[:],
                                      prehi[:, jc * 512:(jc + 1) * 512],
                                      start=False, stop=False)
                        mm = t.matmul(pZ[:, jc * 512:(jc + 1) * 512], ones1b[:],
                                      prelo[:, jc * 512:(jc + 1) * 512],
                                      start=False, stop=True)
                pe_zc = I1(mm, p.s_pe)
                for jc in range(2):
                    mm = t.matmul(pPre[:, jc * 512:(jc + 1) * 512], onescol[:],
                                  estg[:, jc * 512:(jc + 1) * 512],
                                  start=(jj == 0), stop=(jj == 15))
                pe_pre_count = I1(mm, p.s_pe)
                pe_zc_last = pe_pre_count
                v.wait_ge(p.s_pe.h, pe_zc)
                if st_last.get("rz", 0):
                    v.wait_ge(p.s_st.h, st_last["rz"])
                with nc.allow_low_precision(
                        reason="rz stored bf16 by design (ratio structure)"):
                    ve_rz_last = I1(v.reciprocal(rzstg[:], pZ[:]), p.s_ve)
                y.wait_ge(p.s_ve.h, ve_rz_last)
                st_last["E"] = I16(
                    y.dma_start(p.escr[s0, jj * 128:(jj + 1) * 128, :],
                                estg[:]), p.s_st)
                st_last["rz"] = I16(
                    y.dma_start(p.rzscr[s0, jj * 128:(jj + 1) * 128, :],
                                rzstg[:]), p.s_st)
    y.wait_ge(p.s_st.h, p.s_st.n)
    g.wait_ge(p.s_w.h, p.s_w.n)
    if p.tap == "B1a":
        y.wait_ge(p.s_se.h, p.s_se.n)
        I16(y.dma_start(p.stats_tap, p.stats[:]), p.s_st)
        y.wait_ge(p.s_st.h, p.s_st.n)


def emit_B1b(p):
    nc = p.nc
    sb = p.ses.enter_context
    bWv_sb = sb(nc.sbuf_tensor("bWv_sb", [128, 16 * 2048], BF16))
    bWg_sb = sb(nc.sbuf_tensor("bWg_sb", [128, 16 * 2048], BF16))
    xt = [sb(nc.sbuf_tensor(f"xtB2{i}", [128, D], F32)) for i in range(4)]
    xn_bf = sb(nc.sbuf_tensor("xn_bfB2", [128, D], BF16))
    xnT_st = sb(nc.sbuf_tensor("xnT_stB2", [128, 16 * 512], BF16))
    vstg = sb(nc.sbuf_tensor("vstgB2", [128, D], BF16))
    gstg = sb(nc.sbuf_tensor("gstgB2", [128, D], BF16))
    pV4 = sb(nc.psum_tensor("pV4B2", [128, 2048], F32))
    pT0 = sb(nc.psum_tensor("pTB2", [128, 512], BF16))
    pT = [pT0, pT0]
    p.pT_free = {}

    g, v, s, t, y = p.g, p.v, p.s, p.t, p.y

    for c in range(16):
        I16(g.dma_start(out=bWv_sb[:, c * 2048:(c + 1) * 2048],
                        in_=p.bWv[c * 128:(c + 1) * 128, :]), p.s_w)
        I16(g.dma_start(out=bWg_sb[:, c * 2048:(c + 1) * 2048],
                        in_=p.bWg[c * 128:(c + 1) * 128, :]), p.s_w)
    w_done = p.s_w.n

    st_last = {}
    ve_ap = {}
    pv4_evac = {}
    se_stg_last = 0
    first = True
    for sti in range(8):
        s0 = sti // 4
        parity = sti % 2
        sx = p.s_x0 if parity == 0 else p.s_x1
        for j in range(4):
            gi = sti * 4 + j
            if sti >= 1:
                y.wait_ge(p.s_ve.h, ve_ap.get((sti - 1) * 4 + j, 0))
            I16(y.dma_start(xt[j][:], p.x2scr[gi * 128:(gi + 1) * 128, :]), sx)
        x_all = sx.n
        for j in range(4):
            gi = sti * 4 + j
            v.wait_ge(sx.h, x_all)
            if not first:
                v.wait_ge(p.s_pe.h, p.pe_xn_doneF)
            first = False
            ve_ap[gi] = I1(
                v.tensor_scalar(out=xn_bf[:], in0=xt[j][:],
                                scalar1=p.stats[:, 2 * gi:2 * gi + 1],
                                scalar2=p.stats[:, 2 * gi + 1:2 * gi + 2],
                                op0=ALU.subtract, op1=ALU.mult), p.s_ve)
            t.wait_ge(p.s_ve.h, ve_ap[gi])
            if sti == 0 and j == 0:
                t.wait_ge(p.s_w.h, w_done)
            _xnT(p, xn_bf, None, pT,
                 dst_map=lambda grp, j=j: xnT_st[:].rearrange(
                     "p (a b) -> p a b", a=16)[:, grp * 4:(grp + 1) * 4,
                                               j * 128:(j + 1) * 128])
            p.pe_xn_doneF = p.s_pe.n
        for j in range(4):
            gi = sti * 4 + j
            jj = gi - s0 * 16
            for which, wsb, stg, scr, fn in [
                    (0, bWv_sb, vstg, p.vbscr, AF.Copy),
                    (1, bWg_sb, gstg, p.gbscr, AF.Silu)]:
                for jc in range(4):
                    if pv4_evac.get(jc, 0):
                        t.wait_ge(p.s_se.h, pv4_evac[jc])
                    for d in range(16):
                        mm = t.matmul(
                            pV4[:, jc * 512:(jc + 1) * 512],
                            xnT_st[:, d * 512 + j * 128:d * 512 + (j + 1) * 128],
                            wsb[:, d * 2048 + jc * 512:d * 2048 + (jc + 1) * 512],
                            start=(d == 0), stop=(d == 15))
                    pe_p = I1(mm, p.s_pe)
                    s.wait_ge(p.s_pe.h, pe_p)
                    if jc == 0 and st_last.get(which, 0):
                        s.wait_ge(p.s_st.h, st_last[which])
                    pv4_evac[jc] = I1(
                        s.activation(stg[:, jc * 512:(jc + 1) * 512],
                                     pV4[:, jc * 512:(jc + 1) * 512], fn),
                        p.s_se)
                se_stg_last = pv4_evac[3]
                y.wait_ge(p.s_se.h, se_stg_last)
                st_last[which] = I16(
                    y.dma_start(scr[s0, jj * 128:(jj + 1) * 128, :], stg[:]),
                    p.s_st)
    y.wait_ge(p.s_st.h, p.s_st.n)
    g.wait_ge(p.s_w.h, p.s_w.n)


def emit_B2(p):
    nc = p.nc
    sb = p.ses.enter_context
    bWo_sb = sb(nc.sbuf_tensor("bWo_sb", [128, 16 * 2048], BF16))
    utinc8 = sb(nc.sbuf_tensor("utinc8", [128, 128], BF16))
    o2fm = sb(nc.sbuf_tensor("o2fmB2", [128, 16 * 8 * 128], BF16))
    AT_sb = sb(nc.sbuf_tensor("AT_sbB2", [128, 16 * 512], BF16))
    WT_sb = sb(nc.sbuf_tensor("WT_sbB2", [128, 16 * 512], BF16))
    E_fm = [sb(nc.sbuf_tensor(f"E_fmB2{i}", [64, 16 * 128], BF16))
            for i in range(2)]
    P2_tm = sb(nc.sbuf_tensor("P2_tmB2", [128, 8 * 64], BF16))
    P2_fm = sb(nc.sbuf_tensor("P2_fmB2", [64, 8 * 128], BF16))
    q_h = [sb(nc.sbuf_tensor(f"q_hB2{i}", [64, 1024], BF16)) for i in range(2)]
    k_h = [sb(nc.sbuf_tensor(f"k_hB2{i}", [64, 2048], BF16)) for i in range(2)]
    E_h = [sb(nc.sbuf_tensor(f"E_hB2{i}", [128, 16 * 64], BF16)) for i in range(2)]
    rz_h = [sb(nc.sbuf_tensor(f"rz_hB2{i}", [128, 8 * 64], BF16)) for i in range(2)]
    v_h = [sb(nc.sbuf_tensor(f"v_hB2{i}", [128, 16 * 128], BF16)) for i in range(2)]
    gs_h = [sb(nc.sbuf_tensor(f"gs_hB2{i}", [128, 8 * 128], BF16)) for i in range(2)]
    okf8 = sb(nc.sbuf_tensor("okf8B2", [128, 512], F32))
    pvt8 = sb(nc.sbuf_tensor("pvt8B2", [128, 512], BF16))
    o_bf4 = sb(nc.sbuf_tensor("o_bf4B2", [128, 512], BF16))
    o2_4 = sb(nc.sbuf_tensor("o2_4B2", [128, 512], BF16))
    jnk = sb(nc.sbuf_tensor("jnkB2", [128, 128], BF16))
    ep32 = sb(nc.sbuf_tensor("ep32B2", [128, 40], F32))
    xt = [sb(nc.sbuf_tensor(f"xtB3{i}", [128, D], F32)) for i in range(2)]
    x3stg = sb(nc.sbuf_tensor("x3stgB2", [128, D], F32))
    pAT = [sb(nc.psum_tensor(f"pATB2{i}", [128, 512], F32)) for i in range(2)]
    pNO = sb(nc.psum_tensor("pNOB2", [128, 512], F32))
    pWT = [sb(nc.psum_tensor(f"pWTB2{i}", [128, 512], F32)) for i in range(2)]
    pTb = sb(nc.psum_tensor("pTbB2", [128, 512], BF16))
    pWo4 = sb(nc.psum_tensor("pWo4B2", [128, 1024], F32))

    g, v, s, t, y = p.g, p.v, p.s, p.t, p.y

    for c in range(16):
        I16(g.dma_start(out=bWo_sb[:, c * 2048:(c + 1) * 2048],
                        in_=p.bWo[c * 128:(c + 1) * 128, :]), p.s_w)
    w_done = p.s_w.n
    v.wait_ge(p.s_w.h, w_done)
    for c in range(16):
        mm = v.tensor_scalar(out=bWo_sb[:, c * 2048:(c + 1) * 2048],
                             in0=bWo_sb[:, c * 2048:(c + 1) * 2048],
                             scalar1=p.gnwcol[:], scalar2=None, op0=ALU.mult)
    I1(mm, p.s_ve)
    wo_scaled = p.s_ve.n
    I1(v.tensor_scalar(out=utinc8[:], in0=p.utinc[:], scalar1=float(BQ_SCALE),
                       scalar2=None, op0=ALU.mult), p.s_ve)
    ut8_done = p.s_ve.n

    p.pATfree = {}
    p.pWTfree = {}
    p.at_ev = {}
    p.num_rd = {}
    p.num_wr = {}
    p.pTb_free = 0
    p.pe_num_last = 0
    p.pe_ov_last = 0
    p.pe_o2T = 0
    p.ve_NO_read_last = 0
    p.se_NO_read_last = 0
    p.ve_o2_last = 0
    p.wo_resid_last = 0
    hload = {}
    hdone = {}
    ve_resid = {}
    st_x3 = {}
    se_AT_last = 0
    ve_ATd_last = 0
    pe_num_last = 0
    se_WT_last = 0
    pe_ov_last = 0
    pe_P2T_last = 0
    ve_NO_read_last = 0
    se_NO_read_last = 0
    pTb_free = 0
    pe_o2T = 0
    first_wo = True

    for s0 in range(SEQ):
        for half in range(2):
            hi = (half + 1) * 1024
            nt = hi // 128
            hseq = s0 * 2 + half
            def ph1(h):
                hb = h % 2
                stg = p.s_tg0 if hb == 0 else p.s_tg1
                hkey = hseq * BH + h
                if hkey >= 2:
                    y.wait_ge(p.s_pe.h, hdone[hkey - 2])
                oc, ro = h // 2, 64 * (h % 2)
                I16(y.dma_start(q_h[hb][:, 0:1024],
                                p.qbscr[s0, oc, ro:ro + 64,
                                        half * 1024:hi]), stg)
                I16(y.dma_start(k_h[hb][:, 0:hi],
                                p.kbscr[s0, oc, ro:ro + 64, 0:hi]), stg)
                I16(y.dma_start(
                    E_h[hb][:, 0:nt * 64],
                    p.escr[s0, 0:hi, h * 64:(h + 1) * 64].rearrange(
                        "(n q) m -> q n m", q=128)), stg)
                I16(y.dma_start(
                    rz_h[hb][:, 0:512],
                    p.rzscr[s0, half * 1024:hi,
                            h * 64:(h + 1) * 64].rearrange(
                        "(n q) m -> q n m", q=128)), stg)
                I16(y.dma_start(
                    v_h[hb][:, 0:nt * 128],
                    p.vbscr[s0, 0:hi, h * 128:(h + 1) * 128].rearrange(
                        "(n q) m -> q n m", q=128)), stg)
                I16(y.dma_start(
                    gs_h[hb][:, 0:1024],
                    p.gbscr[s0, half * 1024:hi,
                            h * 128:(h + 1) * 128].rearrange(
                        "(n q) m -> q n m", q=128)), stg)
                hload[hkey] = stg.n

                t.wait_ge(stg.h, hload[hkey])
                if h == 0 and hseq == 0:
                    t.wait_ge(p.s_ve.h, ut8_done)
                for grp in range((nt + 3) // 4):
                    if p.pTb_free:
                        t.wait_ge(p.s_v2.h, p.pTb_free)
                    for c in range(min(4, nt - grp * 4)):
                        tt = grp * 4 + c
                        mm = t.transpose(pTb[0:64, c * 128:(c + 1) * 128],
                                         E_h[hb][:, tt * 64:(tt + 1) * 64],
                                         p.ident[:])
                    I1(mm, p.s_pe)
                    v.wait_ge(p.s_pe.h, p.s_pe.n)
                    nw = min(4, nt - grp * 4)
                    p.pTb_free = I1(
                        v.tensor_copy(
                            E_fm[hb][0:64, grp * 512:grp * 512 + nw * 128],
                            pTb[0:64, 0:nw * 128]), p.s_v2)
                    vsep(p)

            def ph1b(h):
                hb = h % 2
                for tc in range(2):
                    nbase = hb * 512 + tc * 256
                    tcg = half * 2 + tc
                    ntau = (tcg + 1) * 4
                    for tt in range(ntau):
                        pb = pAT[tt % 2]
                        fr = p.pATfree.get(tt % 2, (0, 0))
                        if fr[0]:
                            t.wait_ge(p.s_se.h, fr[0])
                        if fr[1]:
                            t.wait_ge(p.s_ve.h, fr[1])
                        if p.pe_num_last and tt == 0:
                            t.wait_ge(p.s_pe.h, p.pe_num_last)
                        mm = t.matmul(pb[:], k_h[hb][:, tt * 128:(tt + 1) * 128],
                                      q_h[hb][:, tc * 512:(tc + 1) * 512],
                                      start=True, stop=True)
                        I1(mm, p.s_pe)
                        cs = tt - tcg * 4
                        if tt % 2 == 0:
                            s.wait_ge(p.s_pe.h, p.s_pe.n)
                            se_m = I1(
                                s.activation(AT_sb[:, tt * 512:(tt + 1) * 512],
                                             pb[:], AF.Copy, bias=0.0,
                                             scale=float(BQ_SCALE)), p.s_se)
                            marks = (se_m, 0)
                            dve_t, dve_n = p.s_se.h, se_m
                        else:
                            v.wait_ge(p.s_pe.h, p.s_pe.n)
                            ve_m = I1(v.tensor_scalar(
                                out=AT_sb[:, tt * 512:(tt + 1) * 512],
                                in0=pb[:], scalar1=float(BQ_SCALE),
                                scalar2=None, op0=ALU.mult), p.s_ve)
                            marks = (0, ve_m)
                            dve_t, dve_n = p.s_ve.h, ve_m
                        if 0 <= cs < 4:
                            v.wait_ge(dve_t, dve_n)
                            ve_m = I1(v.tensor_tensor(
                                AT_sb[:, tt * 512 + cs * 128:
                                      tt * 512 + (cs + 1) * 128],
                                pb[:, cs * 128:(cs + 1) * 128],
                                utinc8[:], ALU.mult), p.s_ve)
                            marks = (marks[0], ve_m)
                            vsep(p)
                        p.pATfree[tt % 2] = marks
                        p.at_ev[tt] = marks
                    if p.num_rd.get((hb, tc), 0):
                        t.wait_ge(p.s_ve.h, p.num_rd[(hb, tc)])
                    if h == 0 and tc == 0 and p.wo_resid_last:
                        t.wait_ge(p.s_ve.h, p.wo_resid_last)
                    for tt in range(ntau):
                        mk = p.at_ev[tt]
                        if mk[0]:
                            t.wait_ge(p.s_se.h, mk[0])
                        if mk[1]:
                            t.wait_ge(p.s_ve.h, mk[1])
                        for c in range(4):
                            jt = tcg * 4 + c
                            if tt <= jt:
                                mm = t.matmul(
                                    pWo4[:, nbase + c * 64:nbase + (c + 1) * 64],
                                    AT_sb[:, tt * 512 + c * 128:
                                          tt * 512 + (c + 1) * 128],
                                    E_h[hb][:, tt * 64:(tt + 1) * 64],
                                    start=(tt == 0 and c == 0),
                                    stop=(tt == jt))
                    p.pe_num_last = I1(mm, p.s_pe)
                    p.num_wr[(hb, tc)] = p.pe_num_last

            def sm(h):
                hb = h % 2
                v.wait_ge(p.s_pe.h, p.num_wr[(hb, 1)])
                nrd = I1(v.tensor_tensor(okf8[:],
                                         pWo4[:, hb * 512:(hb + 1) * 512],
                                         rz_h[hb][:, 0:512], ALU.mult),
                         p.s_ve)
                p.num_rd[(hb, 0)] = p.num_rd[(hb, 1)] = nrd
                for c8 in range(8):
                    mm = v.reduce_max(ep32[:, c8:c8 + 1],
                                      okf8[:, c8 * 64:(c8 + 1) * 64],
                                      axis=AX, negate=True)
                I1(mm, p.s_ve)
                s.wait_ge(p.s_ve.h, p.s_ve.n)
                for c8 in range(8):
                    mm = s.activation(pvt8[:, c8 * 64:(c8 + 1) * 64],
                                      okf8[:, c8 * 64:(c8 + 1) * 64], AF.Exp,
                                      bias=ep32[:, c8:c8 + 1], scale=1.0,
                                      accum_out=ep32[:, 8 + c8:9 + c8])
                I1(mm, p.s_se)
                v.wait_ge(p.s_se.h, p.s_se.n)
                for c8 in range(8):
                    v.reciprocal(ep32[:, 16 + c8:17 + c8],
                                 ep32[:, 8 + c8:9 + c8])
                for c8 in range(8):
                    v.tensor_scalar(out=pvt8[:, c8 * 64:(c8 + 1) * 64],
                                    in0=pvt8[:, c8 * 64:(c8 + 1) * 64],
                                    scalar1=ep32[:, 16 + c8:17 + c8],
                                    scalar2=None, op0=ALU.mult)
                vsep(p)
                I1(v.tensor_tensor(P2_tm[:, 0:512], pvt8[:],
                                   rz_h[hb][:, 0:512], ALU.mult), p.s_ve)
                p.ve_P2 = p.s_ve.n

            def ph2r(h):
                hb = h % 2
                t.wait_ge(p.s_ve.h, p.ve_P2)
                for grp in range(2):
                    if p.pTb_free:
                        t.wait_ge(p.s_v2.h, p.pTb_free)
                    for c in range(4):
                        lt = grp * 4 + c
                        mm = t.transpose(pTb[0:64, c * 128:(c + 1) * 128],
                                         P2_tm[:, lt * 64:(lt + 1) * 64],
                                         p.ident[:])
                    I1(mm, p.s_pe)
                    v.wait_ge(p.s_pe.h, p.s_pe.n)
                    p.pTb_free = I1(
                        v.tensor_copy(P2_fm[0:64, grp * 512:(grp + 1) * 512],
                                      pTb[0:64, 0:512]), p.s_v2)
                    vsep(p)
                for tc in range(2):
                    tcg = half * 2 + tc
                    ntau = (tcg + 1) * 4
                    t.wait_ge(p.s_v2.h, p.pTb_free)
                    for tt in range(ntau):
                        pb = pWT[tt % 2]
                        fr = p.pWTfree.get(tt % 2, (0, 0))
                        if fr[0]:
                            t.wait_ge(p.s_se.h, fr[0])
                        if fr[1]:
                            t.wait_ge(p.s_ve.h, fr[1])
                        if p.pe_ov_last and tt == 0:
                            t.wait_ge(p.s_pe.h, p.pe_ov_last)
                        mm = t.matmul(pb[:], E_fm[hb][0:64, tt * 128:(tt + 1) * 128],
                                      P2_fm[0:64, tc * 512:(tc + 1) * 512],
                                      start=True, stop=True)
                        I1(mm, p.s_pe)
                        cs = tt - tcg * 4
                        if tt % 2 == 0:
                            s.wait_ge(p.s_pe.h, p.s_pe.n)
                            se_m = I1(
                                s.copy(WT_sb[:, tt * 512:(tt + 1) * 512],
                                       pb[:]), p.s_se)
                            marks = (se_m, 0)
                            dve_t, dve_n = p.s_se.h, se_m
                        else:
                            v.wait_ge(p.s_pe.h, p.s_pe.n)
                            ve_m = I1(v.tensor_copy(
                                WT_sb[:, tt * 512:(tt + 1) * 512], pb[:]),
                                p.s_ve)
                            marks = (0, ve_m)
                            dve_t, dve_n = p.s_ve.h, ve_m
                        if 0 <= cs < 4:
                            v.wait_ge(dve_t, dve_n)
                            ve_m = I1(v.tensor_tensor(
                                WT_sb[:, tt * 512 + cs * 128:
                                      tt * 512 + (cs + 1) * 128],
                                pb[:, cs * 128:(cs + 1) * 128],
                                p.utinc[:], ALU.mult), p.s_ve)
                            marks = (marks[0], ve_m)
                            vsep(p)
                        p.pWTfree[tt % 2] = marks
                        p.at_ev[tt] = marks
                    if p.ve_NO_read_last:
                        t.wait_ge(p.s_ve.h, p.ve_NO_read_last)
                    if p.se_NO_read_last:
                        t.wait_ge(p.s_se.h, p.se_NO_read_last)
                    for tt in range(ntau):
                        mk = p.at_ev[tt]
                        if mk[0]:
                            t.wait_ge(p.s_se.h, mk[0])
                        if mk[1]:
                            t.wait_ge(p.s_ve.h, mk[1])
                        for c in range(4):
                            jt = tcg * 4 + c
                            if tt <= jt:
                                mm = t.matmul(
                                    pNO[:, c * 128:(c + 1) * 128],
                                    WT_sb[:, tt * 512 + c * 128:
                                          tt * 512 + (c + 1) * 128],
                                    v_h[hb][:, tt * 128:(tt + 1) * 128],
                                    start=(tt == 0 and c == 0),
                                    stop=(tt == jt))
                    p.pe_ov_last = I1(mm, p.s_pe)
                    s.wait_ge(p.s_pe.h, p.pe_ov_last)
                    if p.ve_o2_last:
                        s.wait_ge(p.s_ve.h, p.ve_o2_last)
                    for c in range(4):
                        s.copy(o_bf4[:, c * 128:(c + 1) * 128],
                               pNO[:, c * 128:(c + 1) * 128])
                    for c in range(4):
                        mm = s.activation(jnk[:], pNO[:, c * 128:(c + 1) * 128],
                                          AF.Square,
                                          accum_out=ep32[:, 24 + c:25 + c])
                    p.se_NO_read_last = I1(mm, p.s_se)
                    for c in range(4):
                        s.activation(ep32[:, 28 + c:29 + c],
                                     ep32[:, 24 + c:25 + c], AF.Ln,
                                     bias=p.eps_c[:], scale=1.0 / BDV)
                    for c in range(4):
                        mm = s.activation(ep32[:, 32 + c:33 + c],
                                          ep32[:, 28 + c:29 + c], AF.Exp,
                                          bias=0.0, scale=-0.5)
                    I1(mm, p.s_se)
                    v.wait_ge(p.s_se.h, p.s_se.n)
                    for c in range(4):
                        v.tensor_scalar(out=o_bf4[:, c * 128:(c + 1) * 128],
                                        in0=o_bf4[:, c * 128:(c + 1) * 128],
                                        scalar1=ep32[:, 32 + c:33 + c],
                                        scalar2=None, op0=ALU.mult)
                    for c in range(4):
                        lt = tc * 4 + c
                        mm = v.tensor_tensor(
                            o2_4[:, c * 128:(c + 1) * 128],
                            o_bf4[:, c * 128:(c + 1) * 128],
                            gs_h[hb][:, lt * 128:(lt + 1) * 128], ALU.mult)
                    ve_o2 = I1(mm, p.s_ve)
                    p.ve_NO_read_last = ve_o2
                    p.ve_o2_last = ve_o2
                    t.wait_ge(p.s_ve.h, ve_o2)
                    if p.pTb_free:
                        t.wait_ge(p.s_v2.h, p.pTb_free)
                    for c in range(4):
                        mm = t.transpose(pTb[:, c * 128:(c + 1) * 128],
                                         o2_4[:, c * 128:(c + 1) * 128],
                                         p.ident[:])
                    p.pe_o2T = I1(mm, p.s_pe)
                    v.wait_ge(p.s_pe.h, p.pe_o2T)
                    p.pTb_free = I1(
                        v.tensor_copy(o2fm[:, (h * 8 + tc * 4) * 128:
                                           (h * 8 + tc * 4 + 4) * 128],
                                      pTb[:, 0:512]), p.s_v2)
                    vsep(p)
                hdone[hseq * BH + h] = p.s_pe.n

            ph1(0)
            ph1b(0)
            for h in range(1, BH):
                ph1(h)
                sm(h - 1)
                ph1b(h)
                ph2r(h - 1)
            sm(BH - 1)
            ph2r(BH - 1)
            if first_wo:
                t.wait_ge(p.s_ve.h, wo_scaled)
                first_wo = False
            t.wait_ge(p.s_v2.h, p.s_v2.n)
            for par in (0, 1):
                for tc in (0, 1):
                    if p.num_rd.get((par, tc), 0):
                        t.wait_ge(p.s_ve.h, p.num_rd[(par, tc)])
            for lt in range(8):
                gt = hseq * 8 + lt
                xb = gt % 2
                sx = p.s_x0 if xb == 0 else p.s_x1
                if gt >= 2:
                    y.wait_ge(p.s_ve.h, ve_resid.get(gt - 2, 0))
                rowg = s0 * T + half * 1024 + lt * 128
                x_here = I16(y.dma_start(xt[xb][:],
                                         p.x2scr[rowg:rowg + 128, :]), sx)
                for dh in range(2):
                    if ve_resid.get((gt, dh - 1), 0):
                        t.wait_ge(p.s_ve.h, ve_resid[(gt, dh - 1)])
                    elif gt >= 1:
                        t.wait_ge(p.s_ve.h, ve_resid.get((gt - 1, 1), 0))
                    for dc in range(2):
                        for j16 in range(16):
                            mm = t.matmul(
                                pWo4[:, dc * 512:(dc + 1) * 512],
                                o2fm[:, (j16 * 8 + lt) * 128:
                                     (j16 * 8 + lt + 1) * 128],
                                bWo_sb[:, j16 * 2048 + (dh * 2 + dc) * 512:
                                       j16 * 2048 + (dh * 2 + dc + 1) * 512],
                                start=(j16 == 0), stop=(j16 == 15))
                    pe_wo = I1(mm, p.s_pe)
                    v.wait_ge(sx.h, x_here)
                    v.wait_ge(p.s_pe.h, pe_wo)
                    if dh == 0 and st_x3.get(gt - 1, 0):
                        v.wait_ge(p.s_st.h, st_x3[gt - 1])
                    ve_resid[(gt, dh)] = I1(
                        v.tensor_tensor(x3stg[:, dh * 1024:(dh + 1) * 1024],
                                        pWo4[:],
                                        xt[xb][:, dh * 1024:(dh + 1) * 1024],
                                        ALU.add), p.s_ve)
                ve_resid[gt] = ve_resid[(gt, 1)]
                y.wait_ge(p.s_ve.h, ve_resid[gt])
                p.wo_resid_last = ve_resid[gt]
                st_x3[gt] = I16(y.dma_start(p.x3scr[rowg:rowg + 128, :],
                                            x3stg[:]), p.s_st)
    y.wait_ge(p.s_st.h, p.s_st.n)
    g.wait_ge(p.s_w.h, p.s_w.n)


def emit_B3(p):
    emit_ffn(p, p.x3scr, p.out, tag="b")



def _host_constants():
    hd = HD
    inv = 1.0 / (10000.0 ** (np.arange(0, hd, 2, dtype=np.float64) / hd))
    pos = np.arange(T, dtype=np.float64)
    fr = pos[:, None] * inv[None, :]
    cos = np.cos(fr).astype(np.float32)
    sin = np.sin(fr).astype(np.float32)
    coscat = np.ascontiguousarray(np.concatenate([cos, cos], axis=1))
    sincat = np.ascontiguousarray(np.concatenate([-sin, sin], axis=1))
    ident = np.eye(128, dtype=np.float32)
    utinc = np.triu(np.ones((128, 128), np.float32))
    return coscat, sincat, ident, utinc


def make_in_maps(inp):
    coscat, sincat, ident, utinc = _host_constants()
    W = {}
    W["aWq"] = np.ascontiguousarray(np.asarray(inp["L0_attn_a_Wq"]).T)
    W["aWk"] = np.ascontiguousarray(np.asarray(inp["L0_attn_a_Wk"]).T)
    W["aWv"] = np.ascontiguousarray(np.asarray(inp["L0_attn_a_Wv"]).T)
    W["aWo"] = np.ascontiguousarray(np.asarray(inp["L0_attn_a_Wo"]).T)
    W["coscat"] = coscat
    W["sincat"] = sincat
    W["identi"] = ident
    W["utinci"] = utinc
    W["fW1"] = np.ascontiguousarray(np.asarray(inp["L0_ffn_W1"]))
    W["fb1c"] = np.ascontiguousarray(
        np.asarray(inp["L0_ffn_b1"]).reshape(16, 128).T)
    W["fW2"] = np.ascontiguousarray(np.asarray(inp["L0_ffn_W2"]))
    W["fb2"] = np.ascontiguousarray(np.asarray(inp["L0_ffn_b2"]).reshape(1, 8))
    W["fV"] = np.ascontiguousarray(
        np.asarray(inp["L0_ffn_V"]).transpose(2, 0, 1).reshape(D, 512))
    W["fU"] = np.ascontiguousarray(
        np.asarray(inp["L0_ffn_U"]).transpose(0, 2, 1).reshape(512, D))
    for k, nm in [("bWq", "L0_attn_b_Wq"), ("bWk", "L0_attn_b_Wk"),
                  ("bWs", "L0_attn_b_Ws"), ("bWv", "L0_attn_b_Wv"),
                  ("bWg", "L0_attn_b_Wg"), ("bWo", "L0_attn_b_Wo")]:
        W[k] = np.ascontiguousarray(np.asarray(inp[nm]))
    W["gnw"] = np.ascontiguousarray(
        np.asarray(inp["L0_attn_b_g_norm_weight"]).reshape(1, BDV))
    x = np.asarray(inp["L0_attn_a_hidden_states"], dtype=np.float32)
    in_maps = []
    for c in range(N_CORES):
        m = dict(W)
        m["x0"] = np.ascontiguousarray(x[2 * c:2 * c + 2].reshape(SEQ * T, D))
        in_maps.append(m)
    return in_maps


def kernel(**inp):
    in_maps = make_in_maps(inp)
    nc = build_program()
    res = run_bass_kernel_spmd(nc, in_maps, core_ids=list(range(N_CORES)))
    out = np.concatenate(
        [np.asarray(r["out"], np.float32).reshape(SEQ, T, D)
         for r in res.results], axis=0)
    return out
